# version 03
# baseline (speedup 1.0000x reference)
import os
import numpy as np
import ml_dtypes
import concourse.bass as bass
import concourse.mybir as mybir
from concourse.bass_utils import run_bass_kernel_spmd

F32 = mybir.dt.float32
BF16 = mybir.dt.bfloat16
AF = mybir.ActivationFunctionType
ALU = mybir.AluOpType
AX = mybir.AxisListType

NCORES = 8
S = 16384
D = 1024
HID = 4096
NSUB = 16
NQ = NSUB + 1
KT = 16
MAGIC = 12582912.0
NM_INIT = 100.0
NEG = -1e9
EPS = 1e-5
GRP = 2
NGRP = NSUB // GRP

ENGINES = ("T", "V", "S", "G", "Y")


class Plan:

    def __init__(self, nc):
        self.nc = nc
        self.ops = {e: [] for e in ENGINES}
        self.sems = {}
        self.cnt = {}
        self.bar_round = 0

    def sem(self, name):
        if name not in self.sems:
            self.sems[name] = self.nc.alloc_semaphore(name)
            self.cnt[name] = 0
        return self.sems[name]

    def emit(self, e, fn, waits=(), inc=None, inc_n=1, drain=False):
        sems = self.sems
        wl = tuple((sems[s], v) for (s, v) in waits if v > 0)

        def run(eng, fn=fn, wl=wl, inc=inc, inc_n=inc_n, drain=drain):
            for s, v in wl:
                eng.wait_ge(s, v)
            r = fn(eng)
            if drain:
                eng.drain()
            if inc is not None:
                r.then_inc(sems[inc], inc_n)

        self.ops[e].append(run)
        if inc is not None:
            self.cnt[inc] += inc_n
            return self.cnt[inc]
        return None

    def wait(self, e, waits):
        sems = self.sems
        wl = tuple((sems[s], v) for (s, v) in waits if v > 0)

        def run(eng, wl=wl):
            for s, v in wl:
                eng.wait_ge(s, v)

        self.ops[e].append(run)

    def c(self, name):
        return self.cnt[name]

    def barrier(self):
        bar = self.sem("bar")
        own_dma = {"Y": "dy", "G": "dg", "S": "ds"}
        self.bar_round += 1
        target = len(ENGINES) * self.bar_round
        for e in ENGINES:
            waits = []
            if e in own_dma:
                waits.append((own_dma[e], self.cnt[own_dma[e]]))
            def run_b(eng, bar=self.sems["bar"], waits=tuple(
                    (self.sems[s], v) for (s, v) in waits if v > 0)):
                for s_, v_ in waits:
                    eng.wait_ge(s_, v_)
                eng.drain()
                eng.sem_inc(bar, 1)
            self.ops[e].append(run_b)
            self.cnt["bar"] += 1
        for e in ENGINES:
            self.wait(e, [("bar", target)])


class Arena:
    def __init__(self, nc, base, limit):
        self.nc = nc
        self.base = base
        self.limit = limit
        self.cur = base

    def alloc(self, name, shape, dtype, align=64):
        per_part = int(np.prod(shape[1:])) * mybir.dt.size(dtype)
        off = (self.cur + align - 1) // align * align
        assert off + per_part <= self.limit, (
            f"arena overflow: {name} at {off}+{per_part} > {self.limit}")
        self.cur = off + per_part
        return self.nc.alloc_sbuf_tensor_at(name, shape, dtype, offset=off)

    def mark(self):
        return self.cur

    def reset(self, to):
        self.cur = to



def _copy(e, dst, src):
    if hasattr(e, "tensor_copy"):
        e_fn = e.tensor_copy
    else:
        e_fn = e.copy
    return e_fn(dst, src)

_PHASES = int(os.environ.get("MK_PHASES", "4"))
_SKIP_AG = os.environ.get("MK_SKIP_AG", "0") == "1"
_P1B = int(os.environ.get("MK_P1B", "5"))
_KT = int(os.environ.get("MK_KT", "16"))
_NPAIR = int(os.environ.get("MK_NPAIR", "9999"))
_PSTEP = int(os.environ.get("MK_PSTEP", "6"))
_SEV = os.environ.get("MK_SEV", "1") == "1"
_DEBUG = os.environ.get("MK_DEBUG", "0") == "1"


def build_program():
    nc = bass.Bass("TRN2", target_bir_lowering=False)

    d = {}
    d["xT_own"] = nc.dram_tensor("xT_own", [D, NQ * 128], F32, kind="ExternalInput").ap()
    d["x_tm"] = nc.dram_tensor("x_tm", [NQ * 128, D], F32, kind="ExternalInput").ap()
    d["qw"] = nc.dram_tensor("qw", [D, D], F32, kind="ExternalInput").ap()
    d["kw"] = nc.dram_tensor("kw", [D, D], F32, kind="ExternalInput").ap()
    d["vw"] = nc.dram_tensor("vw", [D, D], F32, kind="ExternalInput").ap()
    d["w1t"] = nc.dram_tensor("w1t", [D, HID], F32, kind="ExternalInput").ap()
    d["w2t"] = nc.dram_tensor("w2t", [HID, D], F32, kind="ExternalInput").ap()
    d["cw"] = nc.dram_tensor("cw", [128, 128], F32, kind="ExternalInput").ap()
    d["dmask"] = nc.dram_tensor("dmask", [128, 1024], F32, kind="ExternalInput").ap()
    d["hmask"] = nc.dram_tensor("hmask", [KT, 128, 1024], F32, kind="ExternalInput").ap()
    d["hvalid"] = nc.dram_tensor("hvalid", [128, 1], F32, kind="ExternalInput").ap()
    d["identb"] = nc.dram_tensor("identb", [128, 128], BF16, kind="ExternalInput").ap()
    d["identf"] = nc.dram_tensor("identf", [128, 128], F32, kind="ExternalInput").ap()
    d["ones_col"] = nc.dram_tensor("ones_col", [128, 1], F32, kind="ExternalInput").ap()
    d["ones_row"] = nc.dram_tensor("ones_row", [1, 128], F32, kind="ExternalInput").ap()
    y_out = nc.dram_tensor("y", [NSUB * 128, D], F32, kind="ExternalOutput").ap()
    if _DEBUG:
        dbg_o = nc.dram_tensor("dbg_o", [NQ, 128, D], F32, kind="ExternalOutput").ap()
        dbg_l = nc.dram_tensor("dbg_l", [128, NQ], F32, kind="ExternalOutput").ap()
        dbg_nm = nc.dram_tensor("dbg_nm", [128, NQ], F32, kind="ExternalOutput").ap()
        dbg_q = nc.dram_tensor("dbg_q", [128, 8, NQ * 128], BF16, kind="ExternalOutput").ap()
        dbg_sw = nc.dram_tensor("dbg_sw", [128, 4], F32, kind="ExternalOutput").ap()
        dbg_xln = nc.dram_tensor("dbg_xln", [NQ, 128, D], F32, kind="ExternalOutput").ap()
        dbg_kt = nc.dram_tensor("dbg_kt", [128, 8, 1024], BF16, kind="ExternalOutput").ap()
        dbg_vt = nc.dram_tensor("dbg_vt", [128, 8, 1024], BF16, kind="ExternalOutput").ap()
        dbg_p = nc.dram_tensor("dbg_p", [128, 1024], BF16, kind="ExternalOutput").ap()
        dbg_s = nc.dram_tensor("dbg_s", [128, 1024], F32, kind="ExternalOutput").ap()
        dbg_sml = nc.dram_tensor("dbg_sml", [128, 16], F32, kind="ExternalOutput").ap()

    k_own = nc.dram_tensor("k_own", [D, 2048], BF16, kind="Internal").ap()
    v_own = nc.dram_tensor("v_own", [2048, D], BF16, kind="Internal").ap()
    k_all = nc.dram_tensor("k_all", [NCORES * D, 2048], BF16, kind="Internal").ap()
    v_all = nc.dram_tensor("v_all", [NCORES * 2048, D], BF16, kind="Internal").ap()
    w1q_scr = nc.dram_tensor("w1q_scr", [8, 128, HID], BF16, kind="Internal").ap()
    w2q_scr = nc.dram_tensor("w2q_scr", [32, 128, D], BF16, kind="Internal").ap()

    q_fm = nc.alloc_sbuf_tensor("q_fm", [128, 8, NQ * 128], BF16)
    dmask_sb = nc.alloc_sbuf_tensor("dmask_sb", [128, 1024], F32)
    cw_sb = nc.alloc_sbuf_tensor("cw_sb", [128, 128], F32)
    identb_sb = nc.alloc_sbuf_tensor("identb_sb", [128, 128], BF16)
    identf_sb = nc.alloc_sbuf_tensor("identf_sb", [128, 128], F32)
    hvalid_sb = nc.alloc_sbuf_tensor("hvalid_sb", [128, 1], F32)
    ones_col_sb = nc.alloc_sbuf_tensor("ones_col_sb", [128, 1], F32)
    ones_row_sb = nc.alloc_sbuf_tensor("ones_row_sb", [1, 128], F32)
    swb = nc.alloc_sbuf_tensor("swb", [128, 4], F32)
    nm_sb = nc.alloc_sbuf_tensor("nm_sb", [128, NQ], F32)
    l_sb = nc.alloc_sbuf_tensor("l_sb", [128, NQ], F32)
    sml = nc.alloc_sbuf_tensor("sml", [128, 16], F32)
    (A_ALPHA0, A_ALPHA1, A_LP0, A_LP1, A_NMRAW, A_NMNEW, A_MDIFF,
     A_LINV, A_NEGMEAN, A_SSQ, A_VAR, A_RSTD, A_AMAX, A_SX, A_SXINV, A_T15) = range(16)
    w4 = nc.alloc_sbuf_tensor("w4", [1, 8], F32)
    w4b = nc.alloc_sbuf_tensor("w4b", [1, 4], BF16)
    ones_col_bf = nc.alloc_sbuf_tensor("ones_col_bf", [128, 1], BF16)
    ones_row_bf = nc.alloc_sbuf_tensor("ones_row_bf", [1, 128], BF16)

    base = nc.lookup_mloc(w4).addr + 64
    base = (base + 63) // 64 * 64
    ar = Arena(nc, base, 229376)

    obuf = ar.alloc("obuf", [128, NQ, 1024], F32)
    p2base = ar.mark()

    ps_a = nc.alloc_psum_tensor("ps_a", [128, 1024], F32)
    ps_pt = nc.alloc_psum_tensor("ps_pt", [128, 8, 128], BF16)
    ps_b = nc.alloc_psum_tensor("ps_b", [128, 1024], F32)
    ps_t2 = nc.alloc_psum_tensor("ps_t2", [128, 4, 128], F32)
    ps_sm = nc.alloc_psum_tensor("ps_sm", [128, 8], F32)

    p = Plan(nc)
    for s in ("dy", "dg", "ds", "t", "v", "s", "bar", "g"):
        p.sem(s)
    E = p.emit
    C = p.c

    for nm_, dst in (("dmask", dmask_sb), ("cw", cw_sb), ("identb", identb_sb),
                     ("identf", identf_sb), ("hvalid", hvalid_sb),
                     ("ones_col", ones_col_sb), ("ones_row", ones_row_sb)):
        E("S", lambda e, nm_=nm_, dst=dst: e.dma_start(out=dst[:], in_=d[nm_]),
          inc="ds", inc_n=16)
    smalls_done = C("ds")
    E("G", lambda e: e.dma_start(out=ones_col_bf[:], in_=d["ones_col"]),
      inc="dg", inc_n=16)
    E("G", lambda e: e.dma_start(out=ones_row_bf[:], in_=d["ones_row"]),
      inc="dg", inc_n=16)
    p.wait("T", [("dg", C("dg"))])

    ar.reset(p2base)
    wtile = ar.alloc("wtile", [128, 2, 4096], F32)
    wqtile = ar.alloc("wqtile", [128, 2, 4096], BF16)
    wacc = ar.alloc("wacc", [128, 2], F32)
    wpart = ar.alloc("wpart", [128, 40], F32)
    whl = ar.alloc("whl", [128, 4], BF16)
    whl32 = ar.alloc("whl32", [128, 2], F32)
    whl32b = ar.alloc("whl32b", [128, 2], F32)

    wt_r1 = d["w1t"].rearrange("(c p) h -> c p h", p=128)
    wt_r2 = d["w2t"].rearrange("(c p) h -> c p h", p=128)

    wtile_user = {}
    wq_store = {}

    def w_pass(which, quantize):
        ntile, src, cols = ((8, wt_r1, 4096) if which == 0 else (32, wt_r2, 1024))
        scr = w1q_scr if which == 0 else w2q_scr
        swcol = 0 if which == 0 else 2
        for i in range(ntile):
            b = i % 2
            w = []
            if b in wtile_user:
                w.append(("v", wtile_user[b]))
            ld = E("Y", lambda e, i=i, b=b, src=src, cols=cols:
                   e.dma_start(out=wtile[:, b, 0:cols], in_=src[i]),
                   waits=w, inc="dy", inc_n=16)
            if not quantize:
                col = i if which == 0 else 8 + i
                E("V", lambda e, b=b, cols=cols:
                  e.tensor_reduce(sml[:, 8:8 + cols // 1024],
                                  wtile[:, b, 0:cols].rearrange("p (a q) -> p a q", q=1024),
                                  AX.X, ALU.add, apply_absolute_value=True),
                  waits=[("dy", ld)], drain=True)
                wtile_user[b] = E("V", lambda e, cols=cols, col=col:
                                  e.tensor_reduce(wpart[:, col:col + 1],
                                                  sml[:, 8:8 + cols // 1024],
                                                  AX.X, ALU.add),
                                  inc="v", drain=True)
            else:
                w2_ = [("dy", ld)]
                if b in wq_store:
                    w2_.append(("dy", wq_store[b]))
                E("V", lambda e, b=b, cols=cols, swcol=swcol:
                  e.tensor_scalar(wtile[:, b, 0:cols], wtile[:, b, 0:cols],
                                  swb[:, swcol:swcol + 1], MAGIC, ALU.mult, ALU.add),
                  waits=w2_)
                E("V", lambda e, b=b, cols=cols:
                  e.tensor_scalar(wtile[:, b, 0:cols], wtile[:, b, 0:cols],
                                  -MAGIC, 1.0, ALU.add, ALU.min))
                vq = E("V", lambda e, b=b, cols=cols:
                       e.tensor_scalar(wqtile[:, b, 0:cols], wtile[:, b, 0:cols],
                                       -1.0, None, ALU.max), inc="v")
                wtile_user[b] = vq
                wq_store[b] = E("Y", lambda e, i=i, b=b, cols=cols, scr=scr:
                                e.dma_start(out=scr[i], in_=wqtile[:, b, 0:cols]),
                                waits=[("v", vq)], inc="dy", inc_n=16)

    w_pass(0, False)
    w_pass(1, False)

    E("V", lambda e: e.tensor_reduce(wacc[:, 0:1], wpart[:, 0:8], AX.X, ALU.add), drain=True)
    E("V", lambda e: e.tensor_reduce(wacc[:, 1:2], wpart[:, 8:40], AX.X, ALU.add), drain=True)
    E("V", lambda e: e.tensor_copy(whl[:, 0:2], wacc[:]), drain=True)
    E("V", lambda e: e.tensor_copy(whl32[:], whl[:, 0:2]), drain=True)
    E("V", lambda e: e.tensor_sub(whl32b[:], wacc[:], whl32[:]), drain=True)
    vhl = E("V", lambda e: e.tensor_copy(whl[:, 2:4], whl32b[:]), inc="v", drain=True)
    E("T", lambda e: e.matmul(ps_sm[0:1, 0:2], ones_col_bf[:], whl[:, 0:2],
                              start=True, stop=False),
      waits=[("v", vhl), ("ds", smalls_done)])
    E("T", lambda e: e.matmul(ps_sm[0:1, 0:2], ones_col_bf[:], whl[:, 2:4],
                              start=False, stop=True), inc="t")
    vt2 = E("V", lambda e: e.tensor_copy(w4[0:1, 0:2], ps_sm[0:1, 0:2]),
            waits=[("t", C("t"))], inc="v", drain=True)
    E("V", lambda e: e.tensor_copy(w4b[0:1, 0:2], w4[0:1, 0:2]), drain=True)
    E("V", lambda e: e.tensor_copy(w4[0:1, 4:6], w4b[0:1, 0:2]), drain=True)
    E("V", lambda e: e.tensor_sub(w4[0:1, 6:8], w4[0:1, 0:2], w4[0:1, 4:6]), drain=True)
    vt2 = E("V", lambda e: e.tensor_copy(w4b[0:1, 2:4], w4[0:1, 6:8]), inc="v", drain=True)
    E("T", lambda e: e.matmul(ps_sm[:, 2:4], ones_row_bf[:], w4b[0:1, 0:2],
                              start=True, stop=False), waits=[("v", vt2)])
    E("T", lambda e: e.matmul(ps_sm[:, 2:4], ones_row_bf[:], w4b[0:1, 2:4],
                              start=False, stop=True), inc="t")
    E("V", lambda e: e.tensor_scalar(swb[:, 1:2], ps_sm[:, 2:3], 1.0 / (HID * D),
                                     None, ALU.mult), waits=[("t", C("t"))], drain=True)
    E("V", lambda e: e.tensor_scalar(swb[:, 3:4], ps_sm[:, 3:4], 1.0 / (HID * D),
                                     None, ALU.mult), drain=True)
    E("V", lambda e: e.reciprocal(swb[:, 0:1], swb[:, 1:2]), drain=True)
    E("V", lambda e: e.reciprocal(swb[:, 2:3], swb[:, 3:4]), inc="v", drain=True)

    w_pass(0, True)
    w_pass(1, True)
    wq_done = C("dy")

    p.barrier()
    if _PHASES < 1:
        p.wait("Y", [("dy", C("dy"))])
        p.barrier()
        return _finish(nc, p)

    ar.reset(p2base)
    xfm = ar.alloc("xfm", [128, 2, 8, 1024], BF16)
    wq_sb = ar.alloc("wq_sb", [128, 8, 1024], BF16)
    wk_sb = ar.alloc("wk_sb", [128, 8, 1024], BF16)
    wv_sb = ar.alloc("wv_sb", [128, 8, 1024], BF16)
    kvstage = ar.alloc("kvstage", [128, 4, 1024], BF16)

    for nm_, dst in (("qw", wq_sb), ("kw", wk_sb), ("vw", wv_sb)):
        E("G", lambda e, nm_=nm_, dst=dst:
          e.dma_start(out=dst[:], in_=d[nm_].rearrange("(c p) n -> p c n", p=128)),
          inc="dg", inc_n=16)
    w_qkv_loaded = C("dg")

    xT_r = d["xT_own"].rearrange("(c p) n -> p c n", p=128)
    k_own_r = k_own.rearrange("(c p) n -> c p n", p=128)
    v_own_r = v_own.rearrange("(t p) n -> t p n", p=128)

    stage_rot = [0]
    stage_user = {}

    def evac_store(psum, qcols, waits_t, store_fn):
        sl = stage_rot[0] % 4
        stage_rot[0] += 1
        w = list(waits_t)
        if sl in stage_user:
            w.append(("dy", stage_user[sl]))
        vc = E("V", lambda e, sl=sl, qcols=qcols, psum=psum:
               e.tensor_copy(kvstage[:, sl, 0:qcols], psum[:, 0:qcols]),
               waits=w, inc="v")
        dy = E("Y", lambda e, sl=sl, qcols=qcols, store_fn=store_fn:
               store_fn(e, kvstage[:, sl, 0:qcols]),
               waits=[("v", vc)], inc="dy", inc_n=16)
        stage_user[sl] = dy

    xfm_user = {}

    chunks = [(0, 1024, 0), (1024, 1024, 1), (2048, 128, 0)]
    if _P1B < 2:
        chunks = []
    for c0, cn, b in chunks:
        w = [("t", xfm_user[b])] if b in xfm_user else []
        ld = E("G", lambda e, b=b, c0=c0, cn=cn:
               e.dma_start(out=xfm[:, b, :, 0:cn], in_=xT_r[:, :, c0:c0 + cn]),
               waits=w, inc="dg", inc_n=16)
        for do in range(8):
            psum = [ps_a, ps_b][do % 2]
            hs = list(range(0, cn, 512))
            tC = None
            for h in hs:
                hn = min(512, cn - h)
                for di in range(8):
                    waits = []
                    if di == 0 and h == 0:
                        waits = [("dg", max(ld, w_qkv_loaded)), ("v", C("v"))]
                    last = (h == hs[-1] and di == 7)
                    tC = E("T", lambda e, psum=psum, h=h, hn=hn, di=di, do=do, b=b,
                           start=(di == 0), stop=(di == 7):
                           e.matmul(psum[:, h:h + hn],
                                    wq_sb[:, di, do * 128:(do + 1) * 128],
                                    xfm[:, b, di, h:h + hn], start=start, stop=stop),
                           waits=waits, inc="t" if last else None)
            xfm_user[b] = tC
            E("V", lambda e, psum=psum, do=do, c0=c0, cn=cn:
              e.tensor_copy(q_fm[:, do, c0:c0 + cn], psum[:, 0:cn]),
              waits=[("t", tC)], inc="v")

    for ci in range(2 if _P1B >= 3 else 0):
        c0 = ci * 1024
        b = ci % 2
        ld = E("G", lambda e, b=b, c0=c0:
               e.dma_start(out=xfm[:, b, :, :], in_=xT_r[:, :, c0:c0 + 1024]),
               waits=[("t", xfm_user[b])], inc="dg", inc_n=16)
        for do in range(8):
            psum = [ps_a, ps_b][do % 2]
            tC = None
            for h in range(0, 1024, 512):
                for di in range(8):
                    waits = [("dg", ld), ("v", C("v"))] if (di == 0 and h == 0) else []
                    last = (h == 512 and di == 7)
                    tC = E("T", lambda e, psum=psum, h=h, di=di, do=do, b=b,
                           start=(di == 0), stop=(di == 7):
                           e.matmul(psum[:, h:h + 512],
                                    wk_sb[:, di, do * 128:(do + 1) * 128],
                                    xfm[:, b, di, h:h + 512], start=start, stop=stop),
                           waits=waits, inc="t" if last else None)
            evac_store(psum, 1024, [("t", tC)],
                       lambda e, src, do=do, c0=c0:
                       e.dma_start(out=k_own_r[do][:, c0:c0 + 1024], in_=src))
        for tc in range(8 if _P1B >= 4 else 0):
            psum = [ps_a, ps_b][tc % 2]
            tC = None
            for h in range(0, 1024, 512):
                for di in range(8):
                    waits = [("v", C("v"))] if (di == 0 and h == 0) else []
                    last = (h == 512 and di == 7)
                    tC = E("T", lambda e, psum=psum, h=h, di=di, tc=tc, b=b,
                           start=(di == 0), stop=(di == 7):
                           e.matmul(psum[:, h:h + 512],
                                    xfm[:, b, di, tc * 128:(tc + 1) * 128],
                                    wv_sb[:, di, h:h + 512], start=start, stop=stop),
                           waits=waits, inc="t" if last else None)
            xfm_user[b] = tC
            evac_store(psum, 1024, [("t", tC)],
                       lambda e, src, ci=ci, tc=tc:
                       e.dma_start(out=v_own_r[ci * 8 + tc], in_=src))

    if _P1B < 5:
        ag_done = 0
    elif _SKIP_AG:
        E("G", lambda e: e.dma_start(out=k_all[0:D, :], in_=k_own),
          waits=[("dy", C("dy"))], inc="g", inc_n=16)
        E("G", lambda e: e.dma_start(out=v_all[0:2048, :], in_=v_own),
          inc="g", inc_n=16)
        ag_done = C("g")
    else:
        E("G", lambda e: e.collective_compute(
            "AllGather", ALU.bypass, replica_groups=[list(range(NCORES))],
            ins=[k_own], outs=[k_all]),
          waits=[("dy", C("dy"))], inc="g")
        E("G", lambda e: e.collective_compute(
            "AllGather", ALU.bypass, replica_groups=[list(range(NCORES))],
            ins=[v_own], outs=[v_all]), inc="g")
        ag_done = C("g")

    p.barrier()
    if _PHASES < 2:
        p.wait("Y", [("dy", C("dy"))])
        p.barrier()
        return _finish(nc, p)

    ar.reset(p2base)
    kbuf = ar.alloc("kbuf", [128, 2, 8, 1024], BF16)
    vbuf = ar.alloc("vbuf", [128, 2, 8, 1024], BF16)
    p_sb = ar.alloc("p_sb", [128, 1024], BF16)
    pt_sb = ar.alloc("pt_sb", [128, 8, 128], BF16)
    hmbuf = ar.alloc("hmbuf", [128, 2, 1024], F32)
    if _DEBUG:
        dbgbuf = ar.alloc("dbgbuf", [128, 1024], F32)

    E("V", lambda e: e.memset(obuf[:], 0.0), inc="v")
    E("V", lambda e: e.memset(l_sb[:], 0.0), inc="v")
    E("V", lambda e: e.memset(nm_sb[:], NM_INIT), inc="v")

    k_all_r = k_all.rearrange("(j c p) n -> p c j n", p=128, c=8)
    v_all_r = v_all.rearrange("(j t p) n -> p j t n", p=128, t=16)

    kt_load = {}
    kt_users = {}
    hm_load = {}
    hm_users = {}

    def load_kt(kt):
        b = kt % 2
        w = [("g", ag_done)]
        if b in kt_users:
            w.append(("t", kt_users[b]))
        for dc in range(8):
            E("Y", lambda e, kt=kt, b=b, dc=dc:
              e.dma_start(out=kbuf[:, b, dc, :].rearrange("p (j n) -> p j n", n=128),
                          in_=k_all_r[:, dc, :, kt * 128:(kt + 1) * 128]),
              waits=(w if dc == 0 else []), inc="dy", inc_n=16)
        kt_load[kt] = E("Y", lambda e, kt=kt, b=b:
                        e.dma_start(out=vbuf[:, b, :, :], in_=v_all_r[:, :, kt, :]),
                        inc="dy", inc_n=16)
        w = [("v", hm_users[b])] if b in hm_users else []
        hm_load[kt] = E("G", lambda e, kt=kt, b=b:
                        e.dma_start(out=hmbuf[:, b, :], in_=d["hmask"][kt]),
                        waits=w, inc="dg", inc_n=16)

    pair_idx = [0]
    prev_exp = [0]
    prev_nm = [0]
    prev_evac = [(0, 0)]
    prev_oupd = [0]

    def flash_pair(kt, ss):
        n = pair_idx[0]
        pair_idx[0] += 1
        par = n % 2
        b = kt % 2
        tsc = None
        for di in range(8):
            if _PSTEP < 1:
                break
            waits = []
            if di == 0:
                waits = [("dy", kt_load[kt]), ("s", prev_exp[0]), ("v", prev_nm[0])]
                if ss == NQ - 1:
                    waits.append(("dg", hm_load[kt]))
            for h in range(2):
                last = (di == 7 and h == 1)
                tsc = E("T", lambda e, di=di, h=h, ss=ss, b=b,
                        start=(di == 0), stop=(di == 7):
                        e.matmul(ps_a[:, h * 512:(h + 1) * 512],
                                 q_fm[:, di, ss * 128:(ss + 1) * 128],
                                 kbuf[:, b, di, h * 512:(h + 1) * 512],
                                 start=start, stop=stop),
                        waits=waits if h == 0 else [],
                        inc="t" if last else None)
                waits = []
        if _PSTEP < 2:
            return
        if ss == kt or ss == NQ - 1:
            mask_fn = ((lambda: dmask_sb[:]) if ss == kt
                       else (lambda b=b: hmbuf[:, b, :]))
            madd = E("V", lambda e, mask_fn=mask_fn:
                     e.tensor_tensor(ps_a[:], ps_a[:], mask_fn(), ALU.add),
                     waits=[("t", tsc)], inc="v")
            if ss == NQ - 1:
                hm_users[b] = madd
        E("V", lambda e:
          e.tensor_reduce(sml[:, A_NMRAW:A_NMRAW + 1], ps_a[:], AX.X, ALU.max,
                          negate=True), waits=[("t", tsc)], drain=True)
        E("V", lambda e:
          e.tensor_scalar(sml[:, A_T15:A_T15 + 1], sml[:, A_NMRAW:A_NMRAW + 1],
                          1.0 / 32.0, None, ALU.mult), drain=True)
        E("V", lambda e, ss=ss:
          e.tensor_sub(sml[:, A_MDIFF:A_MDIFF + 1], nm_sb[:, ss:ss + 1],
                       sml[:, A_T15:A_T15 + 1]), drain=True)
        E("V", lambda e:
          e.tensor_relu(sml[:, A_NMRAW:A_NMRAW + 1], sml[:, A_MDIFF:A_MDIFF + 1]), drain=True)
        E("V", lambda e:
          e.tensor_scalar(sml[:, A_MDIFF:A_MDIFF + 1], sml[:, A_NMRAW:A_NMRAW + 1],
                          -1.0, None, ALU.mult), drain=True)
        E("V", lambda e, ss=ss:
          e.tensor_scalar(sml[:, A_NMNEW:A_NMNEW + 1], sml[:, A_MDIFF:A_MDIFF + 1],
                          nm_sb[:, ss:ss + 1], None, ALU.add), drain=True)
        if _DEBUG and n == 0:
            E("V", lambda e: e.tensor_copy(dbgbuf[:, 0:1], sml[:, A_NMRAW:A_NMRAW + 1]))
            E("V", lambda e: e.tensor_copy(dbgbuf[:, 1:2], sml[:, A_NMNEW:A_NMNEW + 1]))
            E("V", lambda e, ss=ss: e.tensor_copy(dbgbuf[:, 2:3], nm_sb[:, ss:ss + 1]))
            E("V", lambda e: e.tensor_copy(dbgbuf[:, 3:4], sml[:, A_MDIFF:A_MDIFF + 1]))
        nmv = E("V", lambda e, ss=ss:
                e.tensor_copy(nm_sb[:, ss:ss + 1], sml[:, A_NMNEW:A_NMNEW + 1]),
                inc="v")
        prev_nm[0] = nmv
        if _PSTEP < 3:
            return
        aslot = A_ALPHA0 + par
        lslot = A_LP0 + par
        E("S", lambda e, aslot=aslot:
          e.activation(sml[:, aslot:aslot + 1], sml[:, A_MDIFF:A_MDIFF + 1], AF.Exp),
          waits=[("v", nmv)])
        sexp = E("S", lambda e, lslot=lslot:
                 e.activation(p_sb[:], ps_a[:], AF.Exp,
                              bias=sml[:, A_NMNEW:A_NMNEW + 1],
                              scale=1.0 / 32.0, accum_out=sml[:, lslot:lslot + 1]),
                 inc="s")
        prev_exp[0] = sexp
        if _DEBUG and n == 0:
            dmp = E("Y", lambda e: e.dma_start(out=dbg_sml, in_=dbgbuf[:, 0:16]),
                    waits=[("v", nmv), ("s", sexp)], inc="dy", inc_n=16)
            dmp = E("Y", lambda e: e.dma_start(out=dbg_p, in_=p_sb[:]),
                    inc="dy", inc_n=16)
            vdbg = E("V", lambda e: e.tensor_copy(dbgbuf[:], ps_a[:]),
                     waits=[("dy", dmp)], inc="v")
            dmp = E("Y", lambda e: e.dma_start(out=dbg_s, in_=dbgbuf[:]),
                    waits=[("v", vdbg)], inc="dy", inc_n=16)
            p.wait("S", [("dy", dmp)])
            p.wait("V", [("dy", dmp)])
        E("V", lambda e, ss=ss, aslot=aslot:
          e.tensor_scalar(sml[:, A_T15:A_T15 + 1], l_sb[:, ss:ss + 1],
                          sml[:, aslot:aslot + 1], None, ALU.mult),
          waits=[("s", sexp)], drain=True)
        E("V", lambda e, ss=ss, lslot=lslot:
          e.tensor_scalar(l_sb[:, ss:ss + 1], sml[:, A_T15:A_T15 + 1],
                          sml[:, lslot:lslot + 1], None, ALU.add), inc="v")
        if _PSTEP < 4:
            return
        ev, es = prev_evac[0]
        ttr = None
        for c in range(8):
            w = []
            if c == 0:
                w = [("s", sexp), ("v", ev), ("s", es)]
            ttr = E("T", lambda e, c=c:
                    e.transpose(ps_pt[:, c, :], p_sb[:, c * 128:(c + 1) * 128],
                                identb_sb[:]),
                    waits=w, inc="t" if c == 7 else None)
        E("V", lambda e: e.tensor_copy(pt_sb[:], ps_pt[:]),
          waits=[("t", ttr)], inc="v")
        prev_evac[0] = (C("v"), C("s"))
        evac_v, evac_s = prev_evac[0]
        if _PSTEP < 5:
            return
        tpv = None
        for kc in range(8):
            waits = []
            if kc == 0:
                waits = [("v", evac_v), ("s", evac_s), ("v", prev_oupd[0])]
            for h in range(2):
                last = (kc == 7 and h == 1)
                tpv = E("T", lambda e, kc=kc, h=h, b=b, start=(kc == 0), stop=(kc == 7):
                        e.matmul(ps_b[:, h * 512:(h + 1) * 512], pt_sb[:, kc, :],
                                 vbuf[:, b, kc, h * 512:(h + 1) * 512],
                                 start=start, stop=stop),
                        waits=waits if h == 0 else [],
                        inc="t" if last else None)
                waits = []
        kt_users[b] = tpv
        if _PSTEP < 6:
            return
        for h in range(2):
            E("V", lambda e, ss=ss, h=h, aslot=aslot:
              e.scalar_tensor_tensor(obuf[:, ss, h * 512:(h + 1) * 512],
                                     obuf[:, ss, h * 512:(h + 1) * 512],
                                     sml[:, aslot:aslot + 1],
                                     ps_b[:, h * 512:(h + 1) * 512],
                                     ALU.mult, ALU.add),
              waits=[("t", tpv)] if h == 0 else [], inc="v")
        prev_oupd[0] = C("v")

    load_kt(0)
    if _DEBUG:
        E("Y", lambda e: e.dma_start(out=dbg_kt, in_=kbuf[:, 0, :, :]),
          waits=[("dy", kt_load[0])], inc="dy", inc_n=16)
        E("Y", lambda e: e.dma_start(out=dbg_vt, in_=vbuf[:, 0, :, :]),
          inc="dy", inc_n=16)
    for kt in range(min(KT, _KT)):
        if kt + 1 < min(KT, _KT):
            load_kt(kt + 1)
        for ss in list(range(kt, NSUB)) + [NQ - 1]:
            if pair_idx[0] < _NPAIR:
                flash_pair(kt, ss)

    p.barrier()
    if _DEBUG:
        for ss in range(NQ):
            E("Y", lambda e, ss=ss: e.dma_start(out=dbg_o[ss], in_=obuf[:, ss, :]),
              inc="dy", inc_n=16)
        E("Y", lambda e: e.dma_start(out=dbg_l, in_=l_sb[:]), inc="dy", inc_n=16)
        E("Y", lambda e: e.dma_start(out=dbg_nm, in_=nm_sb[:]), inc="dy", inc_n=16)
        E("Y", lambda e: e.dma_start(out=dbg_q, in_=q_fm[:]), inc="dy", inc_n=16)
        E("Y", lambda e: e.dma_start(out=dbg_sw, in_=swb[:]), inc="dy", inc_n=16)
        p.wait("Y", [("dy", C("dy"))])
        p.barrier()
    if _PHASES < 3:
        p.wait("Y", [("dy", C("dy"))])
        p.barrier()
        return _finish(nc, p)

    ar.reset(p2base)
    y_tm = ar.alloc("y_tm", [128, GRP, HID], F32)
    xq_fm = ar.alloc("xq_fm", [128, 8, GRP * 128], BF16)
    xqh_fm = ar.alloc("xqh_fm", [128, 8, 128], BF16)
    h_fm = ar.alloc("h_fm", [128, 2, GRP, 130], F32)
    ysil = ar.alloc("ysil", [128, 2, GRP, 128], F32)
    w1q_sb = ar.alloc("w1q_sb", [128, 2, 8, 128], BF16)
    w2q_sb = ar.alloc("w2q_sb", [128, 2, 1024], BF16)
    xtile = ar.alloc("xtile", [128, 2, 1024], F32)
    scr1 = ar.alloc("scr1", [128, 1024], F32)
    scr2 = ar.alloc("scr2", [128, 1024], F32)
    xq_tm = ar.alloc("xq_tm", [128, 1024], BF16)
    yq_tm = ar.alloc("yq_tm", [128, HID], BF16)
    yq_fm0 = ar.alloc("yq_fm0", [128, 32, 128], BF16)
    yq_fm1 = ar.alloc("yq_fm1", [128, 32, 128], BF16)
    outbuf = ar.alloc("outbuf", [128, 1024], F32)

    w1q_r = w1q_scr.rearrange("c p h -> p c h")

    xt_users = {}

    def load_xtile(ss, b):
        w = [("v", xt_users[b])] if b in xt_users else []
        return E("G", lambda e, ss=ss, b=b:
                 e.dma_start(out=xtile[:, b, :],
                             in_=d["x_tm"][ss * 128:(ss + 1) * 128, :]),
                 waits=w, inc="dg", inc_n=16)

    def ln_quant1(ss, b, xld, dest_fn):
        E("V", lambda e, ss=ss:
          e.reciprocal(sml[:, A_LINV:A_LINV + 1], l_sb[:, ss:ss + 1]),
          waits=[("dg", xld)], drain=True)
        E("V", lambda e, ss=ss, b=b:
          e.scalar_tensor_tensor(obuf[:, ss, :], obuf[:, ss, :],
                                 sml[:, A_LINV:A_LINV + 1], xtile[:, b, :],
                                 ALU.mult, ALU.add))
        E("V", lambda e, ss=ss:
          e.tensor_reduce(sml[:, A_T15:A_T15 + 1], obuf[:, ss, :], AX.X, ALU.add), drain=True)
        E("V", lambda e:
          e.tensor_scalar(sml[:, A_NEGMEAN:A_NEGMEAN + 1],
                          sml[:, A_T15:A_T15 + 1], -1.0 / D, None, ALU.mult), drain=True)
        E("V", lambda e, ss=ss:
          e.tensor_scalar(obuf[:, ss, :], obuf[:, ss, :],
                          sml[:, A_NEGMEAN:A_NEGMEAN + 1], None, ALU.add))
        E("V", lambda e, ss=ss:
          e.tensor_tensor(scr2[:], obuf[:, ss, :], obuf[:, ss, :], ALU.mult))
        E("V", lambda e:
          e.tensor_reduce(sml[:, A_SSQ:A_SSQ + 1], scr2[:], AX.X, ALU.add), drain=True)
        vv = E("V", lambda e:
               e.tensor_scalar(sml[:, A_VAR:A_VAR + 1], sml[:, A_SSQ:A_SSQ + 1],
                               1.0 / D, EPS, ALU.mult, ALU.add), inc="v", drain=True)
        sq = E("S", lambda e:
               e.activation(sml[:, A_RSTD:A_RSTD + 1], sml[:, A_VAR:A_VAR + 1],
                            AF.Sqrt), waits=[("v", vv)], inc="s")
        E("V", lambda e:
          e.reciprocal(sml[:, A_T15:A_T15 + 1], sml[:, A_RSTD:A_RSTD + 1]),
          waits=[("s", sq)], drain=True)
        E("V", lambda e, ss=ss:
          e.tensor_scalar(obuf[:, ss, :], obuf[:, ss, :],
                          sml[:, A_T15:A_T15 + 1], None, ALU.mult))
        E("V", lambda e, ss=ss:
          e.tensor_reduce(sml[:, A_AMAX:A_AMAX + 1], obuf[:, ss, :], AX.X, ALU.max,
                          apply_absolute_value=True), drain=True)
        E("V", lambda e:
          e.tensor_scalar(sml[:, A_T15:A_T15 + 1], sml[:, A_AMAX:A_AMAX + 1],
                          1e-5, None, ALU.max), drain=True)
        E("V", lambda e: e.reciprocal(sml[:, A_AMAX:A_AMAX + 1], sml[:, A_T15:A_T15 + 1]), drain=True)
        E("V", lambda e:
          e.tensor_scalar(sml[:, A_SX:A_SX + 1], sml[:, A_AMAX:A_AMAX + 1], 127.0,
                          None, ALU.mult), drain=True)
        E("V", lambda e: e.reciprocal(sml[:, A_SXINV:A_SXINV + 1], sml[:, A_SX:A_SX + 1]), drain=True)
        sxinv_slot = A_SXINV
        if ss == NQ - 1:
            E("V", lambda e:
              e.tensor_scalar(sml[:, A_MDIFF:A_MDIFF + 1], sml[:, A_SXINV:A_SXINV + 1],
                              hvalid_sb[:], None, ALU.mult), drain=True)
            sxinv_slot = A_MDIFF
        E("V", lambda e, ss=ss:
          e.tensor_scalar(scr1[:], obuf[:, ss, :], sml[:, A_SX:A_SX + 1], MAGIC,
                          ALU.mult, ALU.add))
        vq = E("V", lambda e, sxinv_slot=sxinv_slot:
               e.tensor_scalar(xq_tm[:], scr1[:], -MAGIC,
                               sml[:, sxinv_slot:sxinv_slot + 1],
                               ALU.add, ALU.mult), inc="v")
        tt = None
        for c in range(8):
            w = [("v", vq), ("s", C("s"))] if c == 0 else []
            tt = E("T", lambda e, c=c:
                   e.transpose(ps_pt[:, c, :], xq_tm[:, c * 128:(c + 1) * 128],
                               identb_sb[:]),
                   waits=w, inc="t" if c == 7 else None)
        E("V", lambda e, dest_fn=dest_fn: e.tensor_copy(dest_fn(), ps_pt[:]),
          waits=[("t", tt)], inc="v")
        return C("v"), C("s")

    xldh = load_xtile(16, 0)
    hv, hs = ln_quant1(16, 0, xldh, lambda: xqh_fm[:])
    xt_users[0] = hv

    w1ld_user = {}
    hevac_user = {}
    ysil_user = {}
    yevac_user = {}
    w2ld_user = {}
    out_dma = [0]

    for g in range(NGRP):
        s0, s1 = 2 * g, 2 * g + 1
        for sl, ss in enumerate((s0, s1)):
            b = ss % 2
            xld = load_xtile(ss, b)
            ln_quant1(ss, b, xld,
                      lambda sl=sl: xq_fm[:, :, sl * 128:(sl + 1) * 128])
            xt_users[b] = C("v")
        xq_v, xq_s = C("v"), C("s")
        if _DEBUG:
            for sl, ss in enumerate((s0, s1)):
                E("Y", lambda e, ss=ss: e.dma_start(out=dbg_xln[ss], in_=obuf[:, ss, :]),
                  waits=[("v", xq_v)], inc="dy", inc_n=16)
            if g == 0:
                E("Y", lambda e: e.dma_start(out=dbg_xln[16], in_=obuf[:, 16, :]),
                  inc="dy", inc_n=16)

        for hc in range(32):
            wb = hc % 2
            w = [("dy", wq_done)]
            if wb in w1ld_user:
                w.append(("t", w1ld_user[wb]))
            ldw1 = E("Y", lambda e, hc=hc, wb=wb:
                     e.dma_start(out=w1q_sb[:, wb, :, :],
                                 in_=w1q_r[:, :, hc * 128:(hc + 1) * 128]),
                     waits=w, inc="dy", inc_n=16)

            hb = hc % 2
            po = ps_b[:, hb * 512:hb * 512 + 256]
            ph = ps_b[:, hb * 512 + 256:hb * 512 + 384]
            for di in range(8):
                waits = []
                if di == 0:
                    waits = [("dy", ldw1), ("s", xq_s), ("v", xq_v),
                             ("v", hevac_user.get(hb, 0))]
                E("T", lambda e, di=di, wb=wb, po=po,
                  start=(di == 0), stop=(di == 7):
                  e.matmul(po, w1q_sb[:, wb, di, :], xq_fm[:, di, :],
                           start=start, stop=stop),
                  waits=waits)
            tw1 = None
            for di in range(8):
                waits = []
                if g == 0 and hc == 0 and di == 0:
                    waits = [("v", hv), ("s", hs)]
                tw1 = E("T", lambda e, di=di, wb=wb, ph=ph,
                        start=(di == 0), stop=(di == 7):
                        e.matmul(ph, w1q_sb[:, wb, di, :], xqh_fm[:, di, :],
                                 start=start, stop=stop),
                        waits=waits, inc="t" if di == 7 else None)
            w1ld_user[wb] = tw1

            E("V", lambda e, hb=hb, po=po:
              e.tensor_scalar(h_fm[:, hb, :, 1:129],
                              po.rearrange("p (g n) -> p g n", g=GRP),
                              swb[:, 1:2], None, ALU.mult),
              waits=[("t", tw1)])
            for si, (sl, ss) in enumerate(((0, s0), (1, s1))):
                E("V", lambda e, hb=hb, sl=sl, ss=ss, ph=ph:
                  e.tensor_scalar(h_fm[:, hb, sl, 0:1], ph[:, 2 * ss:2 * ss + 1],
                                  swb[:, 1:2], None, ALU.mult))
                hevac_user[hb] = E(
                    "V", lambda e, hb=hb, sl=sl, ss=ss, ph=ph:
                    e.tensor_scalar(h_fm[:, hb, sl, 129:130],
                                    ph[:, 2 * ss + 1:2 * ss + 2],
                                    swb[:, 1:2], None, ALU.mult),
                    inc="v" if si == 1 else None)

            cview = scr2[:, 0:GRP * 128].rearrange("p (g n) -> p g n", g=GRP)
            E("V", lambda e, hb=hb, hc=hc, cview=cview:
              e.tensor_scalar(cview, h_fm[:, hb, :, 0:128], cw_sb[:, hc:hc + 1],
                              None, ALU.mult))
            E("V", lambda e, hb=hb, hc=hc, cview=cview:
              e.scalar_tensor_tensor(cview, h_fm[:, hb, :, 1:129],
                                     cw_sb[:, 32 + hc:33 + hc], cview,
                                     ALU.mult, ALU.add))
            vc3 = E("V", lambda e, hb=hb, hc=hc, cview=cview:
                    e.scalar_tensor_tensor(cview, h_fm[:, hb, :, 2:130],
                                           cw_sb[:, 64 + hc:65 + hc], cview,
                                           ALU.mult, ALU.add), inc="v")
            w = [("v", vc3)]
            if hb in ysil_user:
                w.append(("t", ysil_user[hb]))
            ssil = E("S", lambda e, hb=hb, hc=hc, cview=cview:
                     e.activation(ysil[:, hb, :, :], cview, AF.Silu,
                                  bias=cw_sb[:, 96 + hc:97 + hc]),
                     waits=w, inc="s")
            tty = None
            for sl in range(GRP):
                w = []
                if sl == 0:
                    w = [("s", ssil)]
                    if (hc % 2) in yevac_user:
                        yv, ys = yevac_user[hc % 2]
                        w += [("v", yv), ("s", ys)]
                tty = E("T", lambda e, hb=hb, sl=sl:
                        e.transpose(ps_t2[:, sl, :], ysil[:, hb, sl, :], identf_sb[:]),
                        waits=w, inc="t" if sl == GRP - 1 else None)
            ysil_user[hb] = tty
            E("V", lambda e, hc=hc:
              e.tensor_copy(y_tm[:, :, hc * 128:(hc + 1) * 128], ps_t2[:, 0:GRP, :]),
              waits=[("t", tty)], inc="v")
            yevac_user[hc % 2] = (C("v"), C("s"))
        y_v, y_s = C("v"), C("s")

        for sl, yqf in ((0, yq_fm0), (1, yq_fm1)):
            E("V", lambda e, sl=sl:
              e.tensor_reduce(sml[:, A_AMAX:A_AMAX + 1], y_tm[:, sl, :], AX.X,
                              ALU.max, apply_absolute_value=True),
              waits=[("v", y_v), ("s", y_s)] if sl == 0 else [], drain=True)
            E("V", lambda e:
              e.tensor_scalar(sml[:, A_T15:A_T15 + 1], sml[:, A_AMAX:A_AMAX + 1],
                              1e-5, None, ALU.max), drain=True)
            E("V", lambda e:
              e.reciprocal(sml[:, A_AMAX:A_AMAX + 1], sml[:, A_T15:A_T15 + 1]), drain=True)
            E("V", lambda e:
              e.tensor_scalar(sml[:, A_SX:A_SX + 1], sml[:, A_AMAX:A_AMAX + 1], 127.0,
                              None, ALU.mult), drain=True)
            E("V", lambda e:
              e.reciprocal(sml[:, A_SXINV:A_SXINV + 1], sml[:, A_SX:A_SX + 1]), drain=True)
            vyq = None
            for hh in range(4):
                E("V", lambda e, sl=sl, hh=hh:
                  e.tensor_scalar(scr1[:], y_tm[:, sl, hh * 1024:(hh + 1) * 1024],
                                  sml[:, A_SX:A_SX + 1], MAGIC, ALU.mult, ALU.add))
                vyq = E("V", lambda e, hh=hh:
                        e.tensor_scalar(yq_tm[:, hh * 1024:(hh + 1) * 1024], scr1[:],
                                        -MAGIC, sml[:, A_SXINV:A_SXINV + 1],
                                        ALU.add, ALU.mult),
                        inc="v" if hh == 3 else None)
            for blk in range(4):
                ttq = None
                for c in range(8):
                    cc = blk * 8 + c
                    w = []
                    if c == 0:
                        w = [("v", C("v") if blk else vyq), ("s", C("s"))]
                    ttq = E("T", lambda e, cc=cc, c=c:
                            e.transpose(ps_pt[:, c, :],
                                        yq_tm[:, cc * 128:(cc + 1) * 128],
                                        identb_sb[:]),
                            waits=w, inc="t" if c == 7 else None)
                E("V", lambda e, blk=blk, yqf=yqf:
                  e.tensor_copy(yqf[:, blk * 8:(blk + 1) * 8, :], ps_pt[:]),
                  waits=[("t", ttq)], inc="v")
        yq_v, yq_s = C("v"), C("s")

        for hc in range(32):
            wb = hc % 2
            w = [("dy", wq_done)]
            if wb in w2ld_user:
                w.append(("t", w2ld_user[wb]))
            ldw2 = E("S", lambda e, hc=hc, wb=wb:
                     e.dma_start(out=w2q_sb[:, wb, :], in_=w2q_scr[hc]),
                     waits=w, inc="ds", inc_n=16)
            waits = [("ds", ldw2)]
            if hc == 0:
                waits += [("v", yq_v), ("s", yq_s)]
            lastmm = None
            for sl, yqf, psd in ((0, yq_fm0, ps_a), (1, yq_fm1, ps_b)):
                for h in range(2):
                    last = (sl == 1 and h == 1)
                    lastmm = E("T", lambda e, hc=hc, h=h, wb=wb, yqf=yqf, psd=psd,
                               start=(hc == 0), stop=(hc == 31):
                               e.matmul(psd[:, h * 512:(h + 1) * 512], yqf[:, hc, :],
                                        w2q_sb[:, wb, h * 512:(h + 1) * 512],
                                        start=start, stop=stop),
                               waits=waits, inc="t" if last else None)
                    waits = []
            w2ld_user[wb] = lastmm
        tw2 = C("t")

        for sl, ss, psd in ((0, s0, ps_a), (1, s1, ps_b)):
            for h in range(2):
                E("V", lambda e, ss=ss, psd=psd, h=h:
                  e.scalar_tensor_tensor(scr1[:, h * 512:(h + 1) * 512],
                                         psd[:, h * 512:(h + 1) * 512], swb[:, 3:4],
                                         obuf[:, ss, h * 512:(h + 1) * 512],
                                         ALU.mult, ALU.add),
                  waits=[("t", tw2)] if h == 0 else [])
            E("V", lambda e:
              e.tensor_reduce(sml[:, A_T15:A_T15 + 1], scr1[:], AX.X, ALU.add), drain=True)
            E("V", lambda e:
              e.tensor_scalar(sml[:, A_NEGMEAN:A_NEGMEAN + 1],
                              sml[:, A_T15:A_T15 + 1], -1.0 / D, None, ALU.mult), drain=True)
            E("V", lambda e:
              e.tensor_scalar(scr1[:], scr1[:], sml[:, A_NEGMEAN:A_NEGMEAN + 1],
                              None, ALU.add))
            E("V", lambda e:
              e.tensor_tensor(scr2[:], scr1[:], scr1[:], ALU.mult))
            E("V", lambda e:
              e.tensor_reduce(sml[:, A_SSQ:A_SSQ + 1], scr2[:], AX.X, ALU.add), drain=True)
            vv = E("V", lambda e:
                   e.tensor_scalar(sml[:, A_VAR:A_VAR + 1], sml[:, A_SSQ:A_SSQ + 1],
                                   1.0 / D, EPS, ALU.mult, ALU.add), inc="v", drain=True)
            sq = E("S", lambda e:
                   e.activation(sml[:, A_RSTD:A_RSTD + 1], sml[:, A_VAR:A_VAR + 1],
                                AF.Sqrt), waits=[("v", vv)], inc="s")
            E("V", lambda e:
              e.reciprocal(sml[:, A_T15:A_T15 + 1], sml[:, A_RSTD:A_RSTD + 1]),
              waits=[("s", sq)], drain=True)
            vout = E("V", lambda e:
                     e.tensor_scalar(outbuf[:], scr1[:], sml[:, A_T15:A_T15 + 1],
                                     None, ALU.mult),
                     waits=[("dy", out_dma[0])], inc="v")
            out_dma[0] = E("Y", lambda e, ss=ss:
                           e.dma_start(out=y_out[ss * 128:(ss + 1) * 128, :],
                                       in_=outbuf[:]),
                           waits=[("v", vout)], inc="dy", inc_n=16)

    p.wait("Y", [("dy", C("dy"))])
    p.barrier()
    return _finish(nc, p)


def _finish(nc, p):
    with nc.Block() as block:
        @block.tensor
        def _(t):
            for fn in p.ops["T"]:
                fn(t)

        @block.vector
        def _(v):
            for fn in p.ops["V"]:
                fn(v)

        @block.scalar
        def _(s):
            for fn in p.ops["S"]:
                fn(s)

        @block.gpsimd
        def _(g):
            for fn in p.ops["G"]:
                fn(g)

        @block.sync
        def _(y):
            for fn in p.ops["Y"]:
                fn(y)

    return nc


def _host_inputs(x, Qw, Kw, Vw, w1, conv_w, conv_b, w2):
    x2 = np.ascontiguousarray(np.asarray(x, np.float32).reshape(S, D))
    xT = np.ascontiguousarray(x2.T)
    w1t = np.ascontiguousarray(np.asarray(w1, np.float32).T)
    w2t = np.ascontiguousarray(np.asarray(w2, np.float32).T)
    cwf = np.asarray(conv_w, np.float32).reshape(HID, 3)
    cbf = np.asarray(conv_b, np.float32)
    cw = np.zeros((128, 128), np.float32)
    cw[:, 0:32] = cwf[:, 0].reshape(32, 128).T
    cw[:, 32:64] = cwf[:, 1].reshape(32, 128).T
    cw[:, 64:96] = cwf[:, 2].reshape(32, 128).T
    cw[:, 96:128] = cbf.reshape(32, 128).T
    identb = np.eye(128, dtype=ml_dtypes.bfloat16)
    identf = np.eye(128, dtype=np.float32)
    ones_col = np.ones((128, 1), np.float32)
    ones_row = np.ones((1, 128), np.float32)

    in_maps = []
    for j in range(NCORES):
        own_cols = (1024 * np.arange(NSUB)[:, None] + 128 * j
                    + np.arange(128)[None, :]).reshape(-1)
        halo_pos = np.zeros(32, np.int64)
        halo_valid = np.zeros((128, 1), np.float32)
        for i in range(NSUB):
            lp = 1024 * i + 128 * j - 1
            rp = 1024 * i + 128 * j + 128
            halo_pos[2 * i] = lp if lp >= 0 else 0
            halo_pos[2 * i + 1] = rp if rp < S else 0
            halo_valid[2 * i, 0] = 1.0 if lp >= 0 else 0.0
            halo_valid[2 * i + 1, 0] = 1.0 if rp < S else 0.0
        all_cols = np.concatenate([own_cols, halo_pos, np.zeros(96, np.int64)])
        xT_own = np.ascontiguousarray(xT[:, all_cols])
        x_tm = np.ascontiguousarray(x2[all_cols, :])
        for h in range(32):
            if halo_valid[h, 0] == 0.0:
                xT_own[:, 2048 + h] = 0.0
                x_tm[2048 + h, :] = 0.0
        xT_own[:, 2048 + 32:] = 0.0
        x_tm[2048 + 32:, :] = 0.0

        r = np.arange(128)[:, None]
        c = np.arange(1024)[None, :]
        dmask = np.where(c <= 128 * j + r, 0.0, NEG).astype(np.float32)

        hmask = np.full((KT, 128, 1024), NEG, np.float32)
        for h in range(32):
            if halo_valid[h, 0] == 1.0:
                pos = int(halo_pos[h])
                for kt in range(pos // 1024 + 1):
                    n = min(1024, pos - 1024 * kt + 1)
                    hmask[kt, h, 0:n] = 0.0
            else:
                hmask[0, h, 0] = 0.0
        hmask[0, 32:, 0] = 0.0

        in_maps.append({
            "xT_own": xT_own, "x_tm": x_tm,
            "qw": np.asarray(Qw, np.float32), "kw": np.asarray(Kw, np.float32),
            "vw": np.asarray(Vw, np.float32),
            "w1t": w1t, "w2t": w2t,
            "cw": cw, "dmask": dmask, "hmask": hmask, "hvalid": halo_valid,
            "identb": identb, "identf": identf,
            "ones_col": ones_col, "ones_row": ones_row,
        })
    return in_maps


_NC_CACHE = [None]


def kernel(L0_attn_x, L0_attn_Qw, L0_attn_Kw, L0_attn_Vw,
           L0_ffn_w1, L0_ffn_conv_w, L0_ffn_conv_b, L0_ffn_w2):
    in_maps = _host_inputs(L0_attn_x, L0_attn_Qw, L0_attn_Kw, L0_attn_Vw,
                           L0_ffn_w1, L0_ffn_conv_w, L0_ffn_conv_b, L0_ffn_w2)
    if _NC_CACHE[0] is None:
        _NC_CACHE[0] = build_program()
    nc = _NC_CACHE[0]
    res = run_bass_kernel_spmd(nc, in_maps, core_ids=list(range(NCORES)))
    out = np.empty((1, S, D), np.float32)
    ov = out.reshape(NSUB, NCORES, 128, D)
    for j in range(NCORES):
        ov[:, j] = np.asarray(res.results[j]["y"]).reshape(NSUB, 128, D)
    return out


# version 06
# speedup vs baseline: 1.0244x; 1.0244x over previous
import os
import numpy as np
import ml_dtypes
import concourse.bass as bass
import concourse.mybir as mybir
from concourse.bass_utils import run_bass_kernel_spmd

F32 = mybir.dt.float32
BF16 = mybir.dt.bfloat16
AF = mybir.ActivationFunctionType
ALU = mybir.AluOpType
AX = mybir.AxisListType

NCORES = 8
S = 16384
D = 1024
HID = 4096
NSUB = 16
NQ = NSUB + 1
KT = 16
MAGIC = 12582912.0
NM_INIT = 100.0
NEG = -1e9
EPS = 1e-5
GRP = 2
NGRP = NSUB // GRP

ENGINES = ("T", "V", "S", "G", "Y")


class Plan:

    def __init__(self, nc):
        self.nc = nc
        self.ops = {e: [] for e in ENGINES}
        self.sems = {}
        self.cnt = {}
        self.bar_round = 0

    def sem(self, name):
        if name not in self.sems:
            self.sems[name] = self.nc.alloc_semaphore(name)
            self.cnt[name] = 0
        return self.sems[name]

    def emit(self, e, fn, waits=(), inc=None, inc_n=1, drain=False):
        sems = self.sems
        wl = tuple((sems[s], v) for (s, v) in waits if v > 0)

        def run(eng, fn=fn, wl=wl, inc=inc, inc_n=inc_n, drain=drain):
            for s, v in wl:
                eng.wait_ge(s, v)
            r = fn(eng)
            if drain:
                eng.drain()
            if inc is not None:
                r.then_inc(sems[inc], inc_n)

        self.ops[e].append(run)
        if inc is not None:
            self.cnt[inc] += inc_n
            return self.cnt[inc]
        return None

    def wait(self, e, waits):
        sems = self.sems
        wl = tuple((sems[s], v) for (s, v) in waits if v > 0)

        def run(eng, wl=wl):
            for s, v in wl:
                eng.wait_ge(s, v)

        self.ops[e].append(run)

    def c(self, name):
        return self.cnt[name]

    def barrier(self):
        bar = self.sem("bar")
        own_dma = {"Y": "dy", "G": "dg", "S": "ds"}
        self.bar_round += 1
        target = len(ENGINES) * self.bar_round
        for e in ENGINES:
            waits = []
            if e in own_dma:
                waits.append((own_dma[e], self.cnt[own_dma[e]]))
            def run_b(eng, bar=self.sems["bar"], waits=tuple(
                    (self.sems[s], v) for (s, v) in waits if v > 0)):
                for s_, v_ in waits:
                    eng.wait_ge(s_, v_)
                eng.drain()
                eng.sem_inc(bar, 1)
            self.ops[e].append(run_b)
            self.cnt["bar"] += 1
        for e in ENGINES:
            self.wait(e, [("bar", target)])


class Arena:
    def __init__(self, nc, base, limit):
        self.nc = nc
        self.base = base
        self.limit = limit
        self.cur = base

    def alloc(self, name, shape, dtype, align=64):
        per_part = int(np.prod(shape[1:])) * mybir.dt.size(dtype)
        off = (self.cur + align - 1) // align * align
        assert off + per_part <= self.limit, (
            f"arena overflow: {name} at {off}+{per_part} > {self.limit}")
        self.cur = off + per_part
        return self.nc.alloc_sbuf_tensor_at(name, shape, dtype, offset=off)

    def mark(self):
        return self.cur

    def reset(self, to):
        self.cur = to



def _copy(e, dst, src):
    if hasattr(e, "tensor_copy"):
        e_fn = e.tensor_copy
    else:
        e_fn = e.copy
    return e_fn(dst, src)

_PHASES = int(os.environ.get("MK_PHASES", "4"))
_SKIP_AG = os.environ.get("MK_SKIP_AG", "0") == "1"
_P1B = int(os.environ.get("MK_P1B", "5"))
_KT = int(os.environ.get("MK_KT", "16"))
_NPAIR = int(os.environ.get("MK_NPAIR", "9999"))
_PSTEP = int(os.environ.get("MK_PSTEP", "6"))
_SEV = os.environ.get("MK_SEV", "1") == "1"
_DEBUG = os.environ.get("MK_DEBUG", "0") == "1"


def build_program():
    nc = bass.Bass("TRN2", target_bir_lowering=False)

    d = {}
    d["xT_own"] = nc.dram_tensor("xT_own", [D, NQ * 128], F32, kind="ExternalInput").ap()
    d["x_tm"] = nc.dram_tensor("x_tm", [NQ * 128, D], F32, kind="ExternalInput").ap()
    d["qw"] = nc.dram_tensor("qw", [D, D], F32, kind="ExternalInput").ap()
    d["kw"] = nc.dram_tensor("kw", [D, D], F32, kind="ExternalInput").ap()
    d["vw"] = nc.dram_tensor("vw", [D, D], F32, kind="ExternalInput").ap()
    d["w1t"] = nc.dram_tensor("w1t", [D, HID], F32, kind="ExternalInput").ap()
    d["w2t"] = nc.dram_tensor("w2t", [HID, D], F32, kind="ExternalInput").ap()
    d["cw"] = nc.dram_tensor("cw", [128, 128], F32, kind="ExternalInput").ap()
    d["dmask"] = nc.dram_tensor("dmask", [128, 1024], F32, kind="ExternalInput").ap()
    d["hmask"] = nc.dram_tensor("hmask", [KT, 128, 1024], F32, kind="ExternalInput").ap()
    d["hvalid"] = nc.dram_tensor("hvalid", [128, 1], F32, kind="ExternalInput").ap()
    d["identb"] = nc.dram_tensor("identb", [128, 128], BF16, kind="ExternalInput").ap()
    d["identf"] = nc.dram_tensor("identf", [128, 128], F32, kind="ExternalInput").ap()
    d["ones_col"] = nc.dram_tensor("ones_col", [128, 1], F32, kind="ExternalInput").ap()
    d["ones_row"] = nc.dram_tensor("ones_row", [1, 128], F32, kind="ExternalInput").ap()
    y_out = nc.dram_tensor("y", [NSUB * 128, D], F32, kind="ExternalOutput").ap()
    if _DEBUG:
        dbg_o = nc.dram_tensor("dbg_o", [NQ, 128, D], F32, kind="ExternalOutput").ap()
        dbg_l = nc.dram_tensor("dbg_l", [128, NQ], F32, kind="ExternalOutput").ap()
        dbg_nm = nc.dram_tensor("dbg_nm", [128, NQ], F32, kind="ExternalOutput").ap()
        dbg_q = nc.dram_tensor("dbg_q", [128, 8, NQ * 128], BF16, kind="ExternalOutput").ap()
        dbg_sw = nc.dram_tensor("dbg_sw", [128, 4], F32, kind="ExternalOutput").ap()
        dbg_xln = nc.dram_tensor("dbg_xln", [NQ, 128, D], F32, kind="ExternalOutput").ap()
        dbg_kt = nc.dram_tensor("dbg_kt", [128, 8, 1024], BF16, kind="ExternalOutput").ap()
        dbg_vt = nc.dram_tensor("dbg_vt", [128, 8, 1024], BF16, kind="ExternalOutput").ap()
        dbg_p = nc.dram_tensor("dbg_p", [128, 1024], BF16, kind="ExternalOutput").ap()
        dbg_s = nc.dram_tensor("dbg_s", [128, 1024], F32, kind="ExternalOutput").ap()
        dbg_sml = nc.dram_tensor("dbg_sml", [128, 16], F32, kind="ExternalOutput").ap()

    k_own = nc.dram_tensor("k_own", [D, 2048], BF16, kind="Internal").ap()
    v_own = nc.dram_tensor("v_own", [2048, D], BF16, kind="Internal").ap()
    k_all = nc.dram_tensor("k_all", [NCORES * D, 2048], BF16, kind="Internal").ap()
    v_all = nc.dram_tensor("v_all", [NCORES * 2048, D], BF16, kind="Internal").ap()
    w1q_scr = nc.dram_tensor("w1q_scr", [8, 128, HID], BF16, kind="Internal").ap()
    w2q_scr = nc.dram_tensor("w2q_scr", [32, 128, D], BF16, kind="Internal").ap()

    q_fm = nc.alloc_sbuf_tensor("q_fm", [128, 8, NQ * 128], BF16)
    dmask_sb = nc.alloc_sbuf_tensor("dmask_sb", [128, 1024], F32)
    cw_sb = nc.alloc_sbuf_tensor("cw_sb", [128, 128], F32)
    identb_sb = nc.alloc_sbuf_tensor("identb_sb", [128, 128], BF16)
    identf_sb = nc.alloc_sbuf_tensor("identf_sb", [128, 128], F32)
    hvalid_sb = nc.alloc_sbuf_tensor("hvalid_sb", [128, 1], F32)
    ones_col_sb = nc.alloc_sbuf_tensor("ones_col_sb", [128, 1], F32)
    ones_row_sb = nc.alloc_sbuf_tensor("ones_row_sb", [1, 128], F32)
    swb = nc.alloc_sbuf_tensor("swb", [128, 4], F32)
    nm_sb = nc.alloc_sbuf_tensor("nm_sb", [128, NQ], F32)
    l_sb = nc.alloc_sbuf_tensor("l_sb", [128, NQ], F32)
    sml = nc.alloc_sbuf_tensor("sml", [128, 20], F32)
    (A_ALPHA0, A_ALPHA1, A_LP0, A_LP1, A_NMRAW, A_NMNEW, A_MDIFF,
     A_LINV, A_NEGMEAN, A_SSQ, A_VAR, A_RSTD, A_AMAX, A_SX, A_SXINV, A_T15) = range(16)
    w4 = nc.alloc_sbuf_tensor("w4", [1, 8], F32)
    w4b = nc.alloc_sbuf_tensor("w4b", [1, 4], BF16)
    ones_col_bf = nc.alloc_sbuf_tensor("ones_col_bf", [128, 1], BF16)
    ones_row_bf = nc.alloc_sbuf_tensor("ones_row_bf", [1, 128], BF16)

    base = nc.lookup_mloc(w4).addr + 64
    base = (base + 63) // 64 * 64
    ar = Arena(nc, base, 229376)

    obuf = ar.alloc("obuf", [128, NQ, 1024], F32)
    p2base = ar.mark()

    ps_a = nc.alloc_psum_tensor("ps_a", [128, 1024], F32)
    ps_b = nc.alloc_psum_tensor("ps_b", [128, 1024], F32)
    ps_c = nc.alloc_psum_tensor("ps_c", [128, 1024], F32)
    ps_pt = nc.alloc_psum_tensor("ps_pt", [128, 8, 128], BF16)
    ps_d = nc.alloc_psum_tensor("ps_d", [128, 512], F32)
    ps_t2 = ps_d.ap().rearrange("p (a n) -> p a n", n=128)
    ps_sm = ps_d

    p = Plan(nc)
    for s in ("dy", "dg", "ds", "t", "v", "s", "bar", "g"):
        p.sem(s)
    E = p.emit
    C = p.c

    for nm_, dst in (("dmask", dmask_sb), ("cw", cw_sb), ("identb", identb_sb),
                     ("identf", identf_sb), ("hvalid", hvalid_sb),
                     ("ones_col", ones_col_sb), ("ones_row", ones_row_sb)):
        E("S", lambda e, nm_=nm_, dst=dst: e.dma_start(out=dst[:], in_=d[nm_]),
          inc="ds", inc_n=16)
    smalls_done = C("ds")
    E("G", lambda e: e.dma_start(out=ones_col_bf[:], in_=d["ones_col"]),
      inc="dg", inc_n=16)
    E("G", lambda e: e.dma_start(out=ones_row_bf[:], in_=d["ones_row"]),
      inc="dg", inc_n=16)
    p.wait("T", [("dg", C("dg"))])

    ar.reset(p2base)
    wtile = ar.alloc("wtile", [128, 2, 4096], F32)
    wqtile = ar.alloc("wqtile", [128, 2, 4096], BF16)
    wacc = ar.alloc("wacc", [128, 2], F32)
    wpart = ar.alloc("wpart", [128, 40], F32)
    whl = ar.alloc("whl", [128, 4], BF16)
    whl32 = ar.alloc("whl32", [128, 2], F32)
    whl32b = ar.alloc("whl32b", [128, 2], F32)

    wt_r1 = d["w1t"].rearrange("(c p) h -> c p h", p=128)
    wt_r2 = d["w2t"].rearrange("(c p) h -> c p h", p=128)

    wtile_user = {}
    wq_store = {}

    def w_pass(which, quantize):
        ntile, src, cols = ((8, wt_r1, 4096) if which == 0 else (32, wt_r2, 1024))
        scr = w1q_scr if which == 0 else w2q_scr
        swcol = 0 if which == 0 else 2
        for i in range(ntile):
            b = i % 2
            w = []
            if b in wtile_user:
                w.append(("v", wtile_user[b]))
            ld = E("Y", lambda e, i=i, b=b, src=src, cols=cols:
                   e.dma_start(out=wtile[:, b, 0:cols], in_=src[i]),
                   waits=w, inc="dy", inc_n=16)
            if not quantize:
                col = i if which == 0 else 8 + i
                E("V", lambda e, b=b, cols=cols:
                  e.tensor_reduce(sml[:, 8:8 + cols // 1024],
                                  wtile[:, b, 0:cols].rearrange("p (a q) -> p a q", q=1024),
                                  AX.X, ALU.add, apply_absolute_value=True),
                  waits=[("dy", ld)], drain=True)
                wtile_user[b] = E("V", lambda e, cols=cols, col=col:
                                  e.tensor_reduce(wpart[:, col:col + 1],
                                                  sml[:, 8:8 + cols // 1024],
                                                  AX.X, ALU.add),
                                  inc="v", drain=True)
            else:
                w2_ = [("dy", ld)]
                if b in wq_store:
                    w2_.append(("dy", wq_store[b]))
                E("V", lambda e, b=b, cols=cols, swcol=swcol:
                  e.tensor_scalar(wtile[:, b, 0:cols], wtile[:, b, 0:cols],
                                  swb[:, swcol:swcol + 1], MAGIC, ALU.mult, ALU.add),
                  waits=w2_)
                E("V", lambda e, b=b, cols=cols:
                  e.tensor_scalar(wtile[:, b, 0:cols], wtile[:, b, 0:cols],
                                  -MAGIC, 1.0, ALU.add, ALU.min))
                vq = E("V", lambda e, b=b, cols=cols:
                       e.tensor_scalar(wqtile[:, b, 0:cols], wtile[:, b, 0:cols],
                                       -1.0, None, ALU.max), inc="v")
                wtile_user[b] = vq
                wq_store[b] = E("Y", lambda e, i=i, b=b, cols=cols, scr=scr:
                                e.dma_start(out=scr[i], in_=wqtile[:, b, 0:cols]),
                                waits=[("v", vq)], inc="dy", inc_n=16)

    w_pass(0, False)
    w_pass(1, False)

    E("V", lambda e: e.tensor_reduce(wacc[:, 0:1], wpart[:, 0:8], AX.X, ALU.add), drain=True)
    E("V", lambda e: e.tensor_reduce(wacc[:, 1:2], wpart[:, 8:40], AX.X, ALU.add), drain=True)
    E("V", lambda e: e.tensor_copy(whl[:, 0:2], wacc[:]), drain=True)
    E("V", lambda e: e.tensor_copy(whl32[:], whl[:, 0:2]), drain=True)
    E("V", lambda e: e.tensor_sub(whl32b[:], wacc[:], whl32[:]), drain=True)
    vhl = E("V", lambda e: e.tensor_copy(whl[:, 2:4], whl32b[:]), inc="v", drain=True)
    E("T", lambda e: e.matmul(ps_sm.ap()[0:1, 384:386], ones_col_bf[:], whl[:, 0:2],
                              start=True, stop=False),
      waits=[("v", vhl), ("ds", smalls_done)])
    E("T", lambda e: e.matmul(ps_sm.ap()[0:1, 384:386], ones_col_bf[:], whl[:, 2:4],
                              start=False, stop=True), inc="t")
    vt2 = E("V", lambda e: e.tensor_copy(w4[0:1, 0:2], ps_sm.ap()[0:1, 384:386]),
            waits=[("t", C("t"))], inc="v", drain=True)
    E("V", lambda e: e.tensor_copy(w4b[0:1, 0:2], w4[0:1, 0:2]), drain=True)
    E("V", lambda e: e.tensor_copy(w4[0:1, 4:6], w4b[0:1, 0:2]), drain=True)
    E("V", lambda e: e.tensor_sub(w4[0:1, 6:8], w4[0:1, 0:2], w4[0:1, 4:6]), drain=True)
    vt2 = E("V", lambda e: e.tensor_copy(w4b[0:1, 2:4], w4[0:1, 6:8]), inc="v", drain=True)
    E("T", lambda e: e.matmul(ps_sm.ap()[:, 386:388], ones_row_bf[:], w4b[0:1, 0:2],
                              start=True, stop=False), waits=[("v", vt2)])
    E("T", lambda e: e.matmul(ps_sm.ap()[:, 386:388], ones_row_bf[:], w4b[0:1, 2:4],
                              start=False, stop=True), inc="t")
    E("V", lambda e: e.tensor_scalar(swb[:, 1:2], ps_sm.ap()[:, 386:387], 1.0 / (HID * D),
                                     None, ALU.mult), waits=[("t", C("t"))], drain=True)
    E("V", lambda e: e.tensor_scalar(swb[:, 3:4], ps_sm.ap()[:, 387:388], 1.0 / (HID * D),
                                     None, ALU.mult), drain=True)
    E("V", lambda e: e.reciprocal(swb[:, 0:1], swb[:, 1:2]), drain=True)
    E("V", lambda e: e.reciprocal(swb[:, 2:3], swb[:, 3:4]), inc="v", drain=True)

    w_pass(0, True)
    w_pass(1, True)
    wq_done = C("dy")

    p.barrier()
    if _PHASES < 1:
        p.wait("Y", [("dy", C("dy"))])
        p.barrier()
        return _finish(nc, p)

    ar.reset(p2base)
    xfm = ar.alloc("xfm", [128, 2, 8, 1024], BF16)
    wq_sb = ar.alloc("wq_sb", [128, 8, 1024], BF16)
    wk_sb = ar.alloc("wk_sb", [128, 8, 1024], BF16)
    wv_sb = ar.alloc("wv_sb", [128, 8, 1024], BF16)
    kvstage = ar.alloc("kvstage", [128, 4, 1024], BF16)

    for nm_, dst in (("qw", wq_sb), ("kw", wk_sb), ("vw", wv_sb)):
        E("G", lambda e, nm_=nm_, dst=dst:
          e.dma_start(out=dst[:], in_=d[nm_].rearrange("(c p) n -> p c n", p=128)),
          inc="dg", inc_n=16)
    w_qkv_loaded = C("dg")

    xT_r = d["xT_own"].rearrange("(c p) n -> p c n", p=128)
    k_own_r = k_own.rearrange("(c p) n -> c p n", p=128)
    v_own_r = v_own.rearrange("(t p) n -> t p n", p=128)

    stage_rot = [0]
    stage_user = {}

    def evac_store(psum, qcols, waits_t, store_fn):
        sl = stage_rot[0] % 4
        stage_rot[0] += 1
        w = list(waits_t)
        if sl in stage_user:
            w.append(("dy", stage_user[sl]))
        vc = E("V", lambda e, sl=sl, qcols=qcols, psum=psum:
               e.tensor_copy(kvstage[:, sl, 0:qcols], psum[:, 0:qcols]),
               waits=w, inc="v")
        dy = E("Y", lambda e, sl=sl, qcols=qcols, store_fn=store_fn:
               store_fn(e, kvstage[:, sl, 0:qcols]),
               waits=[("v", vc)], inc="dy", inc_n=16)
        stage_user[sl] = dy

    xfm_user = {}

    chunks = [(0, 1024, 0), (1024, 1024, 1), (2048, 128, 0)]
    if _P1B < 2:
        chunks = []
    for c0, cn, b in chunks:
        w = [("t", xfm_user[b])] if b in xfm_user else []
        ld = E("G", lambda e, b=b, c0=c0, cn=cn:
               e.dma_start(out=xfm[:, b, :, 0:cn], in_=xT_r[:, :, c0:c0 + cn]),
               waits=w, inc="dg", inc_n=16)
        for do in range(8):
            psum = [ps_a, ps_b][do % 2]
            hs = list(range(0, cn, 512))
            tC = None
            for h in hs:
                hn = min(512, cn - h)
                for di in range(8):
                    waits = []
                    if di == 0 and h == 0:
                        waits = [("dg", max(ld, w_qkv_loaded)), ("v", C("v"))]
                    last = (h == hs[-1] and di == 7)
                    tC = E("T", lambda e, psum=psum, h=h, hn=hn, di=di, do=do, b=b,
                           start=(di == 0), stop=(di == 7):
                           e.matmul(psum[:, h:h + hn],
                                    wq_sb[:, di, do * 128:(do + 1) * 128],
                                    xfm[:, b, di, h:h + hn], start=start, stop=stop),
                           waits=waits, inc="t" if last else None)
            xfm_user[b] = tC
            E("V", lambda e, psum=psum, do=do, c0=c0, cn=cn:
              e.tensor_copy(q_fm[:, do, c0:c0 + cn], psum[:, 0:cn]),
              waits=[("t", tC)], inc="v")

    for ci in range(2 if _P1B >= 3 else 0):
        c0 = ci * 1024
        b = ci % 2
        ld = E("G", lambda e, b=b, c0=c0:
               e.dma_start(out=xfm[:, b, :, :], in_=xT_r[:, :, c0:c0 + 1024]),
               waits=[("t", xfm_user[b])], inc="dg", inc_n=16)
        for do in range(8):
            psum = [ps_a, ps_b][do % 2]
            tC = None
            for h in range(0, 1024, 512):
                for di in range(8):
                    waits = [("dg", ld), ("v", C("v"))] if (di == 0 and h == 0) else []
                    last = (h == 512 and di == 7)
                    tC = E("T", lambda e, psum=psum, h=h, di=di, do=do, b=b,
                           start=(di == 0), stop=(di == 7):
                           e.matmul(psum[:, h:h + 512],
                                    wk_sb[:, di, do * 128:(do + 1) * 128],
                                    xfm[:, b, di, h:h + 512], start=start, stop=stop),
                           waits=waits, inc="t" if last else None)
            evac_store(psum, 1024, [("t", tC)],
                       lambda e, src, do=do, c0=c0:
                       e.dma_start(out=k_own_r[do][:, c0:c0 + 1024], in_=src))
        for tc in range(8 if _P1B >= 4 else 0):
            psum = [ps_a, ps_b][tc % 2]
            tC = None
            for h in range(0, 1024, 512):
                for di in range(8):
                    waits = [("v", C("v"))] if (di == 0 and h == 0) else []
                    last = (h == 512 and di == 7)
                    tC = E("T", lambda e, psum=psum, h=h, di=di, tc=tc, b=b,
                           start=(di == 0), stop=(di == 7):
                           e.matmul(psum[:, h:h + 512],
                                    xfm[:, b, di, tc * 128:(tc + 1) * 128],
                                    wv_sb[:, di, h:h + 512], start=start, stop=stop),
                           waits=waits, inc="t" if last else None)
            xfm_user[b] = tC
            evac_store(psum, 1024, [("t", tC)],
                       lambda e, src, ci=ci, tc=tc:
                       e.dma_start(out=v_own_r[ci * 8 + tc], in_=src))

    if _P1B < 5:
        ag_done = 0
    elif _SKIP_AG:
        E("G", lambda e: e.dma_start(out=k_all[0:D, :], in_=k_own),
          waits=[("dy", C("dy"))], inc="g", inc_n=16)
        E("G", lambda e: e.dma_start(out=v_all[0:2048, :], in_=v_own),
          inc="g", inc_n=16)
        ag_done = C("g")
    else:
        E("G", lambda e: e.collective_compute(
            "AllGather", ALU.bypass, replica_groups=[list(range(NCORES))],
            ins=[k_own], outs=[k_all]),
          waits=[("dy", C("dy"))], inc="g")
        E("G", lambda e: e.collective_compute(
            "AllGather", ALU.bypass, replica_groups=[list(range(NCORES))],
            ins=[v_own], outs=[v_all]), inc="g")
        ag_done = C("g")

    p.barrier()
    if _PHASES < 2:
        p.wait("Y", [("dy", C("dy"))])
        p.barrier()
        return _finish(nc, p)

    ar.reset(p2base)
    kbuf = ar.alloc("kbuf", [128, 2, 8, 1024], BF16)
    vbuf = ar.alloc("vbuf", [128, 2, 8, 1024], BF16)
    p_sb = ar.alloc("p_sb", [128, 2, 1024], BF16)
    pt_sb = ar.alloc("pt_sb", [128, 8, 128], BF16)
    hmbuf = ar.alloc("hmbuf", [128, 2, 1024], F32)
    if _DEBUG:
        dbgbuf = ar.alloc("dbgbuf", [128, 1024], F32)

    E("V", lambda e: e.memset(obuf[:], 0.0), inc="v")
    E("V", lambda e: e.memset(l_sb[:], 0.0), inc="v")
    E("V", lambda e: e.memset(nm_sb[:], NM_INIT), inc="v")

    k_all_r = k_all.rearrange("(j c p) n -> p c j n", p=128, c=8)
    v_all_r = v_all.rearrange("(j t p) n -> p j t n", p=128, t=16)

    kt_load = {}
    kt_users = {}
    hm_load = {}
    hm_users = {}

    def load_kt(kt):
        b = kt % 2
        w = [("g", ag_done)]
        if b in kt_users:
            w.append(("t", kt_users[b]))
        for dc in range(8):
            E("Y", lambda e, kt=kt, b=b, dc=dc:
              e.dma_start(out=kbuf[:, b, dc, :].rearrange("p (j n) -> p j n", n=128),
                          in_=k_all_r[:, dc, :, kt * 128:(kt + 1) * 128]),
              waits=(w if dc == 0 else []), inc="dy", inc_n=16)
        kt_load[kt] = E("Y", lambda e, kt=kt, b=b:
                        e.dma_start(out=vbuf[:, b, :, :], in_=v_all_r[:, :, kt, :]),
                        inc="dy", inc_n=16)
        w = [("v", hm_users[b])] if b in hm_users else []
        hm_load[kt] = E("G", lambda e, kt=kt, b=b:
                        e.dma_start(out=hmbuf[:, b, :], in_=d["hmask"][kt]),
                        waits=w, inc="dg", inc_n=16)

    pair_idx = [0]
    exp_of = {}
    nm_of = {}
    prev_evac = [(0, 0)]
    prev_oupd = [0]
    pend = []

    def flash_head(kt, ss):
        n = pair_idx[0]
        pair_idx[0] += 1
        par = n % 2
        b = kt % 2
        S = [ps_a, ps_c][par]
        tsc = None
        for di in range(8):
            if _PSTEP < 1:
                break
            waits = []
            if di == 0:
                waits = [("dy", kt_load[kt]),
                         ("s", exp_of.get(n - 2, 0)), ("v", nm_of.get(n - 2, 0))]
                if ss == NQ - 1:
                    waits.append(("dg", hm_load[kt]))
            for h in range(2):
                last = (di == 7 and h == 1)
                tsc = E("T", lambda e, di=di, h=h, ss=ss, b=b, S=S,
                        start=(di == 0), stop=(di == 7):
                        e.matmul(S[:, h * 512:(h + 1) * 512],
                                 q_fm[:, di, ss * 128:(ss + 1) * 128],
                                 kbuf[:, b, di, h * 512:(h + 1) * 512],
                                 start=start, stop=stop),
                        waits=waits if h == 0 else [],
                        inc="t" if last else None)
                waits = []
        if _PSTEP < 2:
            return
        if ss == kt or ss == NQ - 1:
            mask_fn = ((lambda: dmask_sb[:]) if ss == kt
                       else (lambda b=b: hmbuf[:, b, :]))
            madd = E("V", lambda e, mask_fn=mask_fn, S=S:
                     e.tensor_tensor(S[:], S[:], mask_fn(), ALU.add),
                     waits=[("t", tsc)], inc="v")
            if ss == NQ - 1:
                hm_users[b] = madd
        E("V", lambda e, S=S:
          e.tensor_reduce(sml[:, A_NMRAW:A_NMRAW + 1], S[:], AX.X, ALU.max,
                          negate=True), waits=[("t", tsc)], drain=True)
        nmslot = A_NMNEW if par == 0 else 16
        mdslot = A_MDIFF if par == 0 else 17
        E("V", lambda e:
          e.tensor_scalar(sml[:, A_T15:A_T15 + 1], sml[:, A_NMRAW:A_NMRAW + 1],
                          1.0 / 32.0, None, ALU.mult), drain=True)
        E("V", lambda e, ss=ss, mdslot=mdslot:
          e.tensor_sub(sml[:, mdslot:mdslot + 1], nm_sb[:, ss:ss + 1],
                       sml[:, A_T15:A_T15 + 1]), drain=True)
        E("V", lambda e, mdslot=mdslot:
          e.tensor_relu(sml[:, A_NMRAW:A_NMRAW + 1], sml[:, mdslot:mdslot + 1]), drain=True)
        E("V", lambda e, mdslot=mdslot:
          e.tensor_scalar(sml[:, mdslot:mdslot + 1], sml[:, A_NMRAW:A_NMRAW + 1],
                          -1.0, None, ALU.mult), drain=True)
        E("V", lambda e, ss=ss, nmslot=nmslot, mdslot=mdslot:
          e.tensor_scalar(sml[:, nmslot:nmslot + 1], sml[:, mdslot:mdslot + 1],
                          nm_sb[:, ss:ss + 1], None, ALU.add), drain=True)
        if _DEBUG and n == 0:
            E("V", lambda e: e.tensor_copy(dbgbuf[:, 0:1], sml[:, A_NMRAW:A_NMRAW + 1]))
            E("V", lambda e: e.tensor_copy(dbgbuf[:, 1:2], sml[:, A_NMNEW:A_NMNEW + 1]))
            E("V", lambda e, ss=ss: e.tensor_copy(dbgbuf[:, 2:3], nm_sb[:, ss:ss + 1]))
            E("V", lambda e: e.tensor_copy(dbgbuf[:, 3:4], sml[:, A_MDIFF:A_MDIFF + 1]))
        nmv = E("V", lambda e, ss=ss, nmslot=nmslot:
                e.tensor_copy(nm_sb[:, ss:ss + 1], sml[:, nmslot:nmslot + 1]),
                inc="v")
        nm_of[n] = nmv
        if _PSTEP < 3:
            return
        aslot = A_ALPHA0 + par
        lslot = A_LP0 + par
        E("S", lambda e, aslot=aslot, mdslot=mdslot:
          e.activation(sml[:, aslot:aslot + 1], sml[:, mdslot:mdslot + 1], AF.Exp),
          waits=[("v", nmv)])
        sexp = E("S", lambda e, lslot=lslot, par=par, S=S, nmslot=nmslot:
                 e.activation(p_sb[:, par, :], S[:], AF.Exp,
                              bias=sml[:, nmslot:nmslot + 1],
                              scale=1.0 / 32.0, accum_out=sml[:, lslot:lslot + 1]),
                 inc="s")
        exp_of[n] = sexp
        if _DEBUG and n == 0:
            dmp = E("Y", lambda e: e.dma_start(out=dbg_sml, in_=dbgbuf[:, 0:16]),
                    waits=[("v", nmv), ("s", sexp)], inc="dy", inc_n=16)
            dmp = E("Y", lambda e: e.dma_start(out=dbg_p, in_=p_sb[:, 0, :]),
                    inc="dy", inc_n=16)
            vdbg = E("V", lambda e: e.tensor_copy(dbgbuf[:], ps_a[:]),
                     waits=[("dy", dmp)], inc="v")
            dmp = E("Y", lambda e: e.dma_start(out=dbg_s, in_=dbgbuf[:]),
                    waits=[("v", vdbg)], inc="dy", inc_n=16)
            p.wait("S", [("dy", dmp)])
            p.wait("V", [("dy", dmp)])
        E("V", lambda e, ss=ss, aslot=aslot:
          e.tensor_scalar(sml[:, A_T15:A_T15 + 1], l_sb[:, ss:ss + 1],
                          sml[:, aslot:aslot + 1], None, ALU.mult),
          waits=[("s", sexp)], drain=True)
        E("V", lambda e, ss=ss, lslot=lslot:
          e.tensor_scalar(l_sb[:, ss:ss + 1], sml[:, A_T15:A_T15 + 1],
                          sml[:, lslot:lslot + 1], None, ALU.add), inc="v")
        pend.append((n, kt, ss, par, b))

    def flash_tail():
        if not pend:
            return
        n, kt, ss, par, b = pend.pop(0)
        if _PSTEP < 4:
            return
        aslot = A_ALPHA0 + par
        ev, es = prev_evac[0]
        ttr = None
        for c in range(8):
            w = []
            if c == 0:
                w = [("s", exp_of[n]), ("v", ev), ("s", es)]
            ttr = E("T", lambda e, c=c, par=par:
                    e.transpose(ps_pt[:, c, :], p_sb[:, par, c * 128:(c + 1) * 128],
                                identb_sb[:]),
                    waits=w, inc="t" if c == 7 else None)
        E("V", lambda e: e.tensor_copy(pt_sb[:], ps_pt[:]),
          waits=[("t", ttr)], inc="v")
        prev_evac[0] = (C("v"), C("s"))
        evac_v, evac_s = prev_evac[0]
        if _PSTEP < 5:
            return
        tpv = None
        for kc in range(8):
            waits = []
            if kc == 0:
                waits = [("v", evac_v), ("s", evac_s), ("v", prev_oupd[0])]
            for h in range(2):
                last = (kc == 7 and h == 1)
                tpv = E("T", lambda e, kc=kc, h=h, b=b, start=(kc == 0), stop=(kc == 7):
                        e.matmul(ps_b[:, h * 512:(h + 1) * 512], pt_sb[:, kc, :],
                                 vbuf[:, b, kc, h * 512:(h + 1) * 512],
                                 start=start, stop=stop),
                        waits=waits if h == 0 else [],
                        inc="t" if last else None)
                waits = []
        kt_users[b] = tpv
        if _PSTEP < 6:
            return
        for h in range(2):
            E("V", lambda e, ss=ss, h=h, aslot=aslot:
              e.scalar_tensor_tensor(obuf[:, ss, h * 512:(h + 1) * 512],
                                     obuf[:, ss, h * 512:(h + 1) * 512],
                                     sml[:, aslot:aslot + 1],
                                     ps_b[:, h * 512:(h + 1) * 512],
                                     ALU.mult, ALU.add),
              waits=[("t", tpv)] if h == 0 else [], inc="v")
        prev_oupd[0] = C("v")

    load_kt(0)
    if _DEBUG:
        E("Y", lambda e: e.dma_start(out=dbg_kt, in_=kbuf[:, 0, :, :]),
          waits=[("dy", kt_load[0])], inc="dy", inc_n=16)
        E("Y", lambda e: e.dma_start(out=dbg_vt, in_=vbuf[:, 0, :, :]),
          inc="dy", inc_n=16)
    for kt in range(min(KT, _KT)):
        if kt + 1 < min(KT, _KT):
            while pend:
                flash_tail()
            load_kt(kt + 1)
        for ss in list(range(kt, NSUB)) + [NQ - 1]:
            if pair_idx[0] < _NPAIR:
                flash_head(kt, ss)
                if len(pend) > 1:
                    flash_tail()
    while pend:
        flash_tail()

    p.barrier()
    if _DEBUG:
        for ss in range(NQ):
            E("Y", lambda e, ss=ss: e.dma_start(out=dbg_o[ss], in_=obuf[:, ss, :]),
              inc="dy", inc_n=16)
        E("Y", lambda e: e.dma_start(out=dbg_l, in_=l_sb[:]), inc="dy", inc_n=16)
        E("Y", lambda e: e.dma_start(out=dbg_nm, in_=nm_sb[:]), inc="dy", inc_n=16)
        E("Y", lambda e: e.dma_start(out=dbg_q, in_=q_fm[:]), inc="dy", inc_n=16)
        E("Y", lambda e: e.dma_start(out=dbg_sw, in_=swb[:]), inc="dy", inc_n=16)
        p.wait("Y", [("dy", C("dy"))])
        p.barrier()
    if _PHASES < 3:
        p.wait("Y", [("dy", C("dy"))])
        p.barrier()
        return _finish(nc, p)

    ar.reset(p2base)
    y_tm = ar.alloc("y_tm", [128, GRP, HID], F32)
    xq_fm = ar.alloc("xq_fm", [128, 8, GRP * 128], BF16)
    xqh_fm = ar.alloc("xqh_fm", [128, 8, 128], BF16)
    h_fm = ar.alloc("h_fm", [128, 2, GRP, 130], F32)
    ysil = ar.alloc("ysil", [128, 2, GRP, 128], F32)
    w1q_sb = ar.alloc("w1q_sb", [128, 2, 8, 128], BF16)
    w2q_sb = ar.alloc("w2q_sb", [128, 2, 1024], BF16)
    xtile = ar.alloc("xtile", [128, 2, 1024], F32)
    scr1 = ar.alloc("scr1", [128, 1024], F32)
    scr2 = ar.alloc("scr2", [128, 1024], F32)
    xq_tm = ar.alloc("xq_tm", [128, 1024], BF16)
    yq_tm = ar.alloc("yq_tm", [128, HID], BF16)
    yq_fm0 = ar.alloc("yq_fm0", [128, 32, 128], BF16)
    yq_fm1 = ar.alloc("yq_fm1", [128, 32, 128], BF16)
    outbuf = ar.alloc("outbuf", [128, 1024], F32)

    w1q_r = w1q_scr.rearrange("c p h -> p c h")

    xt_users = {}

    def load_xtile(ss, b):
        w = [("v", xt_users[b])] if b in xt_users else []
        return E("G", lambda e, ss=ss, b=b:
                 e.dma_start(out=xtile[:, b, :],
                             in_=d["x_tm"][ss * 128:(ss + 1) * 128, :]),
                 waits=w, inc="dg", inc_n=16)

    def ln_quant1(ss, b, xld, dest_fn):
        E("V", lambda e, ss=ss:
          e.reciprocal(sml[:, A_LINV:A_LINV + 1], l_sb[:, ss:ss + 1]),
          waits=[("dg", xld)], drain=True)
        E("V", lambda e, ss=ss, b=b:
          e.scalar_tensor_tensor(obuf[:, ss, :], obuf[:, ss, :],
                                 sml[:, A_LINV:A_LINV + 1], xtile[:, b, :],
                                 ALU.mult, ALU.add))
        E("V", lambda e, ss=ss:
          e.tensor_reduce(sml[:, A_T15:A_T15 + 1], obuf[:, ss, :], AX.X, ALU.add), drain=True)
        E("V", lambda e:
          e.tensor_scalar(sml[:, A_NEGMEAN:A_NEGMEAN + 1],
                          sml[:, A_T15:A_T15 + 1], -1.0 / D, None, ALU.mult), drain=True)
        E("V", lambda e, ss=ss:
          e.tensor_scalar(obuf[:, ss, :], obuf[:, ss, :],
                          sml[:, A_NEGMEAN:A_NEGMEAN + 1], None, ALU.add))
        E("V", lambda e, ss=ss:
          e.tensor_tensor(scr2[:], obuf[:, ss, :], obuf[:, ss, :], ALU.mult))
        E("V", lambda e:
          e.tensor_reduce(sml[:, A_SSQ:A_SSQ + 1], scr2[:], AX.X, ALU.add), drain=True)
        vv = E("V", lambda e:
               e.tensor_scalar(sml[:, A_VAR:A_VAR + 1], sml[:, A_SSQ:A_SSQ + 1],
                               1.0 / D, EPS, ALU.mult, ALU.add), inc="v", drain=True)
        sq = E("S", lambda e:
               e.activation(sml[:, A_RSTD:A_RSTD + 1], sml[:, A_VAR:A_VAR + 1],
                            AF.Sqrt), waits=[("v", vv)], inc="s")
        E("V", lambda e:
          e.reciprocal(sml[:, A_T15:A_T15 + 1], sml[:, A_RSTD:A_RSTD + 1]),
          waits=[("s", sq)], drain=True)
        E("V", lambda e, ss=ss:
          e.tensor_scalar(obuf[:, ss, :], obuf[:, ss, :],
                          sml[:, A_T15:A_T15 + 1], None, ALU.mult))
        E("V", lambda e, ss=ss:
          e.tensor_reduce(sml[:, A_AMAX:A_AMAX + 1], obuf[:, ss, :], AX.X, ALU.max,
                          apply_absolute_value=True), drain=True)
        E("V", lambda e:
          e.tensor_scalar(sml[:, A_T15:A_T15 + 1], sml[:, A_AMAX:A_AMAX + 1],
                          1e-5, None, ALU.max), drain=True)
        E("V", lambda e: e.reciprocal(sml[:, A_AMAX:A_AMAX + 1], sml[:, A_T15:A_T15 + 1]), drain=True)
        E("V", lambda e:
          e.tensor_scalar(sml[:, A_SX:A_SX + 1], sml[:, A_AMAX:A_AMAX + 1], 127.0,
                          None, ALU.mult), drain=True)
        E("V", lambda e: e.reciprocal(sml[:, A_SXINV:A_SXINV + 1], sml[:, A_SX:A_SX + 1]), drain=True)
        sxinv_slot = A_SXINV
        if ss == NQ - 1:
            E("V", lambda e:
              e.tensor_scalar(sml[:, A_MDIFF:A_MDIFF + 1], sml[:, A_SXINV:A_SXINV + 1],
                              hvalid_sb[:], None, ALU.mult), drain=True)
            sxinv_slot = A_MDIFF
        E("V", lambda e, ss=ss:
          e.tensor_scalar(scr1[:], obuf[:, ss, :], sml[:, A_SX:A_SX + 1], MAGIC,
                          ALU.mult, ALU.add))
        vq = E("V", lambda e, sxinv_slot=sxinv_slot:
               e.tensor_scalar(xq_tm[:], scr1[:], -MAGIC,
                               sml[:, sxinv_slot:sxinv_slot + 1],
                               ALU.add, ALU.mult), inc="v")
        tt = None
        for c in range(8):
            w = [("v", vq), ("s", C("s"))] if c == 0 else []
            tt = E("T", lambda e, c=c:
                   e.transpose(ps_pt[:, c, :], xq_tm[:, c * 128:(c + 1) * 128],
                               identb_sb[:]),
                   waits=w, inc="t" if c == 7 else None)
        E("V", lambda e, dest_fn=dest_fn: e.tensor_copy(dest_fn(), ps_pt[:]),
          waits=[("t", tt)], inc="v")
        return C("v"), C("s")

    xldh = load_xtile(16, 0)
    hv, hs = ln_quant1(16, 0, xldh, lambda: xqh_fm[:])
    xt_users[0] = hv

    w1ld_user = {}
    hevac_user = {}
    ysil_user = {}
    yevac_user = {}
    w2ld_user = {}
    out_dma = [0]

    for g in range(NGRP):
        s0, s1 = 2 * g, 2 * g + 1
        for sl, ss in enumerate((s0, s1)):
            b = ss % 2
            xld = load_xtile(ss, b)
            ln_quant1(ss, b, xld,
                      lambda sl=sl: xq_fm[:, :, sl * 128:(sl + 1) * 128])
            xt_users[b] = C("v")
        xq_v, xq_s = C("v"), C("s")
        if _DEBUG:
            for sl, ss in enumerate((s0, s1)):
                E("Y", lambda e, ss=ss: e.dma_start(out=dbg_xln[ss], in_=obuf[:, ss, :]),
                  waits=[("v", xq_v)], inc="dy", inc_n=16)
            if g == 0:
                E("Y", lambda e: e.dma_start(out=dbg_xln[16], in_=obuf[:, 16, :]),
                  inc="dy", inc_n=16)

        for hc in range(32):
            wb = hc % 2
            w = [("dy", wq_done)]
            if wb in w1ld_user:
                w.append(("t", w1ld_user[wb]))
            ldw1 = E("Y", lambda e, hc=hc, wb=wb:
                     e.dma_start(out=w1q_sb[:, wb, :, :],
                                 in_=w1q_r[:, :, hc * 128:(hc + 1) * 128]),
                     waits=w, inc="dy", inc_n=16)

            hb = hc % 2
            po = ps_b[:, hb * 512:hb * 512 + 256]
            ph = ps_b[:, hb * 512 + 256:hb * 512 + 384]
            for di in range(8):
                waits = []
                if di == 0:
                    waits = [("dy", ldw1), ("s", xq_s), ("v", xq_v),
                             ("v", hevac_user.get(hb, 0))]
                E("T", lambda e, di=di, wb=wb, po=po,
                  start=(di == 0), stop=(di == 7):
                  e.matmul(po, w1q_sb[:, wb, di, :], xq_fm[:, di, :],
                           start=start, stop=stop),
                  waits=waits)
            tw1 = None
            for di in range(8):
                waits = []
                if g == 0 and hc == 0 and di == 0:
                    waits = [("v", hv), ("s", hs)]
                tw1 = E("T", lambda e, di=di, wb=wb, ph=ph,
                        start=(di == 0), stop=(di == 7):
                        e.matmul(ph, w1q_sb[:, wb, di, :], xqh_fm[:, di, :],
                                 start=start, stop=stop),
                        waits=waits, inc="t" if di == 7 else None)
            w1ld_user[wb] = tw1

            E("V", lambda e, hb=hb, po=po:
              e.tensor_scalar(h_fm[:, hb, :, 1:129],
                              po.rearrange("p (g n) -> p g n", g=GRP),
                              swb[:, 1:2], None, ALU.mult),
              waits=[("t", tw1)])
            for si, (sl, ss) in enumerate(((0, s0), (1, s1))):
                E("V", lambda e, hb=hb, sl=sl, ss=ss, ph=ph:
                  e.tensor_scalar(h_fm[:, hb, sl, 0:1], ph[:, 2 * ss:2 * ss + 1],
                                  swb[:, 1:2], None, ALU.mult))
                hevac_user[hb] = E(
                    "V", lambda e, hb=hb, sl=sl, ss=ss, ph=ph:
                    e.tensor_scalar(h_fm[:, hb, sl, 129:130],
                                    ph[:, 2 * ss + 1:2 * ss + 2],
                                    swb[:, 1:2], None, ALU.mult),
                    inc="v" if si == 1 else None)

            cview = scr2[:, 0:GRP * 128].rearrange("p (g n) -> p g n", g=GRP)
            E("V", lambda e, hb=hb, hc=hc, cview=cview:
              e.tensor_scalar(cview, h_fm[:, hb, :, 0:128], cw_sb[:, hc:hc + 1],
                              None, ALU.mult))
            E("V", lambda e, hb=hb, hc=hc, cview=cview:
              e.scalar_tensor_tensor(cview, h_fm[:, hb, :, 1:129],
                                     cw_sb[:, 32 + hc:33 + hc], cview,
                                     ALU.mult, ALU.add))
            vc3 = E("V", lambda e, hb=hb, hc=hc, cview=cview:
                    e.scalar_tensor_tensor(cview, h_fm[:, hb, :, 2:130],
                                           cw_sb[:, 64 + hc:65 + hc], cview,
                                           ALU.mult, ALU.add), inc="v")
            w = [("v", vc3)]
            if hb in ysil_user:
                w.append(("t", ysil_user[hb]))
            ssil = E("S", lambda e, hb=hb, hc=hc, cview=cview:
                     e.activation(ysil[:, hb, :, :], cview, AF.Silu,
                                  bias=cw_sb[:, 96 + hc:97 + hc]),
                     waits=w, inc="s")
            tty = None
            for sl in range(GRP):
                w = []
                if sl == 0:
                    w = [("s", ssil)]
                    if (hc % 2) in yevac_user:
                        yv, ys = yevac_user[hc % 2]
                        w += [("v", yv), ("s", ys)]
                tty = E("T", lambda e, hb=hb, sl=sl:
                        e.transpose(ps_t2[:, sl, :], ysil[:, hb, sl, :], identf_sb[:]),
                        waits=w, inc="t" if sl == GRP - 1 else None)
            ysil_user[hb] = tty
            E("V", lambda e, hc=hc:
              e.tensor_copy(y_tm[:, :, hc * 128:(hc + 1) * 128], ps_t2[:, 0:GRP, :]),
              waits=[("t", tty)], inc="v")
            yevac_user[hc % 2] = (C("v"), C("s"))
        y_v, y_s = C("v"), C("s")

        for sl, yqf in ((0, yq_fm0), (1, yq_fm1)):
            E("V", lambda e, sl=sl:
              e.tensor_reduce(sml[:, A_AMAX:A_AMAX + 1], y_tm[:, sl, :], AX.X,
                              ALU.max, apply_absolute_value=True),
              waits=[("v", y_v), ("s", y_s)] if sl == 0 else [], drain=True)
            E("V", lambda e:
              e.tensor_scalar(sml[:, A_T15:A_T15 + 1], sml[:, A_AMAX:A_AMAX + 1],
                              1e-5, None, ALU.max), drain=True)
            E("V", lambda e:
              e.reciprocal(sml[:, A_AMAX:A_AMAX + 1], sml[:, A_T15:A_T15 + 1]), drain=True)
            E("V", lambda e:
              e.tensor_scalar(sml[:, A_SX:A_SX + 1], sml[:, A_AMAX:A_AMAX + 1], 127.0,
                              None, ALU.mult), drain=True)
            E("V", lambda e:
              e.reciprocal(sml[:, A_SXINV:A_SXINV + 1], sml[:, A_SX:A_SX + 1]), drain=True)
            vyq = None
            for hh in range(4):
                E("V", lambda e, sl=sl, hh=hh:
                  e.tensor_scalar(scr1[:], y_tm[:, sl, hh * 1024:(hh + 1) * 1024],
                                  sml[:, A_SX:A_SX + 1], MAGIC, ALU.mult, ALU.add))
                vyq = E("V", lambda e, hh=hh:
                        e.tensor_scalar(yq_tm[:, hh * 1024:(hh + 1) * 1024], scr1[:],
                                        -MAGIC, sml[:, A_SXINV:A_SXINV + 1],
                                        ALU.add, ALU.mult),
                        inc="v" if hh == 3 else None)
            for blk in range(4):
                ttq = None
                for c in range(8):
                    cc = blk * 8 + c
                    w = []
                    if c == 0:
                        w = [("v", C("v") if blk else vyq), ("s", C("s"))]
                    ttq = E("T", lambda e, cc=cc, c=c:
                            e.transpose(ps_pt[:, c, :],
                                        yq_tm[:, cc * 128:(cc + 1) * 128],
                                        identb_sb[:]),
                            waits=w, inc="t" if c == 7 else None)
                E("V", lambda e, blk=blk, yqf=yqf:
                  e.tensor_copy(yqf[:, blk * 8:(blk + 1) * 8, :], ps_pt[:]),
                  waits=[("t", ttq)], inc="v")
        yq_v, yq_s = C("v"), C("s")

        for hc in range(32):
            wb = hc % 2
            w = [("dy", wq_done)]
            if wb in w2ld_user:
                w.append(("t", w2ld_user[wb]))
            ldw2 = E("S", lambda e, hc=hc, wb=wb:
                     e.dma_start(out=w2q_sb[:, wb, :], in_=w2q_scr[hc]),
                     waits=w, inc="ds", inc_n=16)
            waits = [("ds", ldw2)]
            if hc == 0:
                waits += [("v", yq_v), ("s", yq_s)]
            lastmm = None
            for sl, yqf, psd in ((0, yq_fm0, ps_a), (1, yq_fm1, ps_b)):
                for h in range(2):
                    last = (sl == 1 and h == 1)
                    lastmm = E("T", lambda e, hc=hc, h=h, wb=wb, yqf=yqf, psd=psd,
                               start=(hc == 0), stop=(hc == 31):
                               e.matmul(psd[:, h * 512:(h + 1) * 512], yqf[:, hc, :],
                                        w2q_sb[:, wb, h * 512:(h + 1) * 512],
                                        start=start, stop=stop),
                               waits=waits, inc="t" if last else None)
                    waits = []
            w2ld_user[wb] = lastmm
        tw2 = C("t")

        for sl, ss, psd in ((0, s0, ps_a), (1, s1, ps_b)):
            for h in range(2):
                E("V", lambda e, ss=ss, psd=psd, h=h:
                  e.scalar_tensor_tensor(scr1[:, h * 512:(h + 1) * 512],
                                         psd[:, h * 512:(h + 1) * 512], swb[:, 3:4],
                                         obuf[:, ss, h * 512:(h + 1) * 512],
                                         ALU.mult, ALU.add),
                  waits=[("t", tw2)] if h == 0 else [])
            E("V", lambda e:
              e.tensor_reduce(sml[:, A_T15:A_T15 + 1], scr1[:], AX.X, ALU.add), drain=True)
            E("V", lambda e:
              e.tensor_scalar(sml[:, A_NEGMEAN:A_NEGMEAN + 1],
                              sml[:, A_T15:A_T15 + 1], -1.0 / D, None, ALU.mult), drain=True)
            E("V", lambda e:
              e.tensor_scalar(scr1[:], scr1[:], sml[:, A_NEGMEAN:A_NEGMEAN + 1],
                              None, ALU.add))
            E("V", lambda e:
              e.tensor_tensor(scr2[:], scr1[:], scr1[:], ALU.mult))
            E("V", lambda e:
              e.tensor_reduce(sml[:, A_SSQ:A_SSQ + 1], scr2[:], AX.X, ALU.add), drain=True)
            vv = E("V", lambda e:
                   e.tensor_scalar(sml[:, A_VAR:A_VAR + 1], sml[:, A_SSQ:A_SSQ + 1],
                                   1.0 / D, EPS, ALU.mult, ALU.add), inc="v", drain=True)
            sq = E("S", lambda e:
                   e.activation(sml[:, A_RSTD:A_RSTD + 1], sml[:, A_VAR:A_VAR + 1],
                                AF.Sqrt), waits=[("v", vv)], inc="s")
            E("V", lambda e:
              e.reciprocal(sml[:, A_T15:A_T15 + 1], sml[:, A_RSTD:A_RSTD + 1]),
              waits=[("s", sq)], drain=True)
            vout = E("V", lambda e:
                     e.tensor_scalar(outbuf[:], scr1[:], sml[:, A_T15:A_T15 + 1],
                                     None, ALU.mult),
                     waits=[("dy", out_dma[0])], inc="v")
            out_dma[0] = E("Y", lambda e, ss=ss:
                           e.dma_start(out=y_out[ss * 128:(ss + 1) * 128, :],
                                       in_=outbuf[:]),
                           waits=[("v", vout)], inc="dy", inc_n=16)

    p.wait("Y", [("dy", C("dy"))])
    p.barrier()
    return _finish(nc, p)


def _finish(nc, p):
    with nc.Block() as block:
        @block.tensor
        def _(t):
            for fn in p.ops["T"]:
                fn(t)

        @block.vector
        def _(v):
            for fn in p.ops["V"]:
                fn(v)

        @block.scalar
        def _(s):
            for fn in p.ops["S"]:
                fn(s)

        @block.gpsimd
        def _(g):
            for fn in p.ops["G"]:
                fn(g)

        @block.sync
        def _(y):
            for fn in p.ops["Y"]:
                fn(y)

    return nc


def _host_inputs(x, Qw, Kw, Vw, w1, conv_w, conv_b, w2):
    x2 = np.ascontiguousarray(np.asarray(x, np.float32).reshape(S, D))
    xT = np.ascontiguousarray(x2.T)
    w1t = np.ascontiguousarray(np.asarray(w1, np.float32).T)
    w2t = np.ascontiguousarray(np.asarray(w2, np.float32).T)
    cwf = np.asarray(conv_w, np.float32).reshape(HID, 3)
    cbf = np.asarray(conv_b, np.float32)
    cw = np.zeros((128, 128), np.float32)
    cw[:, 0:32] = cwf[:, 0].reshape(32, 128).T
    cw[:, 32:64] = cwf[:, 1].reshape(32, 128).T
    cw[:, 64:96] = cwf[:, 2].reshape(32, 128).T
    cw[:, 96:128] = cbf.reshape(32, 128).T
    identb = np.eye(128, dtype=ml_dtypes.bfloat16)
    identf = np.eye(128, dtype=np.float32)
    ones_col = np.ones((128, 1), np.float32)
    ones_row = np.ones((1, 128), np.float32)

    in_maps = []
    for j in range(NCORES):
        own_cols = (1024 * np.arange(NSUB)[:, None] + 128 * j
                    + np.arange(128)[None, :]).reshape(-1)
        halo_pos = np.zeros(32, np.int64)
        halo_valid = np.zeros((128, 1), np.float32)
        for i in range(NSUB):
            lp = 1024 * i + 128 * j - 1
            rp = 1024 * i + 128 * j + 128
            halo_pos[2 * i] = lp if lp >= 0 else 0
            halo_pos[2 * i + 1] = rp if rp < S else 0
            halo_valid[2 * i, 0] = 1.0 if lp >= 0 else 0.0
            halo_valid[2 * i + 1, 0] = 1.0 if rp < S else 0.0
        all_cols = np.concatenate([own_cols, halo_pos, np.zeros(96, np.int64)])
        xT_own = np.ascontiguousarray(xT[:, all_cols])
        x_tm = np.ascontiguousarray(x2[all_cols, :])
        for h in range(32):
            if halo_valid[h, 0] == 0.0:
                xT_own[:, 2048 + h] = 0.0
                x_tm[2048 + h, :] = 0.0
        xT_own[:, 2048 + 32:] = 0.0
        x_tm[2048 + 32:, :] = 0.0

        r = np.arange(128)[:, None]
        c = np.arange(1024)[None, :]
        dmask = np.where(c <= 128 * j + r, 0.0, NEG).astype(np.float32)

        hmask = np.full((KT, 128, 1024), NEG, np.float32)
        for h in range(32):
            if halo_valid[h, 0] == 1.0:
                pos = int(halo_pos[h])
                for kt in range(pos // 1024 + 1):
                    n = min(1024, pos - 1024 * kt + 1)
                    hmask[kt, h, 0:n] = 0.0
            else:
                hmask[0, h, 0] = 0.0
        hmask[0, 32:, 0] = 0.0

        in_maps.append({
            "xT_own": xT_own, "x_tm": x_tm,
            "qw": np.asarray(Qw, np.float32), "kw": np.asarray(Kw, np.float32),
            "vw": np.asarray(Vw, np.float32),
            "w1t": w1t, "w2t": w2t,
            "cw": cw, "dmask": dmask, "hmask": hmask, "hvalid": halo_valid,
            "identb": identb, "identf": identf,
            "ones_col": ones_col, "ones_row": ones_row,
        })
    return in_maps


_NC_CACHE = [None]


def kernel(L0_attn_x, L0_attn_Qw, L0_attn_Kw, L0_attn_Vw,
           L0_ffn_w1, L0_ffn_conv_w, L0_ffn_conv_b, L0_ffn_w2):
    in_maps = _host_inputs(L0_attn_x, L0_attn_Qw, L0_attn_Kw, L0_attn_Vw,
                           L0_ffn_w1, L0_ffn_conv_w, L0_ffn_conv_b, L0_ffn_w2)
    if _NC_CACHE[0] is None:
        _NC_CACHE[0] = build_program()
    nc = _NC_CACHE[0]
    res = run_bass_kernel_spmd(nc, in_maps, core_ids=list(range(NCORES)))
    out = np.empty((1, S, D), np.float32)
    ov = out.reshape(NSUB, NCORES, 128, D)
    for j in range(NCORES):
        ov[:, j] = np.asarray(res.results[j]["y"]).reshape(NSUB, 128, D)
    return out


# version 07
# speedup vs baseline: 1.0692x; 1.0438x over previous
import os
import numpy as np
import ml_dtypes
import concourse.bass as bass
import concourse.mybir as mybir
from concourse.bass_utils import run_bass_kernel_spmd

F32 = mybir.dt.float32
BF16 = mybir.dt.bfloat16
AF = mybir.ActivationFunctionType
ALU = mybir.AluOpType
AX = mybir.AxisListType

NCORES = 8
S = 16384
D = 1024
HID = 4096
NSUB = 16
NQ = NSUB + 1
KT = 16
MAGIC = 12582912.0
NM_INIT = 100.0
NEG = -1e9
EPS = 1e-5
GRP = 2
NGRP = NSUB // GRP

ENGINES = ("T", "V", "S", "G", "Y")


class Plan:

    def __init__(self, nc):
        self.nc = nc
        self.ops = {e: [] for e in ENGINES}
        self.sems = {}
        self.cnt = {}
        self.bar_round = 0

    def sem(self, name):
        if name not in self.sems:
            self.sems[name] = self.nc.alloc_semaphore(name)
            self.cnt[name] = 0
        return self.sems[name]

    def emit(self, e, fn, waits=(), inc=None, inc_n=1, drain=False):
        sems = self.sems
        wl = tuple((sems[s], v) for (s, v) in waits if v > 0)

        def run(eng, fn=fn, wl=wl, inc=inc, inc_n=inc_n, drain=drain):
            for s, v in wl:
                eng.wait_ge(s, v)
            r = fn(eng)
            if drain:
                eng.drain()
            if inc is not None:
                r.then_inc(sems[inc], inc_n)

        self.ops[e].append(run)
        if inc is not None:
            self.cnt[inc] += inc_n
            return self.cnt[inc]
        return None

    def wait(self, e, waits):
        sems = self.sems
        wl = tuple((sems[s], v) for (s, v) in waits if v > 0)

        def run(eng, wl=wl):
            for s, v in wl:
                eng.wait_ge(s, v)

        self.ops[e].append(run)

    def c(self, name):
        return self.cnt[name]

    def barrier(self):
        bar = self.sem("bar")
        own_dma = {"Y": "dy", "G": "dg", "S": "ds"}
        self.bar_round += 1
        target = len(ENGINES) * self.bar_round
        for e in ENGINES:
            waits = []
            if e in own_dma:
                waits.append((own_dma[e], self.cnt[own_dma[e]]))
            def run_b(eng, bar=self.sems["bar"], waits=tuple(
                    (self.sems[s], v) for (s, v) in waits if v > 0)):
                for s_, v_ in waits:
                    eng.wait_ge(s_, v_)
                eng.drain()
                eng.sem_inc(bar, 1)
            self.ops[e].append(run_b)
            self.cnt["bar"] += 1
        for e in ENGINES:
            self.wait(e, [("bar", target)])


class Arena:
    def __init__(self, nc, base, limit):
        self.nc = nc
        self.base = base
        self.limit = limit
        self.cur = base

    def alloc(self, name, shape, dtype, align=64):
        per_part = int(np.prod(shape[1:])) * mybir.dt.size(dtype)
        off = (self.cur + align - 1) // align * align
        assert off + per_part <= self.limit, (
            f"arena overflow: {name} at {off}+{per_part} > {self.limit}")
        self.cur = off + per_part
        return self.nc.alloc_sbuf_tensor_at(name, shape, dtype, offset=off)

    def mark(self):
        return self.cur

    def reset(self, to):
        self.cur = to



def _copy(e, dst, src):
    if hasattr(e, "tensor_copy"):
        e_fn = e.tensor_copy
    else:
        e_fn = e.copy
    return e_fn(dst, src)

_PHASES = int(os.environ.get("MK_PHASES", "4"))
_SKIP_AG = os.environ.get("MK_SKIP_AG", "0") == "1"
_P1B = int(os.environ.get("MK_P1B", "5"))
_KT = int(os.environ.get("MK_KT", "16"))
_NPAIR = int(os.environ.get("MK_NPAIR", "9999"))
_PSTEP = int(os.environ.get("MK_PSTEP", "6"))
_SEV = os.environ.get("MK_SEV", "1") == "1"
_DEBUG = os.environ.get("MK_DEBUG", "0") == "1"


def build_program():
    nc = bass.Bass("TRN2", target_bir_lowering=False)

    d = {}
    d["xT_own"] = nc.dram_tensor("xT_own", [D, NQ * 128], F32, kind="ExternalInput").ap()
    d["x_tm"] = nc.dram_tensor("x_tm", [NQ * 128, D], F32, kind="ExternalInput").ap()
    d["qw"] = nc.dram_tensor("qw", [D, D], F32, kind="ExternalInput").ap()
    d["kw"] = nc.dram_tensor("kw", [D, D], F32, kind="ExternalInput").ap()
    d["vw"] = nc.dram_tensor("vw", [D, D], F32, kind="ExternalInput").ap()
    d["w1t"] = nc.dram_tensor("w1t", [D, HID], F32, kind="ExternalInput").ap()
    d["w2t"] = nc.dram_tensor("w2t", [HID, D], F32, kind="ExternalInput").ap()
    d["cw"] = nc.dram_tensor("cw", [128, 128], F32, kind="ExternalInput").ap()
    d["dmask"] = nc.dram_tensor("dmask", [128, 1024], F32, kind="ExternalInput").ap()
    d["hmask"] = nc.dram_tensor("hmask", [KT, 128, 1024], F32, kind="ExternalInput").ap()
    d["hvalid"] = nc.dram_tensor("hvalid", [128, 1], F32, kind="ExternalInput").ap()
    d["identb"] = nc.dram_tensor("identb", [128, 128], BF16, kind="ExternalInput").ap()
    d["identf"] = nc.dram_tensor("identf", [128, 128], F32, kind="ExternalInput").ap()
    d["ones_col"] = nc.dram_tensor("ones_col", [128, 1], F32, kind="ExternalInput").ap()
    d["ones_row"] = nc.dram_tensor("ones_row", [1, 128], F32, kind="ExternalInput").ap()
    y_out = nc.dram_tensor("y", [NSUB * 128, D], F32, kind="ExternalOutput").ap()
    if _DEBUG:
        dbg_o = nc.dram_tensor("dbg_o", [NQ, 128, D], F32, kind="ExternalOutput").ap()
        dbg_l = nc.dram_tensor("dbg_l", [128, NQ], F32, kind="ExternalOutput").ap()
        dbg_nm = nc.dram_tensor("dbg_nm", [128, NQ], F32, kind="ExternalOutput").ap()
        dbg_q = nc.dram_tensor("dbg_q", [128, 8, NQ * 128], BF16, kind="ExternalOutput").ap()
        dbg_sw = nc.dram_tensor("dbg_sw", [128, 4], F32, kind="ExternalOutput").ap()
        dbg_xln = nc.dram_tensor("dbg_xln", [NQ, 128, D], F32, kind="ExternalOutput").ap()
        dbg_kt = nc.dram_tensor("dbg_kt", [128, 8, 1024], BF16, kind="ExternalOutput").ap()
        dbg_vt = nc.dram_tensor("dbg_vt", [128, 8, 1024], BF16, kind="ExternalOutput").ap()
        dbg_p = nc.dram_tensor("dbg_p", [128, 1024], BF16, kind="ExternalOutput").ap()
        dbg_s = nc.dram_tensor("dbg_s", [128, 1024], F32, kind="ExternalOutput").ap()
        dbg_sml = nc.dram_tensor("dbg_sml", [128, 16], F32, kind="ExternalOutput").ap()

    k_own = nc.dram_tensor("k_own", [D, 2048], BF16, kind="Internal").ap()
    v_own = nc.dram_tensor("v_own", [2048, D], BF16, kind="Internal").ap()
    k_all = nc.dram_tensor("k_all", [NCORES * D, 2048], BF16, kind="Internal").ap()
    v_all = nc.dram_tensor("v_all", [NCORES * 2048, D], BF16, kind="Internal").ap()
    w1q_scr = nc.dram_tensor("w1q_scr", [8, 128, HID], BF16, kind="Internal").ap()
    w2q_scr = nc.dram_tensor("w2q_scr", [32, 128, D], BF16, kind="Internal").ap()

    q_fm = nc.alloc_sbuf_tensor("q_fm", [128, 8, NQ * 128], BF16)
    dmask_sb = nc.alloc_sbuf_tensor("dmask_sb", [128, 1024], F32)
    cw_sb = nc.alloc_sbuf_tensor("cw_sb", [128, 128], F32)
    identb_sb = nc.alloc_sbuf_tensor("identb_sb", [128, 128], BF16)
    identf_sb = nc.alloc_sbuf_tensor("identf_sb", [128, 128], F32)
    hvalid_sb = nc.alloc_sbuf_tensor("hvalid_sb", [128, 1], F32)
    ones_col_sb = nc.alloc_sbuf_tensor("ones_col_sb", [128, 1], F32)
    ones_row_sb = nc.alloc_sbuf_tensor("ones_row_sb", [1, 128], F32)
    swb = nc.alloc_sbuf_tensor("swb", [128, 4], F32)
    nm_sb = nc.alloc_sbuf_tensor("nm_sb", [128, NQ], F32)
    l_sb = nc.alloc_sbuf_tensor("l_sb", [128, NQ], F32)
    sml = nc.alloc_sbuf_tensor("sml", [128, 20], F32)
    (A_ALPHA0, A_ALPHA1, A_LP0, A_LP1, A_NMRAW, A_NMNEW, A_MDIFF,
     A_LINV, A_NEGMEAN, A_SSQ, A_VAR, A_RSTD, A_AMAX, A_SX, A_SXINV, A_T15) = range(16)
    w4 = nc.alloc_sbuf_tensor("w4", [1, 8], F32)
    w4b = nc.alloc_sbuf_tensor("w4b", [1, 4], BF16)
    ones_col_bf = nc.alloc_sbuf_tensor("ones_col_bf", [128, 1], BF16)
    ones_row_bf = nc.alloc_sbuf_tensor("ones_row_bf", [1, 128], BF16)

    base = nc.lookup_mloc(w4).addr + 64
    base = (base + 63) // 64 * 64
    ar = Arena(nc, base, 229376)

    obuf = ar.alloc("obuf", [128, NQ, 1024], F32)
    p2base = ar.mark()

    ps_a = nc.alloc_psum_tensor("ps_a", [128, 1024], F32)
    ps_b = nc.alloc_psum_tensor("ps_b", [128, 1024], F32)
    ps_c = nc.alloc_psum_tensor("ps_c", [128, 1024], F32)
    ps_pt = nc.alloc_psum_tensor("ps_pt", [128, 8, 128], BF16)
    ps_d = nc.alloc_psum_tensor("ps_d", [128, 512], F32)
    ps_t2 = ps_d.ap().rearrange("p (a n) -> p a n", n=128)
    ps_sm = ps_d

    p = Plan(nc)
    for s in ("dy", "dg", "ds", "t", "v", "s", "bar", "g"):
        p.sem(s)
    E = p.emit
    C = p.c

    for nm_, dst in (("dmask", dmask_sb), ("cw", cw_sb), ("identb", identb_sb),
                     ("identf", identf_sb), ("hvalid", hvalid_sb),
                     ("ones_col", ones_col_sb), ("ones_row", ones_row_sb)):
        E("S", lambda e, nm_=nm_, dst=dst: e.dma_start(out=dst[:], in_=d[nm_]),
          inc="ds", inc_n=16)
    smalls_done = C("ds")
    E("G", lambda e: e.dma_start(out=ones_col_bf[:], in_=d["ones_col"]),
      inc="dg", inc_n=16)
    E("G", lambda e: e.dma_start(out=ones_row_bf[:], in_=d["ones_row"]),
      inc="dg", inc_n=16)
    p.wait("T", [("dg", C("dg"))])

    ar.reset(p2base)
    wtile = ar.alloc("wtile", [128, 2, 4096], F32)
    wqtile = ar.alloc("wqtile", [128, 2, 4096], BF16)
    wacc = ar.alloc("wacc", [128, 2], F32)
    wpart = ar.alloc("wpart", [128, 40], F32)
    whl = ar.alloc("whl", [128, 4], BF16)
    whl32 = ar.alloc("whl32", [128, 2], F32)
    whl32b = ar.alloc("whl32b", [128, 2], F32)

    wt_r1 = d["w1t"].rearrange("(c p) h -> c p h", p=128)
    wt_r2 = d["w2t"].rearrange("(c p) h -> c p h", p=128)

    wtile_user = {}
    wq_store = {}

    def w_pass(which, quantize):
        ntile, src, cols = ((8, wt_r1, 4096) if which == 0 else (32, wt_r2, 1024))
        scr = w1q_scr if which == 0 else w2q_scr
        swcol = 0 if which == 0 else 2
        for i in range(ntile):
            b = i % 2
            w = []
            if b in wtile_user:
                w.append(("v", wtile_user[b]))
            ld = E("Y", lambda e, i=i, b=b, src=src, cols=cols:
                   e.dma_start(out=wtile[:, b, 0:cols], in_=src[i]),
                   waits=w, inc="dy", inc_n=16)
            if not quantize:
                col = i if which == 0 else 8 + i
                E("V", lambda e, b=b, cols=cols:
                  e.tensor_reduce(sml[:, 8:8 + cols // 1024],
                                  wtile[:, b, 0:cols].rearrange("p (a q) -> p a q", q=1024),
                                  AX.X, ALU.add, apply_absolute_value=True),
                  waits=[("dy", ld)], drain=True)
                wtile_user[b] = E("V", lambda e, cols=cols, col=col:
                                  e.tensor_reduce(wpart[:, col:col + 1],
                                                  sml[:, 8:8 + cols // 1024],
                                                  AX.X, ALU.add),
                                  inc="v", drain=True)
            else:
                w2_ = [("dy", ld)]
                if b in wq_store:
                    w2_.append(("dy", wq_store[b]))
                E("V", lambda e, b=b, cols=cols, swcol=swcol:
                  e.tensor_scalar(wtile[:, b, 0:cols], wtile[:, b, 0:cols],
                                  swb[:, swcol:swcol + 1], MAGIC, ALU.mult, ALU.add),
                  waits=w2_)
                E("V", lambda e, b=b, cols=cols:
                  e.tensor_scalar(wtile[:, b, 0:cols], wtile[:, b, 0:cols],
                                  -MAGIC, 1.0, ALU.add, ALU.min))
                vq = E("V", lambda e, b=b, cols=cols:
                       e.tensor_scalar(wqtile[:, b, 0:cols], wtile[:, b, 0:cols],
                                       -1.0, None, ALU.max), inc="v")
                wtile_user[b] = vq
                wq_store[b] = E("Y", lambda e, i=i, b=b, cols=cols, scr=scr:
                                e.dma_start(out=scr[i], in_=wqtile[:, b, 0:cols]),
                                waits=[("v", vq)], inc="dy", inc_n=16)

    w_pass(0, False)
    w_pass(1, False)

    E("V", lambda e: e.tensor_reduce(wacc[:, 0:1], wpart[:, 0:8], AX.X, ALU.add), drain=True)
    E("V", lambda e: e.tensor_reduce(wacc[:, 1:2], wpart[:, 8:40], AX.X, ALU.add), drain=True)
    E("V", lambda e: e.tensor_copy(whl[:, 0:2], wacc[:]), drain=True)
    E("V", lambda e: e.tensor_copy(whl32[:], whl[:, 0:2]), drain=True)
    E("V", lambda e: e.tensor_sub(whl32b[:], wacc[:], whl32[:]), drain=True)
    vhl = E("V", lambda e: e.tensor_copy(whl[:, 2:4], whl32b[:]), inc="v", drain=True)
    E("T", lambda e: e.matmul(ps_sm.ap()[0:1, 384:386], ones_col_bf[:], whl[:, 0:2],
                              start=True, stop=False),
      waits=[("v", vhl), ("ds", smalls_done)])
    E("T", lambda e: e.matmul(ps_sm.ap()[0:1, 384:386], ones_col_bf[:], whl[:, 2:4],
                              start=False, stop=True), inc="t")
    vt2 = E("V", lambda e: e.tensor_copy(w4[0:1, 0:2], ps_sm.ap()[0:1, 384:386]),
            waits=[("t", C("t"))], inc="v", drain=True)
    E("V", lambda e: e.tensor_copy(w4b[0:1, 0:2], w4[0:1, 0:2]), drain=True)
    E("V", lambda e: e.tensor_copy(w4[0:1, 4:6], w4b[0:1, 0:2]), drain=True)
    E("V", lambda e: e.tensor_sub(w4[0:1, 6:8], w4[0:1, 0:2], w4[0:1, 4:6]), drain=True)
    vt2 = E("V", lambda e: e.tensor_copy(w4b[0:1, 2:4], w4[0:1, 6:8]), inc="v", drain=True)
    E("T", lambda e: e.matmul(ps_sm.ap()[:, 386:388], ones_row_bf[:], w4b[0:1, 0:2],
                              start=True, stop=False), waits=[("v", vt2)])
    E("T", lambda e: e.matmul(ps_sm.ap()[:, 386:388], ones_row_bf[:], w4b[0:1, 2:4],
                              start=False, stop=True), inc="t")
    E("V", lambda e: e.tensor_scalar(swb[:, 1:2], ps_sm.ap()[:, 386:387], 1.0 / (HID * D),
                                     None, ALU.mult), waits=[("t", C("t"))], drain=True)
    E("V", lambda e: e.tensor_scalar(swb[:, 3:4], ps_sm.ap()[:, 387:388], 1.0 / (HID * D),
                                     None, ALU.mult), drain=True)
    E("V", lambda e: e.reciprocal(swb[:, 0:1], swb[:, 1:2]), drain=True)
    E("V", lambda e: e.reciprocal(swb[:, 2:3], swb[:, 3:4]), inc="v", drain=True)

    w_pass(0, True)
    w_pass(1, True)
    wq_done = C("dy")

    p.barrier()
    if _PHASES < 1:
        p.wait("Y", [("dy", C("dy"))])
        p.barrier()
        return _finish(nc, p)

    ar.reset(p2base)
    xfm = ar.alloc("xfm", [128, 2, 8, 1024], BF16)
    wq_sb = ar.alloc("wq_sb", [128, 8, 1024], BF16)
    wk_sb = ar.alloc("wk_sb", [128, 8, 1024], BF16)
    wv_sb = ar.alloc("wv_sb", [128, 8, 1024], BF16)
    kvstage = ar.alloc("kvstage", [128, 4, 1024], BF16)

    for nm_, dst in (("qw", wq_sb), ("kw", wk_sb), ("vw", wv_sb)):
        E("G", lambda e, nm_=nm_, dst=dst:
          e.dma_start(out=dst[:], in_=d[nm_].rearrange("(c p) n -> p c n", p=128)),
          inc="dg", inc_n=16)
    w_qkv_loaded = C("dg")

    xT_r = d["xT_own"].rearrange("(c p) n -> p c n", p=128)
    k_own_r = k_own.rearrange("(c p) n -> c p n", p=128)
    v_own_r = v_own.rearrange("(t p) n -> t p n", p=128)

    stage_rot = [0]
    stage_user = {}

    def evac_store(psum, qcols, waits_t, store_fn):
        sl = stage_rot[0] % 4
        stage_rot[0] += 1
        w = list(waits_t)
        if sl in stage_user:
            w.append(("dy", stage_user[sl]))
        vc = E("V", lambda e, sl=sl, qcols=qcols, psum=psum:
               e.tensor_copy(kvstage[:, sl, 0:qcols], psum[:, 0:qcols]),
               waits=w, inc="v")
        dy = E("Y", lambda e, sl=sl, qcols=qcols, store_fn=store_fn:
               store_fn(e, kvstage[:, sl, 0:qcols]),
               waits=[("v", vc)], inc="dy", inc_n=16)
        stage_user[sl] = dy

    xfm_user = {}

    chunks = [(0, 1024, 0), (1024, 1024, 1), (2048, 128, 0)]
    if _P1B < 2:
        chunks = []
    for c0, cn, b in chunks:
        w = [("t", xfm_user[b])] if b in xfm_user else []
        ld = E("G", lambda e, b=b, c0=c0, cn=cn:
               e.dma_start(out=xfm[:, b, :, 0:cn], in_=xT_r[:, :, c0:c0 + cn]),
               waits=w, inc="dg", inc_n=16)
        for do in range(8):
            psum = [ps_a, ps_b][do % 2]
            hs = list(range(0, cn, 512))
            tC = None
            for h in hs:
                hn = min(512, cn - h)
                for di in range(8):
                    waits = []
                    if di == 0 and h == 0:
                        waits = [("dg", max(ld, w_qkv_loaded)), ("v", C("v"))]
                    last = (h == hs[-1] and di == 7)
                    tC = E("T", lambda e, psum=psum, h=h, hn=hn, di=di, do=do, b=b,
                           start=(di == 0), stop=(di == 7):
                           e.matmul(psum[:, h:h + hn],
                                    wq_sb[:, di, do * 128:(do + 1) * 128],
                                    xfm[:, b, di, h:h + hn], start=start, stop=stop),
                           waits=waits, inc="t" if last else None)
            xfm_user[b] = tC
            E("V", lambda e, psum=psum, do=do, c0=c0, cn=cn:
              e.tensor_copy(q_fm[:, do, c0:c0 + cn], psum[:, 0:cn]),
              waits=[("t", tC)], inc="v")

    for ci in range(2 if _P1B >= 3 else 0):
        c0 = ci * 1024
        b = ci % 2
        ld = E("G", lambda e, b=b, c0=c0:
               e.dma_start(out=xfm[:, b, :, :], in_=xT_r[:, :, c0:c0 + 1024]),
               waits=[("t", xfm_user[b])], inc="dg", inc_n=16)
        for do in range(8):
            psum = [ps_a, ps_b][do % 2]
            tC = None
            for h in range(0, 1024, 512):
                for di in range(8):
                    waits = [("dg", ld), ("v", C("v"))] if (di == 0 and h == 0) else []
                    last = (h == 512 and di == 7)
                    tC = E("T", lambda e, psum=psum, h=h, di=di, do=do, b=b,
                           start=(di == 0), stop=(di == 7):
                           e.matmul(psum[:, h:h + 512],
                                    wk_sb[:, di, do * 128:(do + 1) * 128],
                                    xfm[:, b, di, h:h + 512], start=start, stop=stop),
                           waits=waits, inc="t" if last else None)
            evac_store(psum, 1024, [("t", tC)],
                       lambda e, src, do=do, c0=c0:
                       e.dma_start(out=k_own_r[do][:, c0:c0 + 1024], in_=src))
        for tc in range(8 if _P1B >= 4 else 0):
            psum = [ps_a, ps_b][tc % 2]
            tC = None
            for h in range(0, 1024, 512):
                for di in range(8):
                    waits = [("v", C("v"))] if (di == 0 and h == 0) else []
                    last = (h == 512 and di == 7)
                    tC = E("T", lambda e, psum=psum, h=h, di=di, tc=tc, b=b,
                           start=(di == 0), stop=(di == 7):
                           e.matmul(psum[:, h:h + 512],
                                    xfm[:, b, di, tc * 128:(tc + 1) * 128],
                                    wv_sb[:, di, h:h + 512], start=start, stop=stop),
                           waits=waits, inc="t" if last else None)
            xfm_user[b] = tC
            evac_store(psum, 1024, [("t", tC)],
                       lambda e, src, ci=ci, tc=tc:
                       e.dma_start(out=v_own_r[ci * 8 + tc], in_=src))

    if _P1B < 5:
        ag_done = 0
    elif _SKIP_AG:
        E("G", lambda e: e.dma_start(out=k_all[0:D, :], in_=k_own),
          waits=[("dy", C("dy"))], inc="g", inc_n=16)
        E("G", lambda e: e.dma_start(out=v_all[0:2048, :], in_=v_own),
          inc="g", inc_n=16)
        ag_done = C("g")
    else:
        E("G", lambda e: e.collective_compute(
            "AllGather", ALU.bypass, replica_groups=[list(range(NCORES))],
            ins=[k_own], outs=[k_all]),
          waits=[("dy", C("dy"))], inc="g")
        E("G", lambda e: e.collective_compute(
            "AllGather", ALU.bypass, replica_groups=[list(range(NCORES))],
            ins=[v_own], outs=[v_all]), inc="g")
        ag_done = C("g")

    p.barrier()
    if _PHASES < 2:
        p.wait("Y", [("dy", C("dy"))])
        p.barrier()
        return _finish(nc, p)

    ar.reset(p2base)
    kbuf = ar.alloc("kbuf", [128, 2, 8, 1024], BF16)
    vbuf = ar.alloc("vbuf", [128, 2, 8, 1024], BF16)
    p_sb = ar.alloc("p_sb", [128, 2, 1024], BF16)
    pt_sb = ar.alloc("pt_sb", [128, 8, 128], BF16)
    hmbuf = ar.alloc("hmbuf", [128, 2, 1024], F32)
    if _DEBUG:
        dbgbuf = ar.alloc("dbgbuf", [128, 1024], F32)

    E("V", lambda e: e.memset(obuf[:], 0.0), inc="v")
    E("V", lambda e: e.memset(l_sb[:], 0.0), inc="v")
    E("V", lambda e: e.memset(nm_sb[:], NM_INIT), inc="v")

    k_all_r = k_all.rearrange("(j c p) n -> p c j n", p=128, c=8)
    v_all_r = v_all.rearrange("(j t p) n -> p j t n", p=128, t=16)

    kt_load = {}
    kt_users = {}
    hm_load = {}
    hm_users = {}

    def load_kt(kt):
        b = kt % 2
        w = [("g", ag_done)]
        if b in kt_users:
            w.append(("t", kt_users[b]))
        for dc in range(8):
            E("Y", lambda e, kt=kt, b=b, dc=dc:
              e.dma_start(out=kbuf[:, b, dc, :].rearrange("p (j n) -> p j n", n=128),
                          in_=k_all_r[:, dc, :, kt * 128:(kt + 1) * 128]),
              waits=(w if dc == 0 else []), inc="dy", inc_n=16)
        kt_load[kt] = E("Y", lambda e, kt=kt, b=b:
                        e.dma_start(out=vbuf[:, b, :, :], in_=v_all_r[:, :, kt, :]),
                        inc="dy", inc_n=16)
        w = [("v", hm_users[b])] if b in hm_users else []
        hm_load[kt] = E("G", lambda e, kt=kt, b=b:
                        e.dma_start(out=hmbuf[:, b, :], in_=d["hmask"][kt]),
                        waits=w, inc="dg", inc_n=16)

    pair_idx = [0]
    exp_of = {}
    nm_of = {}
    prev_evac = [(0, 0)]
    prev_oupd = [0]
    pend = []

    def flash_head(kt, ss):
        n = pair_idx[0]
        pair_idx[0] += 1
        par = n % 2
        b = kt % 2
        S = [ps_a, ps_c][par]
        tsc = None
        for di in range(8):
            if _PSTEP < 1:
                break
            waits = []
            if di == 0:
                waits = [("dy", kt_load[kt]),
                         ("s", exp_of.get(n - 2, 0)), ("v", nm_of.get(n - 2, 0))]
                if ss == NQ - 1:
                    waits.append(("dg", hm_load[kt]))
            for h in range(2):
                last = (di == 7 and h == 1)
                tsc = E("T", lambda e, di=di, h=h, ss=ss, b=b, S=S,
                        start=(di == 0), stop=(di == 7):
                        e.matmul(S[:, h * 512:(h + 1) * 512],
                                 q_fm[:, di, ss * 128:(ss + 1) * 128],
                                 kbuf[:, b, di, h * 512:(h + 1) * 512],
                                 start=start, stop=stop),
                        waits=waits if h == 0 else [],
                        inc="t" if last else None)
                waits = []
        if _PSTEP < 2:
            return
        if ss == kt or ss == NQ - 1:
            mask_fn = ((lambda: dmask_sb[:]) if ss == kt
                       else (lambda b=b: hmbuf[:, b, :]))
            madd = E("V", lambda e, mask_fn=mask_fn, S=S:
                     e.tensor_tensor(S[:], S[:], mask_fn(), ALU.add),
                     waits=[("t", tsc)], inc="v")
            if ss == NQ - 1:
                hm_users[b] = madd
        rslot = A_NMRAW if par == 0 else 16
        rmx = E("V", lambda e, S=S, rslot=rslot:
                e.tensor_reduce(sml[:, rslot:rslot + 1], S[:], AX.X, ALU.max,
                                negate=True), waits=[("t", tsc)], inc="v")
        nm_of[n] = rmx
        if _PSTEP < 3:
            return
        aslot = A_ALPHA0 + par
        lslot = A_LP0 + par
        E("S", lambda e, ss=ss, rslot=rslot:
          e.activation(sml[:, A_T15:A_T15 + 1], sml[:, rslot:rslot + 1],
                       AF.Relu, bias=nm_sb[:, ss:ss + 1], scale=-1.0 / 32.0),
          waits=[("v", rmx)])
        E("S", lambda e, aslot=aslot:
          e.activation(sml[:, aslot:aslot + 1], sml[:, A_T15:A_T15 + 1], AF.Exp,
                       scale=-1.0))
        E("S", lambda e, ss=ss:
          e.activation(sml[:, A_NMNEW:A_NMNEW + 1], sml[:, A_T15:A_T15 + 1],
                       AF.Identity, bias=nm_sb[:, ss:ss + 1], scale=-1.0))
        E("S", lambda e, ss=ss:
          e.copy(nm_sb[:, ss:ss + 1], sml[:, A_NMNEW:A_NMNEW + 1]))
        sexp = E("S", lambda e, lslot=lslot, par=par, S=S:
                 e.activation(p_sb[:, par, :], S[:], AF.Exp,
                              bias=sml[:, A_NMNEW:A_NMNEW + 1],
                              scale=1.0 / 32.0, accum_out=sml[:, lslot:lslot + 1]),
                 inc="s")
        E("S", lambda e, ss=ss, aslot=aslot:
          e.activation(sml[:, A_MDIFF:A_MDIFF + 1], l_sb[:, ss:ss + 1],
                       AF.Identity, scale=sml[:, aslot:aslot + 1]))
        sexp = E("S", lambda e, ss=ss, lslot=lslot:
                 e.activation(l_sb[:, ss:ss + 1], sml[:, A_MDIFF:A_MDIFF + 1],
                              AF.Identity, bias=sml[:, lslot:lslot + 1]),
                 inc="s")
        exp_of[n] = sexp
        if _DEBUG and n == 0:
            dmp = E("Y", lambda e: e.dma_start(out=dbg_sml, in_=dbgbuf[:, 0:16]),
                    waits=[("v", rmx), ("s", sexp)], inc="dy", inc_n=16)
            dmp = E("Y", lambda e: e.dma_start(out=dbg_p, in_=p_sb[:, 0, :]),
                    inc="dy", inc_n=16)
            vdbg = E("V", lambda e: e.tensor_copy(dbgbuf[:], ps_a[:]),
                     waits=[("dy", dmp)], inc="v")
            dmp = E("Y", lambda e: e.dma_start(out=dbg_s, in_=dbgbuf[:]),
                    waits=[("v", vdbg)], inc="dy", inc_n=16)
            p.wait("S", [("dy", dmp)])
            p.wait("V", [("dy", dmp)])

        pend.append((n, kt, ss, par, b))

    def flash_tail():
        if not pend:
            return
        n, kt, ss, par, b = pend.pop(0)
        if _PSTEP < 4:
            return
        aslot = A_ALPHA0 + par
        ev, es = prev_evac[0]
        ttr = None
        for c in range(8):
            w = []
            if c == 0:
                w = [("s", exp_of[n]), ("v", ev), ("s", es)]
            ttr = E("T", lambda e, c=c, par=par:
                    e.transpose(ps_pt[:, c, :], p_sb[:, par, c * 128:(c + 1) * 128],
                                identb_sb[:]),
                    waits=w, inc="t" if c == 7 else None)
        E("V", lambda e: e.tensor_copy(pt_sb[:], ps_pt[:]),
          waits=[("t", ttr)], inc="v")
        prev_evac[0] = (C("v"), C("s"))
        evac_v, evac_s = prev_evac[0]
        if _PSTEP < 5:
            return
        tpv = None
        for kc in range(8):
            waits = []
            if kc == 0:
                waits = [("v", evac_v), ("s", evac_s), ("v", prev_oupd[0])]
            for h in range(2):
                last = (kc == 7 and h == 1)
                tpv = E("T", lambda e, kc=kc, h=h, b=b, start=(kc == 0), stop=(kc == 7):
                        e.matmul(ps_b[:, h * 512:(h + 1) * 512], pt_sb[:, kc, :],
                                 vbuf[:, b, kc, h * 512:(h + 1) * 512],
                                 start=start, stop=stop),
                        waits=waits if h == 0 else [],
                        inc="t" if last else None)
                waits = []
        kt_users[b] = tpv
        if _PSTEP < 6:
            return
        for h in range(2):
            E("V", lambda e, ss=ss, h=h, aslot=aslot:
              e.scalar_tensor_tensor(obuf[:, ss, h * 512:(h + 1) * 512],
                                     obuf[:, ss, h * 512:(h + 1) * 512],
                                     sml[:, aslot:aslot + 1],
                                     ps_b[:, h * 512:(h + 1) * 512],
                                     ALU.mult, ALU.add),
              waits=[("t", tpv)] if h == 0 else [], inc="v")
        prev_oupd[0] = C("v")

    load_kt(0)
    if _DEBUG:
        E("Y", lambda e: e.dma_start(out=dbg_kt, in_=kbuf[:, 0, :, :]),
          waits=[("dy", kt_load[0])], inc="dy", inc_n=16)
        E("Y", lambda e: e.dma_start(out=dbg_vt, in_=vbuf[:, 0, :, :]),
          inc="dy", inc_n=16)
    for kt in range(min(KT, _KT)):
        if kt + 1 < min(KT, _KT):
            while pend:
                flash_tail()
            load_kt(kt + 1)
        for ss in list(range(kt, NSUB)) + [NQ - 1]:
            if pair_idx[0] < _NPAIR:
                flash_head(kt, ss)
                if len(pend) > 1:
                    flash_tail()
    while pend:
        flash_tail()

    p.barrier()
    if _DEBUG:
        for ss in range(NQ):
            E("Y", lambda e, ss=ss: e.dma_start(out=dbg_o[ss], in_=obuf[:, ss, :]),
              inc="dy", inc_n=16)
        E("Y", lambda e: e.dma_start(out=dbg_l, in_=l_sb[:]), inc="dy", inc_n=16)
        E("Y", lambda e: e.dma_start(out=dbg_nm, in_=nm_sb[:]), inc="dy", inc_n=16)
        E("Y", lambda e: e.dma_start(out=dbg_q, in_=q_fm[:]), inc="dy", inc_n=16)
        E("Y", lambda e: e.dma_start(out=dbg_sw, in_=swb[:]), inc="dy", inc_n=16)
        p.wait("Y", [("dy", C("dy"))])
        p.barrier()
    if _PHASES < 3:
        p.wait("Y", [("dy", C("dy"))])
        p.barrier()
        return _finish(nc, p)

    ar.reset(p2base)
    y_tm = ar.alloc("y_tm", [128, GRP, HID], F32)
    xq_fm = ar.alloc("xq_fm", [128, 8, GRP * 128], BF16)
    xqh_fm = ar.alloc("xqh_fm", [128, 8, 128], BF16)
    h_fm = ar.alloc("h_fm", [128, 2, GRP, 130], F32)
    ysil = ar.alloc("ysil", [128, 2, GRP, 128], F32)
    w1q_sb = ar.alloc("w1q_sb", [128, 2, 8, 128], BF16)
    w2q_sb = ar.alloc("w2q_sb", [128, 2, 1024], BF16)
    xtile = ar.alloc("xtile", [128, 2, 1024], F32)
    scr1 = ar.alloc("scr1", [128, 1024], F32)
    scr2 = ar.alloc("scr2", [128, 1024], F32)
    xq_tm = ar.alloc("xq_tm", [128, 1024], BF16)
    yq_tm = ar.alloc("yq_tm", [128, HID], BF16)
    yq_fm0 = ar.alloc("yq_fm0", [128, 32, 128], BF16)
    yq_fm1 = ar.alloc("yq_fm1", [128, 32, 128], BF16)
    outbuf = ar.alloc("outbuf", [128, 1024], F32)

    w1q_r = w1q_scr.rearrange("c p h -> p c h")

    xt_users = {}

    def load_xtile(ss, b):
        w = [("v", xt_users[b])] if b in xt_users else []
        return E("G", lambda e, ss=ss, b=b:
                 e.dma_start(out=xtile[:, b, :],
                             in_=d["x_tm"][ss * 128:(ss + 1) * 128, :]),
                 waits=w, inc="dg", inc_n=16)

    def ln_quant1(ss, b, xld, dest_fn):
        E("V", lambda e, ss=ss:
          e.reciprocal(sml[:, A_LINV:A_LINV + 1], l_sb[:, ss:ss + 1]),
          waits=[("dg", xld)], drain=True)
        E("V", lambda e, ss=ss, b=b:
          e.scalar_tensor_tensor(obuf[:, ss, :], obuf[:, ss, :],
                                 sml[:, A_LINV:A_LINV + 1], xtile[:, b, :],
                                 ALU.mult, ALU.add))
        E("V", lambda e, ss=ss:
          e.tensor_reduce(sml[:, A_T15:A_T15 + 1], obuf[:, ss, :], AX.X, ALU.add), drain=True)
        E("V", lambda e:
          e.tensor_scalar(sml[:, A_NEGMEAN:A_NEGMEAN + 1],
                          sml[:, A_T15:A_T15 + 1], -1.0 / D, None, ALU.mult), drain=True)
        E("V", lambda e, ss=ss:
          e.tensor_scalar(obuf[:, ss, :], obuf[:, ss, :],
                          sml[:, A_NEGMEAN:A_NEGMEAN + 1], None, ALU.add))
        E("V", lambda e, ss=ss:
          e.tensor_tensor(scr2[:], obuf[:, ss, :], obuf[:, ss, :], ALU.mult))
        E("V", lambda e:
          e.tensor_reduce(sml[:, A_SSQ:A_SSQ + 1], scr2[:], AX.X, ALU.add), drain=True)
        vv = E("V", lambda e:
               e.tensor_scalar(sml[:, A_VAR:A_VAR + 1], sml[:, A_SSQ:A_SSQ + 1],
                               1.0 / D, EPS, ALU.mult, ALU.add), inc="v", drain=True)
        sq = E("S", lambda e:
               e.activation(sml[:, A_RSTD:A_RSTD + 1], sml[:, A_VAR:A_VAR + 1],
                            AF.Sqrt), waits=[("v", vv)], inc="s")
        E("V", lambda e:
          e.reciprocal(sml[:, A_T15:A_T15 + 1], sml[:, A_RSTD:A_RSTD + 1]),
          waits=[("s", sq)], drain=True)
        E("V", lambda e, ss=ss:
          e.tensor_scalar(obuf[:, ss, :], obuf[:, ss, :],
                          sml[:, A_T15:A_T15 + 1], None, ALU.mult))
        E("V", lambda e, ss=ss:
          e.tensor_reduce(sml[:, A_AMAX:A_AMAX + 1], obuf[:, ss, :], AX.X, ALU.max,
                          apply_absolute_value=True), drain=True)
        E("V", lambda e:
          e.tensor_scalar(sml[:, A_T15:A_T15 + 1], sml[:, A_AMAX:A_AMAX + 1],
                          1e-5, None, ALU.max), drain=True)
        E("V", lambda e: e.reciprocal(sml[:, A_AMAX:A_AMAX + 1], sml[:, A_T15:A_T15 + 1]), drain=True)
        E("V", lambda e:
          e.tensor_scalar(sml[:, A_SX:A_SX + 1], sml[:, A_AMAX:A_AMAX + 1], 127.0,
                          None, ALU.mult), drain=True)
        E("V", lambda e: e.reciprocal(sml[:, A_SXINV:A_SXINV + 1], sml[:, A_SX:A_SX + 1]), drain=True)
        sxinv_slot = A_SXINV
        if ss == NQ - 1:
            E("V", lambda e:
              e.tensor_scalar(sml[:, A_MDIFF:A_MDIFF + 1], sml[:, A_SXINV:A_SXINV + 1],
                              hvalid_sb[:], None, ALU.mult), drain=True)
            sxinv_slot = A_MDIFF
        E("V", lambda e, ss=ss:
          e.tensor_scalar(scr1[:], obuf[:, ss, :], sml[:, A_SX:A_SX + 1], MAGIC,
                          ALU.mult, ALU.add))
        vq = E("V", lambda e, sxinv_slot=sxinv_slot:
               e.tensor_scalar(xq_tm[:], scr1[:], -MAGIC,
                               sml[:, sxinv_slot:sxinv_slot + 1],
                               ALU.add, ALU.mult), inc="v")
        tt = None
        for c in range(8):
            w = [("v", vq), ("s", C("s"))] if c == 0 else []
            tt = E("T", lambda e, c=c:
                   e.transpose(ps_pt[:, c, :], xq_tm[:, c * 128:(c + 1) * 128],
                               identb_sb[:]),
                   waits=w, inc="t" if c == 7 else None)
        E("V", lambda e, dest_fn=dest_fn: e.tensor_copy(dest_fn(), ps_pt[:]),
          waits=[("t", tt)], inc="v")
        return C("v"), C("s")

    xldh = load_xtile(16, 0)
    hv, hs = ln_quant1(16, 0, xldh, lambda: xqh_fm[:])
    xt_users[0] = hv

    w1ld_user = {}
    hevac_user = {}
    ysil_user = {}
    yevac_user = {}
    w2ld_user = {}
    out_dma = [0]

    for g in range(NGRP):
        s0, s1 = 2 * g, 2 * g + 1
        for sl, ss in enumerate((s0, s1)):
            b = ss % 2
            xld = load_xtile(ss, b)
            ln_quant1(ss, b, xld,
                      lambda sl=sl: xq_fm[:, :, sl * 128:(sl + 1) * 128])
            xt_users[b] = C("v")
        xq_v, xq_s = C("v"), C("s")
        if _DEBUG:
            for sl, ss in enumerate((s0, s1)):
                E("Y", lambda e, ss=ss: e.dma_start(out=dbg_xln[ss], in_=obuf[:, ss, :]),
                  waits=[("v", xq_v)], inc="dy", inc_n=16)
            if g == 0:
                E("Y", lambda e: e.dma_start(out=dbg_xln[16], in_=obuf[:, 16, :]),
                  inc="dy", inc_n=16)

        for hc in range(32):
            wb = hc % 2
            w = [("dy", wq_done)]
            if wb in w1ld_user:
                w.append(("t", w1ld_user[wb]))
            ldw1 = E("Y", lambda e, hc=hc, wb=wb:
                     e.dma_start(out=w1q_sb[:, wb, :, :],
                                 in_=w1q_r[:, :, hc * 128:(hc + 1) * 128]),
                     waits=w, inc="dy", inc_n=16)

            hb = hc % 2
            po = ps_b[:, hb * 512:hb * 512 + 256]
            ph = ps_b[:, hb * 512 + 256:hb * 512 + 384]
            for di in range(8):
                waits = []
                if di == 0:
                    waits = [("dy", ldw1), ("s", xq_s), ("v", xq_v),
                             ("v", hevac_user.get(hb, 0))]
                E("T", lambda e, di=di, wb=wb, po=po,
                  start=(di == 0), stop=(di == 7):
                  e.matmul(po, w1q_sb[:, wb, di, :], xq_fm[:, di, :],
                           start=start, stop=stop),
                  waits=waits)
            tw1 = None
            for di in range(8):
                waits = []
                if g == 0 and hc == 0 and di == 0:
                    waits = [("v", hv), ("s", hs)]
                tw1 = E("T", lambda e, di=di, wb=wb, ph=ph,
                        start=(di == 0), stop=(di == 7):
                        e.matmul(ph, w1q_sb[:, wb, di, :], xqh_fm[:, di, :],
                                 start=start, stop=stop),
                        waits=waits, inc="t" if di == 7 else None)
            w1ld_user[wb] = tw1

            E("V", lambda e, hb=hb, po=po:
              e.tensor_scalar(h_fm[:, hb, :, 1:129],
                              po.rearrange("p (g n) -> p g n", g=GRP),
                              swb[:, 1:2], None, ALU.mult),
              waits=[("t", tw1)])
            for si, (sl, ss) in enumerate(((0, s0), (1, s1))):
                E("V", lambda e, hb=hb, sl=sl, ss=ss, ph=ph:
                  e.tensor_scalar(h_fm[:, hb, sl, 0:1], ph[:, 2 * ss:2 * ss + 1],
                                  swb[:, 1:2], None, ALU.mult))
                hevac_user[hb] = E(
                    "V", lambda e, hb=hb, sl=sl, ss=ss, ph=ph:
                    e.tensor_scalar(h_fm[:, hb, sl, 129:130],
                                    ph[:, 2 * ss + 1:2 * ss + 2],
                                    swb[:, 1:2], None, ALU.mult),
                    inc="v" if si == 1 else None)

            cview = scr2[:, 0:GRP * 128].rearrange("p (g n) -> p g n", g=GRP)
            E("V", lambda e, hb=hb, hc=hc, cview=cview:
              e.tensor_scalar(cview, h_fm[:, hb, :, 0:128], cw_sb[:, hc:hc + 1],
                              None, ALU.mult))
            E("V", lambda e, hb=hb, hc=hc, cview=cview:
              e.scalar_tensor_tensor(cview, h_fm[:, hb, :, 1:129],
                                     cw_sb[:, 32 + hc:33 + hc], cview,
                                     ALU.mult, ALU.add))
            vc3 = E("V", lambda e, hb=hb, hc=hc, cview=cview:
                    e.scalar_tensor_tensor(cview, h_fm[:, hb, :, 2:130],
                                           cw_sb[:, 64 + hc:65 + hc], cview,
                                           ALU.mult, ALU.add), inc="v")
            w = [("v", vc3)]
            if hb in ysil_user:
                w.append(("t", ysil_user[hb]))
            ssil = E("S", lambda e, hb=hb, hc=hc, cview=cview:
                     e.activation(ysil[:, hb, :, :], cview, AF.Silu,
                                  bias=cw_sb[:, 96 + hc:97 + hc]),
                     waits=w, inc="s")
            tty = None
            for sl in range(GRP):
                w = []
                if sl == 0:
                    w = [("s", ssil)]
                    if (hc % 2) in yevac_user:
                        yv, ys = yevac_user[hc % 2]
                        w += [("v", yv), ("s", ys)]
                tty = E("T", lambda e, hb=hb, sl=sl:
                        e.transpose(ps_t2[:, sl, :], ysil[:, hb, sl, :], identf_sb[:]),
                        waits=w, inc="t" if sl == GRP - 1 else None)
            ysil_user[hb] = tty
            E("V", lambda e, hc=hc:
              e.tensor_copy(y_tm[:, :, hc * 128:(hc + 1) * 128], ps_t2[:, 0:GRP, :]),
              waits=[("t", tty)], inc="v")
            yevac_user[hc % 2] = (C("v"), C("s"))
        y_v, y_s = C("v"), C("s")

        for sl, yqf in ((0, yq_fm0), (1, yq_fm1)):
            E("V", lambda e, sl=sl:
              e.tensor_reduce(sml[:, A_AMAX:A_AMAX + 1], y_tm[:, sl, :], AX.X,
                              ALU.max, apply_absolute_value=True),
              waits=[("v", y_v), ("s", y_s)] if sl == 0 else [], drain=True)
            E("V", lambda e:
              e.tensor_scalar(sml[:, A_T15:A_T15 + 1], sml[:, A_AMAX:A_AMAX + 1],
                              1e-5, None, ALU.max), drain=True)
            E("V", lambda e:
              e.reciprocal(sml[:, A_AMAX:A_AMAX + 1], sml[:, A_T15:A_T15 + 1]), drain=True)
            E("V", lambda e:
              e.tensor_scalar(sml[:, A_SX:A_SX + 1], sml[:, A_AMAX:A_AMAX + 1], 127.0,
                              None, ALU.mult), drain=True)
            E("V", lambda e:
              e.reciprocal(sml[:, A_SXINV:A_SXINV + 1], sml[:, A_SX:A_SX + 1]), drain=True)
            vyq = None
            for hh in range(4):
                E("V", lambda e, sl=sl, hh=hh:
                  e.tensor_scalar(scr1[:], y_tm[:, sl, hh * 1024:(hh + 1) * 1024],
                                  sml[:, A_SX:A_SX + 1], MAGIC, ALU.mult, ALU.add))
                vyq = E("V", lambda e, hh=hh:
                        e.tensor_scalar(yq_tm[:, hh * 1024:(hh + 1) * 1024], scr1[:],
                                        -MAGIC, sml[:, A_SXINV:A_SXINV + 1],
                                        ALU.add, ALU.mult),
                        inc="v" if hh == 3 else None)
            for blk in range(4):
                ttq = None
                for c in range(8):
                    cc = blk * 8 + c
                    w = []
                    if c == 0:
                        w = [("v", C("v") if blk else vyq), ("s", C("s"))]
                    ttq = E("T", lambda e, cc=cc, c=c:
                            e.transpose(ps_pt[:, c, :],
                                        yq_tm[:, cc * 128:(cc + 1) * 128],
                                        identb_sb[:]),
                            waits=w, inc="t" if c == 7 else None)
                E("V", lambda e, blk=blk, yqf=yqf:
                  e.tensor_copy(yqf[:, blk * 8:(blk + 1) * 8, :], ps_pt[:]),
                  waits=[("t", ttq)], inc="v")
        yq_v, yq_s = C("v"), C("s")

        for hc in range(32):
            wb = hc % 2
            w = [("dy", wq_done)]
            if wb in w2ld_user:
                w.append(("t", w2ld_user[wb]))
            ldw2 = E("S", lambda e, hc=hc, wb=wb:
                     e.dma_start(out=w2q_sb[:, wb, :], in_=w2q_scr[hc]),
                     waits=w, inc="ds", inc_n=16)
            waits = [("ds", ldw2)]
            if hc == 0:
                waits += [("v", yq_v), ("s", yq_s)]
            lastmm = None
            for sl, yqf, psd in ((0, yq_fm0, ps_a), (1, yq_fm1, ps_b)):
                for h in range(2):
                    last = (sl == 1 and h == 1)
                    lastmm = E("T", lambda e, hc=hc, h=h, wb=wb, yqf=yqf, psd=psd,
                               start=(hc == 0), stop=(hc == 31):
                               e.matmul(psd[:, h * 512:(h + 1) * 512], yqf[:, hc, :],
                                        w2q_sb[:, wb, h * 512:(h + 1) * 512],
                                        start=start, stop=stop),
                               waits=waits, inc="t" if last else None)
                    waits = []
            w2ld_user[wb] = lastmm
        tw2 = C("t")

        for sl, ss, psd in ((0, s0, ps_a), (1, s1, ps_b)):
            for h in range(2):
                E("V", lambda e, ss=ss, psd=psd, h=h:
                  e.scalar_tensor_tensor(scr1[:, h * 512:(h + 1) * 512],
                                         psd[:, h * 512:(h + 1) * 512], swb[:, 3:4],
                                         obuf[:, ss, h * 512:(h + 1) * 512],
                                         ALU.mult, ALU.add),
                  waits=[("t", tw2)] if h == 0 else [])
            E("V", lambda e:
              e.tensor_reduce(sml[:, A_T15:A_T15 + 1], scr1[:], AX.X, ALU.add), drain=True)
            E("V", lambda e:
              e.tensor_scalar(sml[:, A_NEGMEAN:A_NEGMEAN + 1],
                              sml[:, A_T15:A_T15 + 1], -1.0 / D, None, ALU.mult), drain=True)
            E("V", lambda e:
              e.tensor_scalar(scr1[:], scr1[:], sml[:, A_NEGMEAN:A_NEGMEAN + 1],
                              None, ALU.add))
            E("V", lambda e:
              e.tensor_tensor(scr2[:], scr1[:], scr1[:], ALU.mult))
            E("V", lambda e:
              e.tensor_reduce(sml[:, A_SSQ:A_SSQ + 1], scr2[:], AX.X, ALU.add), drain=True)
            vv = E("V", lambda e:
                   e.tensor_scalar(sml[:, A_VAR:A_VAR + 1], sml[:, A_SSQ:A_SSQ + 1],
                                   1.0 / D, EPS, ALU.mult, ALU.add), inc="v", drain=True)
            sq = E("S", lambda e:
                   e.activation(sml[:, A_RSTD:A_RSTD + 1], sml[:, A_VAR:A_VAR + 1],
                                AF.Sqrt), waits=[("v", vv)], inc="s")
            E("V", lambda e:
              e.reciprocal(sml[:, A_T15:A_T15 + 1], sml[:, A_RSTD:A_RSTD + 1]),
              waits=[("s", sq)], drain=True)
            vout = E("V", lambda e:
                     e.tensor_scalar(outbuf[:], scr1[:], sml[:, A_T15:A_T15 + 1],
                                     None, ALU.mult),
                     waits=[("dy", out_dma[0])], inc="v")
            out_dma[0] = E("Y", lambda e, ss=ss:
                           e.dma_start(out=y_out[ss * 128:(ss + 1) * 128, :],
                                       in_=outbuf[:]),
                           waits=[("v", vout)], inc="dy", inc_n=16)

    p.wait("Y", [("dy", C("dy"))])
    p.barrier()
    return _finish(nc, p)


def _finish(nc, p):
    with nc.Block() as block:
        @block.tensor
        def _(t):
            for fn in p.ops["T"]:
                fn(t)

        @block.vector
        def _(v):
            for fn in p.ops["V"]:
                fn(v)

        @block.scalar
        def _(s):
            for fn in p.ops["S"]:
                fn(s)

        @block.gpsimd
        def _(g):
            for fn in p.ops["G"]:
                fn(g)

        @block.sync
        def _(y):
            for fn in p.ops["Y"]:
                fn(y)

    return nc


def _host_inputs(x, Qw, Kw, Vw, w1, conv_w, conv_b, w2):
    x2 = np.ascontiguousarray(np.asarray(x, np.float32).reshape(S, D))
    xT = np.ascontiguousarray(x2.T)
    w1t = np.ascontiguousarray(np.asarray(w1, np.float32).T)
    w2t = np.ascontiguousarray(np.asarray(w2, np.float32).T)
    cwf = np.asarray(conv_w, np.float32).reshape(HID, 3)
    cbf = np.asarray(conv_b, np.float32)
    cw = np.zeros((128, 128), np.float32)
    cw[:, 0:32] = cwf[:, 0].reshape(32, 128).T
    cw[:, 32:64] = cwf[:, 1].reshape(32, 128).T
    cw[:, 64:96] = cwf[:, 2].reshape(32, 128).T
    cw[:, 96:128] = cbf.reshape(32, 128).T
    identb = np.eye(128, dtype=ml_dtypes.bfloat16)
    identf = np.eye(128, dtype=np.float32)
    ones_col = np.ones((128, 1), np.float32)
    ones_row = np.ones((1, 128), np.float32)

    in_maps = []
    for j in range(NCORES):
        own_cols = (1024 * np.arange(NSUB)[:, None] + 128 * j
                    + np.arange(128)[None, :]).reshape(-1)
        halo_pos = np.zeros(32, np.int64)
        halo_valid = np.zeros((128, 1), np.float32)
        for i in range(NSUB):
            lp = 1024 * i + 128 * j - 1
            rp = 1024 * i + 128 * j + 128
            halo_pos[2 * i] = lp if lp >= 0 else 0
            halo_pos[2 * i + 1] = rp if rp < S else 0
            halo_valid[2 * i, 0] = 1.0 if lp >= 0 else 0.0
            halo_valid[2 * i + 1, 0] = 1.0 if rp < S else 0.0
        all_cols = np.concatenate([own_cols, halo_pos, np.zeros(96, np.int64)])
        xT_own = np.ascontiguousarray(xT[:, all_cols])
        x_tm = np.ascontiguousarray(x2[all_cols, :])
        for h in range(32):
            if halo_valid[h, 0] == 0.0:
                xT_own[:, 2048 + h] = 0.0
                x_tm[2048 + h, :] = 0.0
        xT_own[:, 2048 + 32:] = 0.0
        x_tm[2048 + 32:, :] = 0.0

        r = np.arange(128)[:, None]
        c = np.arange(1024)[None, :]
        dmask = np.where(c <= 128 * j + r, 0.0, NEG).astype(np.float32)

        hmask = np.full((KT, 128, 1024), NEG, np.float32)
        for h in range(32):
            if halo_valid[h, 0] == 1.0:
                pos = int(halo_pos[h])
                for kt in range(pos // 1024 + 1):
                    n = min(1024, pos - 1024 * kt + 1)
                    hmask[kt, h, 0:n] = 0.0
            else:
                hmask[0, h, 0] = 0.0
        hmask[0, 32:, 0] = 0.0

        in_maps.append({
            "xT_own": xT_own, "x_tm": x_tm,
            "qw": np.asarray(Qw, np.float32), "kw": np.asarray(Kw, np.float32),
            "vw": np.asarray(Vw, np.float32),
            "w1t": w1t, "w2t": w2t,
            "cw": cw, "dmask": dmask, "hmask": hmask, "hvalid": halo_valid,
            "identb": identb, "identf": identf,
            "ones_col": ones_col, "ones_row": ones_row,
        })
    return in_maps


_NC_CACHE = [None]


def kernel(L0_attn_x, L0_attn_Qw, L0_attn_Kw, L0_attn_Vw,
           L0_ffn_w1, L0_ffn_conv_w, L0_ffn_conv_b, L0_ffn_w2):
    in_maps = _host_inputs(L0_attn_x, L0_attn_Qw, L0_attn_Kw, L0_attn_Vw,
                           L0_ffn_w1, L0_ffn_conv_w, L0_ffn_conv_b, L0_ffn_w2)
    if _NC_CACHE[0] is None:
        _NC_CACHE[0] = build_program()
    nc = _NC_CACHE[0]
    res = run_bass_kernel_spmd(nc, in_maps, core_ids=list(range(NCORES)))
    out = np.empty((1, S, D), np.float32)
    ov = out.reshape(NSUB, NCORES, 128, D)
    for j in range(NCORES):
        ov[:, j] = np.asarray(res.results[j]["y"]).reshape(NSUB, 128, D)
    return out


# version 08
# speedup vs baseline: 1.1423x; 1.0684x over previous
import os
import numpy as np
import ml_dtypes
import concourse.bass as bass
import concourse.mybir as mybir
from concourse.bass_utils import run_bass_kernel_spmd

F32 = mybir.dt.float32
BF16 = mybir.dt.bfloat16
AF = mybir.ActivationFunctionType
ALU = mybir.AluOpType
AX = mybir.AxisListType

NCORES = 8
S = 16384
D = 1024
HID = 4096
NSUB = 16
NQ = NSUB + 1
KT = 16
MAGIC = 12582912.0
NM_INIT = 100.0
NEG = -1e9
EPS = 1e-5
GRP = 2
NGRP = NSUB // GRP

ENGINES = ("T", "V", "S", "G", "Y")


class Plan:

    def __init__(self, nc):
        self.nc = nc
        self.ops = {e: [] for e in ENGINES}
        self.sems = {}
        self.cnt = {}
        self.bar_round = 0

    def sem(self, name):
        if name not in self.sems:
            self.sems[name] = self.nc.alloc_semaphore(name)
            self.cnt[name] = 0
        return self.sems[name]

    def emit(self, e, fn, waits=(), inc=None, inc_n=1, drain=False):
        sems = self.sems
        wl = tuple((sems[s], v) for (s, v) in waits if v > 0)

        def run(eng, fn=fn, wl=wl, inc=inc, inc_n=inc_n, drain=drain):
            for s, v in wl:
                eng.wait_ge(s, v)
            r = fn(eng)
            if drain:
                eng.drain()
            if inc is not None:
                r.then_inc(sems[inc], inc_n)

        self.ops[e].append(run)
        if inc is not None:
            self.cnt[inc] += inc_n
            return self.cnt[inc]
        return None

    def wait(self, e, waits):
        sems = self.sems
        wl = tuple((sems[s], v) for (s, v) in waits if v > 0)

        def run(eng, wl=wl):
            for s, v in wl:
                eng.wait_ge(s, v)

        self.ops[e].append(run)

    def c(self, name):
        return self.cnt[name]

    def barrier(self):
        bar = self.sem("bar")
        own_dma = {"Y": "dy", "G": "dg", "S": "ds"}
        self.bar_round += 1
        target = len(ENGINES) * self.bar_round
        for e in ENGINES:
            waits = []
            if e in own_dma:
                waits.append((own_dma[e], self.cnt[own_dma[e]]))
            def run_b(eng, bar=self.sems["bar"], waits=tuple(
                    (self.sems[s], v) for (s, v) in waits if v > 0)):
                for s_, v_ in waits:
                    eng.wait_ge(s_, v_)
                eng.drain()
                eng.sem_inc(bar, 1)
            self.ops[e].append(run_b)
            self.cnt["bar"] += 1
        for e in ENGINES:
            self.wait(e, [("bar", target)])


class Arena:
    def __init__(self, nc, base, limit):
        self.nc = nc
        self.base = base
        self.limit = limit
        self.cur = base

    def alloc(self, name, shape, dtype, align=64):
        per_part = int(np.prod(shape[1:])) * mybir.dt.size(dtype)
        off = (self.cur + align - 1) // align * align
        assert off + per_part <= self.limit, (
            f"arena overflow: {name} at {off}+{per_part} > {self.limit}")
        self.cur = off + per_part
        return self.nc.alloc_sbuf_tensor_at(name, shape, dtype, offset=off)

    def mark(self):
        return self.cur

    def reset(self, to):
        self.cur = to



def _copy(e, dst, src):
    if hasattr(e, "tensor_copy"):
        e_fn = e.tensor_copy
    else:
        e_fn = e.copy
    return e_fn(dst, src)

_PHASES = int(os.environ.get("MK_PHASES", "4"))
_SKIP_AG = os.environ.get("MK_SKIP_AG", "0") == "1"
_P1B = int(os.environ.get("MK_P1B", "5"))
_KT = int(os.environ.get("MK_KT", "16"))
_NPAIR = int(os.environ.get("MK_NPAIR", "9999"))
_PSTEP = int(os.environ.get("MK_PSTEP", "6"))
_SEV = os.environ.get("MK_SEV", "1") == "1"
_DEBUG = os.environ.get("MK_DEBUG", "0") == "1"


def build_program():
    nc = bass.Bass("TRN2", target_bir_lowering=False)

    d = {}
    d["xT_own"] = nc.dram_tensor("xT_own", [D, NQ * 128], F32, kind="ExternalInput").ap()
    d["x_tm"] = nc.dram_tensor("x_tm", [NQ * 128, D], F32, kind="ExternalInput").ap()
    d["qw"] = nc.dram_tensor("qw", [D, D], F32, kind="ExternalInput").ap()
    d["kw"] = nc.dram_tensor("kw", [D, D], F32, kind="ExternalInput").ap()
    d["vw"] = nc.dram_tensor("vw", [D, D], F32, kind="ExternalInput").ap()
    d["w1t"] = nc.dram_tensor("w1t", [D, HID], F32, kind="ExternalInput").ap()
    d["w2t"] = nc.dram_tensor("w2t", [HID, D], F32, kind="ExternalInput").ap()
    d["cw"] = nc.dram_tensor("cw", [128, 128], F32, kind="ExternalInput").ap()
    d["dmask"] = nc.dram_tensor("dmask", [128, 1024], F32, kind="ExternalInput").ap()
    d["hmask"] = nc.dram_tensor("hmask", [KT, 128, 1024], F32, kind="ExternalInput").ap()
    d["hvalid"] = nc.dram_tensor("hvalid", [128, 1], F32, kind="ExternalInput").ap()
    d["identb"] = nc.dram_tensor("identb", [128, 128], BF16, kind="ExternalInput").ap()
    d["identf"] = nc.dram_tensor("identf", [128, 128], F32, kind="ExternalInput").ap()
    d["ones_col"] = nc.dram_tensor("ones_col", [128, 1], F32, kind="ExternalInput").ap()
    d["ones_row"] = nc.dram_tensor("ones_row", [1, 128], F32, kind="ExternalInput").ap()
    y_out = nc.dram_tensor("y", [NSUB * 128, D], F32, kind="ExternalOutput").ap()
    if _DEBUG:
        dbg_o = nc.dram_tensor("dbg_o", [NQ, 128, D], F32, kind="ExternalOutput").ap()
        dbg_l = nc.dram_tensor("dbg_l", [128, NQ], F32, kind="ExternalOutput").ap()
        dbg_nm = nc.dram_tensor("dbg_nm", [128, NQ], F32, kind="ExternalOutput").ap()
        dbg_q = nc.dram_tensor("dbg_q", [128, 8, NQ * 128], BF16, kind="ExternalOutput").ap()
        dbg_sw = nc.dram_tensor("dbg_sw", [128, 4], F32, kind="ExternalOutput").ap()
        dbg_xln = nc.dram_tensor("dbg_xln", [NQ, 128, D], F32, kind="ExternalOutput").ap()
        dbg_kt = nc.dram_tensor("dbg_kt", [128, 8, 1024], BF16, kind="ExternalOutput").ap()
        dbg_vt = nc.dram_tensor("dbg_vt", [128, 8, 1024], BF16, kind="ExternalOutput").ap()
        dbg_p = nc.dram_tensor("dbg_p", [128, 1024], BF16, kind="ExternalOutput").ap()
        dbg_s = nc.dram_tensor("dbg_s", [128, 1024], F32, kind="ExternalOutput").ap()
        dbg_sml = nc.dram_tensor("dbg_sml", [128, 16], F32, kind="ExternalOutput").ap()

    k_own = nc.dram_tensor("k_own", [D, 2048], BF16, kind="Internal").ap()
    v_own = nc.dram_tensor("v_own", [2048, D], BF16, kind="Internal").ap()
    k_all = nc.dram_tensor("k_all", [NCORES * D, 2048], BF16, kind="Internal").ap()
    v_all = nc.dram_tensor("v_all", [NCORES * 2048, D], BF16, kind="Internal").ap()
    w1q_scr = nc.dram_tensor("w1q_scr", [8, 128, HID], BF16, kind="Internal").ap()
    w2q_scr = nc.dram_tensor("w2q_scr", [32, 128, D], BF16, kind="Internal").ap()

    q_fm = nc.alloc_sbuf_tensor("q_fm", [128, 8, NQ * 128], BF16)
    dmask_sb = nc.alloc_sbuf_tensor("dmask_sb", [128, 1024], F32)
    cw_sb = nc.alloc_sbuf_tensor("cw_sb", [128, 128], F32)
    identb_sb = nc.alloc_sbuf_tensor("identb_sb", [128, 128], BF16)
    identf_sb = nc.alloc_sbuf_tensor("identf_sb", [128, 128], F32)
    hvalid_sb = nc.alloc_sbuf_tensor("hvalid_sb", [128, 1], F32)
    ones_col_sb = nc.alloc_sbuf_tensor("ones_col_sb", [128, 1], F32)
    ones_row_sb = nc.alloc_sbuf_tensor("ones_row_sb", [1, 128], F32)
    swb = nc.alloc_sbuf_tensor("swb", [128, 4], F32)
    nm_sb = nc.alloc_sbuf_tensor("nm_sb", [128, NQ], F32)
    l_sb = nc.alloc_sbuf_tensor("l_sb", [128, NQ], F32)
    sml = nc.alloc_sbuf_tensor("sml", [128, 20], F32)
    (A_ALPHA0, A_ALPHA1, A_LP0, A_LP1, A_NMRAW, A_NMNEW, A_MDIFF,
     A_LINV, A_NEGMEAN, A_SSQ, A_VAR, A_RSTD, A_AMAX, A_SX, A_SXINV, A_T15) = range(16)
    w4 = nc.alloc_sbuf_tensor("w4", [1, 8], F32)
    w4b = nc.alloc_sbuf_tensor("w4b", [1, 4], BF16)
    ones_col_bf = nc.alloc_sbuf_tensor("ones_col_bf", [128, 1], BF16)
    ones_row_bf = nc.alloc_sbuf_tensor("ones_row_bf", [1, 128], BF16)

    base = nc.lookup_mloc(w4).addr + 64
    base = (base + 63) // 64 * 64
    ar = Arena(nc, base, 229376)

    obuf = ar.alloc("obuf", [128, NQ, 1024], F32)
    p2base = ar.mark()

    ps_a = nc.alloc_psum_tensor("ps_a", [128, 1024], F32)
    ps_b = nc.alloc_psum_tensor("ps_b", [128, 1024], F32)
    ps_c = nc.alloc_psum_tensor("ps_c", [128, 1024], F32)
    ps_pt = nc.alloc_psum_tensor("ps_pt", [128, 8, 128], BF16)
    ps_d = nc.alloc_psum_tensor("ps_d", [128, 512], F32)
    ps_t2 = ps_d.ap().rearrange("p (a n) -> p a n", n=128)
    ps_sm = ps_d

    p = Plan(nc)
    for s in ("dy", "dg", "ds", "t", "v", "s", "bar", "g"):
        p.sem(s)
    E = p.emit
    C = p.c

    for nm_, dst in (("dmask", dmask_sb), ("cw", cw_sb), ("identb", identb_sb),
                     ("identf", identf_sb), ("hvalid", hvalid_sb),
                     ("ones_col", ones_col_sb), ("ones_row", ones_row_sb)):
        E("S", lambda e, nm_=nm_, dst=dst: e.dma_start(out=dst[:], in_=d[nm_]),
          inc="ds", inc_n=16)
    smalls_done = C("ds")
    E("G", lambda e: e.dma_start(out=ones_col_bf[:], in_=d["ones_col"]),
      inc="dg", inc_n=16)
    E("G", lambda e: e.dma_start(out=ones_row_bf[:], in_=d["ones_row"]),
      inc="dg", inc_n=16)
    p.wait("T", [("dg", C("dg"))])

    ar.reset(p2base)
    wtile = ar.alloc("wtile", [128, 2, 4096], F32)
    wqtile = ar.alloc("wqtile", [128, 2, 4096], BF16)
    wacc = ar.alloc("wacc", [128, 2], F32)
    wpart = ar.alloc("wpart", [128, 40], F32)
    whl = ar.alloc("whl", [128, 4], BF16)
    whl32 = ar.alloc("whl32", [128, 2], F32)
    whl32b = ar.alloc("whl32b", [128, 2], F32)

    wt_r1 = d["w1t"].rearrange("(c p) h -> c p h", p=128)
    wt_r2 = d["w2t"].rearrange("(c p) h -> c p h", p=128)

    wtile_user = {}
    wq_store = {}

    def w_pass(which, quantize):
        ntile, src, cols = ((8, wt_r1, 4096) if which == 0 else (32, wt_r2, 1024))
        scr = w1q_scr if which == 0 else w2q_scr
        swcol = 0 if which == 0 else 2
        for i in range(ntile):
            b = i % 2
            w = []
            if b in wtile_user:
                w.append(("v", wtile_user[b]))
            ld = E("Y", lambda e, i=i, b=b, src=src, cols=cols:
                   e.dma_start(out=wtile[:, b, 0:cols], in_=src[i]),
                   waits=w, inc="dy", inc_n=16)
            if not quantize:
                col = i if which == 0 else 8 + i
                E("V", lambda e, b=b, cols=cols:
                  e.tensor_reduce(sml[:, 8:8 + cols // 1024],
                                  wtile[:, b, 0:cols].rearrange("p (a q) -> p a q", q=1024),
                                  AX.X, ALU.add, apply_absolute_value=True),
                  waits=[("dy", ld)], drain=True)
                wtile_user[b] = E("V", lambda e, cols=cols, col=col:
                                  e.tensor_reduce(wpart[:, col:col + 1],
                                                  sml[:, 8:8 + cols // 1024],
                                                  AX.X, ALU.add),
                                  inc="v", drain=True)
            else:
                w2_ = [("dy", ld)]
                if b in wq_store:
                    w2_.append(("dy", wq_store[b]))
                E("V", lambda e, b=b, cols=cols, swcol=swcol:
                  e.tensor_scalar(wtile[:, b, 0:cols], wtile[:, b, 0:cols],
                                  swb[:, swcol:swcol + 1], MAGIC, ALU.mult, ALU.add),
                  waits=w2_)
                E("V", lambda e, b=b, cols=cols:
                  e.tensor_scalar(wtile[:, b, 0:cols], wtile[:, b, 0:cols],
                                  -MAGIC, 1.0, ALU.add, ALU.min))
                vq = E("V", lambda e, b=b, cols=cols:
                       e.tensor_scalar(wqtile[:, b, 0:cols], wtile[:, b, 0:cols],
                                       -1.0, None, ALU.max), inc="v")
                wtile_user[b] = vq
                wq_store[b] = E("Y", lambda e, i=i, b=b, cols=cols, scr=scr:
                                e.dma_start(out=scr[i], in_=wqtile[:, b, 0:cols]),
                                waits=[("v", vq)], inc="dy", inc_n=16)

    w_pass(0, False)
    w_pass(1, False)

    E("V", lambda e: e.tensor_reduce(wacc[:, 0:1], wpart[:, 0:8], AX.X, ALU.add), drain=True)
    E("V", lambda e: e.tensor_reduce(wacc[:, 1:2], wpart[:, 8:40], AX.X, ALU.add), drain=True)
    E("V", lambda e: e.tensor_copy(whl[:, 0:2], wacc[:]), drain=True)
    E("V", lambda e: e.tensor_copy(whl32[:], whl[:, 0:2]), drain=True)
    E("V", lambda e: e.tensor_sub(whl32b[:], wacc[:], whl32[:]), drain=True)
    vhl = E("V", lambda e: e.tensor_copy(whl[:, 2:4], whl32b[:]), inc="v", drain=True)
    E("T", lambda e: e.matmul(ps_sm.ap()[0:1, 384:386], ones_col_bf[:], whl[:, 0:2],
                              start=True, stop=False),
      waits=[("v", vhl), ("ds", smalls_done)])
    E("T", lambda e: e.matmul(ps_sm.ap()[0:1, 384:386], ones_col_bf[:], whl[:, 2:4],
                              start=False, stop=True), inc="t")
    vt2 = E("V", lambda e: e.tensor_copy(w4[0:1, 0:2], ps_sm.ap()[0:1, 384:386]),
            waits=[("t", C("t"))], inc="v", drain=True)
    E("V", lambda e: e.tensor_copy(w4b[0:1, 0:2], w4[0:1, 0:2]), drain=True)
    E("V", lambda e: e.tensor_copy(w4[0:1, 4:6], w4b[0:1, 0:2]), drain=True)
    E("V", lambda e: e.tensor_sub(w4[0:1, 6:8], w4[0:1, 0:2], w4[0:1, 4:6]), drain=True)
    vt2 = E("V", lambda e: e.tensor_copy(w4b[0:1, 2:4], w4[0:1, 6:8]), inc="v", drain=True)
    E("T", lambda e: e.matmul(ps_sm.ap()[:, 386:388], ones_row_bf[:], w4b[0:1, 0:2],
                              start=True, stop=False), waits=[("v", vt2)])
    E("T", lambda e: e.matmul(ps_sm.ap()[:, 386:388], ones_row_bf[:], w4b[0:1, 2:4],
                              start=False, stop=True), inc="t")
    E("V", lambda e: e.tensor_scalar(swb[:, 1:2], ps_sm.ap()[:, 386:387], 1.0 / (HID * D),
                                     None, ALU.mult), waits=[("t", C("t"))], drain=True)
    E("V", lambda e: e.tensor_scalar(swb[:, 3:4], ps_sm.ap()[:, 387:388], 1.0 / (HID * D),
                                     None, ALU.mult), drain=True)
    E("V", lambda e: e.reciprocal(swb[:, 0:1], swb[:, 1:2]), drain=True)
    E("V", lambda e: e.reciprocal(swb[:, 2:3], swb[:, 3:4]), inc="v", drain=True)

    w_pass(0, True)
    w_pass(1, True)
    wq_done = C("dy")

    p.barrier()
    if _PHASES < 1:
        p.wait("Y", [("dy", C("dy"))])
        p.barrier()
        return _finish(nc, p)

    ar.reset(p2base)
    xfm = ar.alloc("xfm", [128, 2, 8, 1024], BF16)
    wq_sb = ar.alloc("wq_sb", [128, 8, 1024], BF16)
    wk_sb = ar.alloc("wk_sb", [128, 8, 1024], BF16)
    wv_sb = ar.alloc("wv_sb", [128, 8, 1024], BF16)
    kvstage = ar.alloc("kvstage", [128, 4, 1024], BF16)

    for nm_, dst in (("qw", wq_sb), ("kw", wk_sb), ("vw", wv_sb)):
        E("G", lambda e, nm_=nm_, dst=dst:
          e.dma_start(out=dst[:], in_=d[nm_].rearrange("(c p) n -> p c n", p=128)),
          inc="dg", inc_n=16)
    w_qkv_loaded = C("dg")

    xT_r = d["xT_own"].rearrange("(c p) n -> p c n", p=128)
    k_own_r = k_own.rearrange("(c p) n -> c p n", p=128)
    v_own_r = v_own.rearrange("(t p) n -> t p n", p=128)

    stage_rot = [0]
    stage_user = {}

    def evac_store(psum, qcols, waits_t, store_fn):
        sl = stage_rot[0] % 4
        stage_rot[0] += 1
        w = list(waits_t)
        if sl in stage_user:
            w.append(("dy", stage_user[sl]))
        vc = E("V", lambda e, sl=sl, qcols=qcols, psum=psum:
               e.tensor_copy(kvstage[:, sl, 0:qcols], psum[:, 0:qcols]),
               waits=w, inc="v")
        dy = E("Y", lambda e, sl=sl, qcols=qcols, store_fn=store_fn:
               store_fn(e, kvstage[:, sl, 0:qcols]),
               waits=[("v", vc)], inc="dy", inc_n=16)
        stage_user[sl] = dy

    xfm_user = {}

    chunks = [(0, 1024, 0), (1024, 1024, 1), (2048, 128, 0)]
    if _P1B < 2:
        chunks = []
    for c0, cn, b in chunks:
        w = [("t", xfm_user[b])] if b in xfm_user else []
        ld = E("G", lambda e, b=b, c0=c0, cn=cn:
               e.dma_start(out=xfm[:, b, :, 0:cn], in_=xT_r[:, :, c0:c0 + cn]),
               waits=w, inc="dg", inc_n=16)
        for do in range(8):
            psum = [ps_a, ps_b][do % 2]
            hs = list(range(0, cn, 512))
            tC = None
            for h in hs:
                hn = min(512, cn - h)
                for di in range(8):
                    waits = []
                    if di == 0 and h == 0:
                        waits = [("dg", max(ld, w_qkv_loaded)), ("v", C("v"))]
                    last = (h == hs[-1] and di == 7)
                    tC = E("T", lambda e, psum=psum, h=h, hn=hn, di=di, do=do, b=b,
                           start=(di == 0), stop=(di == 7):
                           e.matmul(psum[:, h:h + hn],
                                    wq_sb[:, di, do * 128:(do + 1) * 128],
                                    xfm[:, b, di, h:h + hn], start=start, stop=stop),
                           waits=waits, inc="t" if last else None)
            xfm_user[b] = tC
            E("V", lambda e, psum=psum, do=do, c0=c0, cn=cn:
              e.tensor_copy(q_fm[:, do, c0:c0 + cn], psum[:, 0:cn]),
              waits=[("t", tC)], inc="v")

    for ci in range(2 if _P1B >= 3 else 0):
        c0 = ci * 1024
        b = ci % 2
        ld = E("G", lambda e, b=b, c0=c0:
               e.dma_start(out=xfm[:, b, :, :], in_=xT_r[:, :, c0:c0 + 1024]),
               waits=[("t", xfm_user[b])], inc="dg", inc_n=16)
        for do in range(8):
            psum = [ps_a, ps_b][do % 2]
            tC = None
            for h in range(0, 1024, 512):
                for di in range(8):
                    waits = [("dg", ld), ("v", C("v"))] if (di == 0 and h == 0) else []
                    last = (h == 512 and di == 7)
                    tC = E("T", lambda e, psum=psum, h=h, di=di, do=do, b=b,
                           start=(di == 0), stop=(di == 7):
                           e.matmul(psum[:, h:h + 512],
                                    wk_sb[:, di, do * 128:(do + 1) * 128],
                                    xfm[:, b, di, h:h + 512], start=start, stop=stop),
                           waits=waits, inc="t" if last else None)
            evac_store(psum, 1024, [("t", tC)],
                       lambda e, src, do=do, c0=c0:
                       e.dma_start(out=k_own_r[do][:, c0:c0 + 1024], in_=src))
        for tc in range(8 if _P1B >= 4 else 0):
            psum = [ps_a, ps_b][tc % 2]
            tC = None
            for h in range(0, 1024, 512):
                for di in range(8):
                    waits = [("v", C("v"))] if (di == 0 and h == 0) else []
                    last = (h == 512 and di == 7)
                    tC = E("T", lambda e, psum=psum, h=h, di=di, tc=tc, b=b,
                           start=(di == 0), stop=(di == 7):
                           e.matmul(psum[:, h:h + 512],
                                    xfm[:, b, di, tc * 128:(tc + 1) * 128],
                                    wv_sb[:, di, h:h + 512], start=start, stop=stop),
                           waits=waits, inc="t" if last else None)
            xfm_user[b] = tC
            evac_store(psum, 1024, [("t", tC)],
                       lambda e, src, ci=ci, tc=tc:
                       e.dma_start(out=v_own_r[ci * 8 + tc], in_=src))

    if _P1B < 5:
        ag_done = 0
    elif _SKIP_AG:
        E("G", lambda e: e.dma_start(out=k_all[0:D, :], in_=k_own),
          waits=[("dy", C("dy"))], inc="g", inc_n=16)
        E("G", lambda e: e.dma_start(out=v_all[0:2048, :], in_=v_own),
          inc="g", inc_n=16)
        ag_done = C("g")
    else:
        E("G", lambda e: e.collective_compute(
            "AllGather", ALU.bypass, replica_groups=[list(range(NCORES))],
            ins=[k_own], outs=[k_all]),
          waits=[("dy", C("dy"))], inc="g")
        E("G", lambda e: e.collective_compute(
            "AllGather", ALU.bypass, replica_groups=[list(range(NCORES))],
            ins=[v_own], outs=[v_all]), inc="g")
        ag_done = C("g")

    p.barrier()
    if _PHASES < 2:
        p.wait("Y", [("dy", C("dy"))])
        p.barrier()
        return _finish(nc, p)

    ar.reset(p2base)
    kbuf = ar.alloc("kbuf", [128, 2, 8, 1024], BF16)
    vbuf = ar.alloc("vbuf", [128, 2, 8, 1024], BF16)
    p_sb = ar.alloc("p_sb", [128, 2, 1024], BF16)
    pt_sb = ar.alloc("pt_sb", [128, 8, 128], BF16)
    hmbuf = ar.alloc("hmbuf", [128, 2, 1024], F32)
    if _DEBUG:
        dbgbuf = ar.alloc("dbgbuf", [128, 1024], F32)

    E("V", lambda e: e.memset(obuf[:], 0.0), inc="v")
    E("V", lambda e: e.memset(l_sb[:], 0.0), inc="v")
    E("V", lambda e: e.memset(nm_sb[:], NM_INIT), inc="v")

    k_all_r = k_all.rearrange("(j c p) n -> p c j n", p=128, c=8)
    v_all_r = v_all.rearrange("(j t p) n -> p j t n", p=128, t=16)

    kt_load = {}
    kt_users = {}
    hm_load = {}
    hm_users = {}

    def load_kt(kt):
        b = kt % 2
        w = [("g", ag_done)]
        if b in kt_users:
            w.append(("t", kt_users[b]))
        for dc in range(8):
            E("Y", lambda e, kt=kt, b=b, dc=dc:
              e.dma_start(out=kbuf[:, b, dc, :].rearrange("p (j n) -> p j n", n=128),
                          in_=k_all_r[:, dc, :, kt * 128:(kt + 1) * 128]),
              waits=(w if dc == 0 else []), inc="dy", inc_n=16)
        kt_load[kt] = E("Y", lambda e, kt=kt, b=b:
                        e.dma_start(out=vbuf[:, b, :, :], in_=v_all_r[:, :, kt, :]),
                        inc="dy", inc_n=16)
        w = [("v", hm_users[b])] if b in hm_users else []
        hm_load[kt] = E("G", lambda e, kt=kt, b=b:
                        e.dma_start(out=hmbuf[:, b, :], in_=d["hmask"][kt]),
                        waits=w, inc="dg", inc_n=16)

    pair_idx = [0]
    exp_of = {}
    nm_of = {}
    prev_evac = [(0, 0)]
    prev_oupd = [0]
    pend = []

    def flash_head(kt, ss):
        n = pair_idx[0]
        pair_idx[0] += 1
        par = n % 2
        b = kt % 2
        S = [ps_a, ps_c][par]
        tsc = None
        for di in range(8):
            if _PSTEP < 1:
                break
            waits = []
            if di == 0:
                waits = [("dy", kt_load[kt]),
                         ("s", exp_of.get(n - 2, 0)), ("v", nm_of.get(n - 2, 0))]
                if ss == NQ - 1:
                    waits.append(("dg", hm_load[kt]))
            for h in range(2):
                last = (di == 7 and h == 1)
                tsc = E("T", lambda e, di=di, h=h, ss=ss, b=b, S=S,
                        start=(di == 0), stop=(di == 7):
                        e.matmul(S[:, h * 512:(h + 1) * 512],
                                 q_fm[:, di, ss * 128:(ss + 1) * 128],
                                 kbuf[:, b, di, h * 512:(h + 1) * 512],
                                 start=start, stop=stop),
                        waits=waits if h == 0 else [],
                        inc="t" if last else None)
                waits = []
        if _PSTEP < 2:
            return
        if ss == kt or ss == NQ - 1:
            mask_fn = ((lambda: dmask_sb[:]) if ss == kt
                       else (lambda b=b: hmbuf[:, b, :]))
            madd = E("V", lambda e, mask_fn=mask_fn, S=S:
                     e.tensor_tensor(S[:], S[:], mask_fn(), ALU.add),
                     waits=[("t", tsc)], inc="v")
            if ss == NQ - 1:
                hm_users[b] = madd
        rslot = A_NMRAW if par == 0 else 16
        rmx = E("V", lambda e, S=S, rslot=rslot:
                e.tensor_reduce(sml[:, rslot:rslot + 1], S[:], AX.X, ALU.max,
                                negate=True), waits=[("t", tsc)], inc="v")
        nm_of[n] = rmx
        if _PSTEP < 3:
            return
        aslot = A_ALPHA0 + par
        lslot = A_LP0 + par
        E("S", lambda e, ss=ss, rslot=rslot:
          e.activation(sml[:, A_T15:A_T15 + 1], sml[:, rslot:rslot + 1],
                       AF.Relu, bias=nm_sb[:, ss:ss + 1], scale=-1.0 / 32.0),
          waits=[("v", rmx)])
        E("S", lambda e, aslot=aslot:
          e.activation(sml[:, aslot:aslot + 1], sml[:, A_T15:A_T15 + 1], AF.Exp,
                       scale=-1.0))
        E("S", lambda e, ss=ss:
          e.activation(sml[:, A_NMNEW:A_NMNEW + 1], sml[:, A_T15:A_T15 + 1],
                       AF.Identity, bias=nm_sb[:, ss:ss + 1], scale=-1.0))
        E("S", lambda e, ss=ss:
          e.copy(nm_sb[:, ss:ss + 1], sml[:, A_NMNEW:A_NMNEW + 1]))
        sexp = E("S", lambda e, lslot=lslot, par=par, S=S:
                 e.activation(p_sb[:, par, :], S[:], AF.Exp,
                              bias=sml[:, A_NMNEW:A_NMNEW + 1],
                              scale=1.0 / 32.0, accum_out=sml[:, lslot:lslot + 1]),
                 inc="s")
        E("S", lambda e, ss=ss, aslot=aslot:
          e.activation(sml[:, A_MDIFF:A_MDIFF + 1], l_sb[:, ss:ss + 1],
                       AF.Identity, scale=sml[:, aslot:aslot + 1]))
        sexp = E("S", lambda e, ss=ss, lslot=lslot:
                 e.activation(l_sb[:, ss:ss + 1], sml[:, A_MDIFF:A_MDIFF + 1],
                              AF.Identity, bias=sml[:, lslot:lslot + 1]),
                 inc="s")
        exp_of[n] = sexp
        if _DEBUG and n == 0:
            dmp = E("Y", lambda e: e.dma_start(out=dbg_sml, in_=dbgbuf[:, 0:16]),
                    waits=[("v", rmx), ("s", sexp)], inc="dy", inc_n=16)
            dmp = E("Y", lambda e: e.dma_start(out=dbg_p, in_=p_sb[:, 0, :]),
                    inc="dy", inc_n=16)
            vdbg = E("V", lambda e: e.tensor_copy(dbgbuf[:], ps_a[:]),
                     waits=[("dy", dmp)], inc="v")
            dmp = E("Y", lambda e: e.dma_start(out=dbg_s, in_=dbgbuf[:]),
                    waits=[("v", vdbg)], inc="dy", inc_n=16)
            p.wait("S", [("dy", dmp)])
            p.wait("V", [("dy", dmp)])

        pend.append((n, kt, ss, par, b))

    def flash_tail():
        if not pend:
            return
        n, kt, ss, par, b = pend.pop(0)
        if _PSTEP < 4:
            return
        aslot = A_ALPHA0 + par
        ev, es = prev_evac[0]
        ttr = None
        for c in range(8):
            w = []
            if c == 0:
                w = [("s", exp_of[n]), ("v", ev), ("s", es)]
            ttr = E("T", lambda e, c=c, par=par:
                    e.transpose(ps_pt[:, c, :], p_sb[:, par, c * 128:(c + 1) * 128],
                                identb_sb[:]),
                    waits=w, inc="t" if c == 7 else None)
        E("V", lambda e: e.tensor_copy(pt_sb[:], ps_pt[:]),
          waits=[("t", ttr)], inc="v")
        prev_evac[0] = (C("v"), 0)
        evac_v, evac_s = C("v"), 0
        if _PSTEP < 5:
            return
        tpv = None
        for kc in range(8):
            waits = []
            if kc == 0:
                waits = [("v", evac_v), ("s", evac_s), ("v", prev_oupd[0])]
            for h in range(2):
                last = (kc == 7 and h == 1)
                tpv = E("T", lambda e, kc=kc, h=h, b=b, start=(kc == 0), stop=(kc == 7):
                        e.matmul(ps_b[:, h * 512:(h + 1) * 512], pt_sb[:, kc, :],
                                 vbuf[:, b, kc, h * 512:(h + 1) * 512],
                                 start=start, stop=stop),
                        waits=waits if h == 0 else [],
                        inc="t" if last else None)
                waits = []
        kt_users[b] = tpv
        if _PSTEP < 6:
            return
        for h in range(2):
            E("V", lambda e, ss=ss, h=h, aslot=aslot:
              e.scalar_tensor_tensor(obuf[:, ss, h * 512:(h + 1) * 512],
                                     obuf[:, ss, h * 512:(h + 1) * 512],
                                     sml[:, aslot:aslot + 1],
                                     ps_b[:, h * 512:(h + 1) * 512],
                                     ALU.mult, ALU.add),
              waits=[("t", tpv)] if h == 0 else [], inc="v")
        prev_oupd[0] = C("v")

    load_kt(0)
    if _DEBUG:
        E("Y", lambda e: e.dma_start(out=dbg_kt, in_=kbuf[:, 0, :, :]),
          waits=[("dy", kt_load[0])], inc="dy", inc_n=16)
        E("Y", lambda e: e.dma_start(out=dbg_vt, in_=vbuf[:, 0, :, :]),
          inc="dy", inc_n=16)
    for kt in range(min(KT, _KT)):
        if kt + 1 < min(KT, _KT):
            while pend:
                flash_tail()
            load_kt(kt + 1)
        for ss in list(range(kt, NSUB)) + [NQ - 1]:
            if pair_idx[0] < _NPAIR:
                flash_head(kt, ss)
                if len(pend) > 1:
                    flash_tail()
    while pend:
        flash_tail()

    p.barrier()
    if _DEBUG:
        for ss in range(NQ):
            E("Y", lambda e, ss=ss: e.dma_start(out=dbg_o[ss], in_=obuf[:, ss, :]),
              inc="dy", inc_n=16)
        E("Y", lambda e: e.dma_start(out=dbg_l, in_=l_sb[:]), inc="dy", inc_n=16)
        E("Y", lambda e: e.dma_start(out=dbg_nm, in_=nm_sb[:]), inc="dy", inc_n=16)
        E("Y", lambda e: e.dma_start(out=dbg_q, in_=q_fm[:]), inc="dy", inc_n=16)
        E("Y", lambda e: e.dma_start(out=dbg_sw, in_=swb[:]), inc="dy", inc_n=16)
        p.wait("Y", [("dy", C("dy"))])
        p.barrier()
    if _PHASES < 3:
        p.wait("Y", [("dy", C("dy"))])
        p.barrier()
        return _finish(nc, p)

    ar.reset(p2base)
    y_tm = ar.alloc("y_tm", [128, GRP, HID], F32)
    xq_fm = ar.alloc("xq_fm", [128, 8, GRP * 128], BF16)
    xqh_fm = ar.alloc("xqh_fm", [128, 8, 128], BF16)
    h_fm = ar.alloc("h_fm", [128, 2, GRP, 130], F32)
    ysil = ar.alloc("ysil", [128, 2, GRP, 128], F32)
    w1q_sb = ar.alloc("w1q_sb", [128, 2, 8, 128], BF16)
    w2q_sb = ar.alloc("w2q_sb", [128, 2, 1024], BF16)
    xtile = ar.alloc("xtile", [128, 2, 1024], F32)
    scr1 = ar.alloc("scr1", [128, 1024], F32)
    scr2 = ar.alloc("scr2", [128, 1024], F32)
    xq_tm = ar.alloc("xq_tm", [128, 1024], BF16)
    yq_tm = ar.alloc("yq_tm", [128, HID], BF16)
    yq_fm0 = ar.alloc("yq_fm0", [128, 32, 128], BF16)
    yq_fm1 = ar.alloc("yq_fm1", [128, 32, 128], BF16)
    outbuf = ar.alloc("outbuf", [128, 1024], F32)

    w1q_r = w1q_scr.rearrange("c p h -> p c h")

    xt_users = {}

    def load_xtile(ss, b):
        w = [("v", xt_users[b])] if b in xt_users else []
        return E("G", lambda e, ss=ss, b=b:
                 e.dma_start(out=xtile[:, b, :],
                             in_=d["x_tm"][ss * 128:(ss + 1) * 128, :]),
                 waits=w, inc="dg", inc_n=16)

    def ln_quant1(ss, b, xld, dest_fn):
        E("V", lambda e, ss=ss:
          e.reciprocal(sml[:, A_LINV:A_LINV + 1], l_sb[:, ss:ss + 1]),
          waits=[("dg", xld)], drain=True)
        E("V", lambda e, ss=ss, b=b:
          e.scalar_tensor_tensor(obuf[:, ss, :], obuf[:, ss, :],
                                 sml[:, A_LINV:A_LINV + 1], xtile[:, b, :],
                                 ALU.mult, ALU.add))
        E("V", lambda e, ss=ss:
          e.tensor_reduce(sml[:, A_T15:A_T15 + 1], obuf[:, ss, :], AX.X, ALU.add), drain=True)
        E("V", lambda e:
          e.tensor_scalar(sml[:, A_NEGMEAN:A_NEGMEAN + 1],
                          sml[:, A_T15:A_T15 + 1], -1.0 / D, None, ALU.mult), drain=True)
        E("V", lambda e, ss=ss:
          e.tensor_scalar(obuf[:, ss, :], obuf[:, ss, :],
                          sml[:, A_NEGMEAN:A_NEGMEAN + 1], None, ALU.add))
        E("V", lambda e, ss=ss:
          e.tensor_tensor(scr2[:], obuf[:, ss, :], obuf[:, ss, :], ALU.mult))
        E("V", lambda e:
          e.tensor_reduce(sml[:, A_SSQ:A_SSQ + 1], scr2[:], AX.X, ALU.add), drain=True)
        vv = E("V", lambda e:
               e.tensor_scalar(sml[:, A_VAR:A_VAR + 1], sml[:, A_SSQ:A_SSQ + 1],
                               1.0 / D, EPS, ALU.mult, ALU.add), inc="v", drain=True)
        sq = E("S", lambda e:
               e.activation(sml[:, A_RSTD:A_RSTD + 1], sml[:, A_VAR:A_VAR + 1],
                            AF.Sqrt), waits=[("v", vv)], inc="s")
        E("V", lambda e:
          e.reciprocal(sml[:, A_T15:A_T15 + 1], sml[:, A_RSTD:A_RSTD + 1]),
          waits=[("s", sq)], drain=True)
        E("V", lambda e, ss=ss:
          e.tensor_scalar(obuf[:, ss, :], obuf[:, ss, :],
                          sml[:, A_T15:A_T15 + 1], None, ALU.mult))
        E("V", lambda e, ss=ss:
          e.tensor_reduce(sml[:, A_AMAX:A_AMAX + 1], obuf[:, ss, :], AX.X, ALU.max,
                          apply_absolute_value=True), drain=True)
        E("V", lambda e:
          e.tensor_scalar(sml[:, A_T15:A_T15 + 1], sml[:, A_AMAX:A_AMAX + 1],
                          1e-5, None, ALU.max), drain=True)
        E("V", lambda e: e.reciprocal(sml[:, A_AMAX:A_AMAX + 1], sml[:, A_T15:A_T15 + 1]), drain=True)
        E("V", lambda e:
          e.tensor_scalar(sml[:, A_SX:A_SX + 1], sml[:, A_AMAX:A_AMAX + 1], 127.0,
                          None, ALU.mult), drain=True)
        E("V", lambda e: e.reciprocal(sml[:, A_SXINV:A_SXINV + 1], sml[:, A_SX:A_SX + 1]), drain=True)
        sxinv_slot = A_SXINV
        if ss == NQ - 1:
            E("V", lambda e:
              e.tensor_scalar(sml[:, A_MDIFF:A_MDIFF + 1], sml[:, A_SXINV:A_SXINV + 1],
                              hvalid_sb[:], None, ALU.mult), drain=True)
            sxinv_slot = A_MDIFF
        E("V", lambda e, ss=ss:
          e.tensor_scalar(scr1[:], obuf[:, ss, :], sml[:, A_SX:A_SX + 1], MAGIC,
                          ALU.mult, ALU.add))
        vq = E("V", lambda e, sxinv_slot=sxinv_slot:
               e.tensor_scalar(xq_tm[:], scr1[:], -MAGIC,
                               sml[:, sxinv_slot:sxinv_slot + 1],
                               ALU.add, ALU.mult), inc="v")
        tt = None
        for c in range(8):
            w = [("v", vq), ("s", C("s"))] if c == 0 else []
            tt = E("T", lambda e, c=c:
                   e.transpose(ps_pt[:, c, :], xq_tm[:, c * 128:(c + 1) * 128],
                               identb_sb[:]),
                   waits=w, inc="t" if c == 7 else None)
        E("V", lambda e, dest_fn=dest_fn: e.tensor_copy(dest_fn(), ps_pt[:]),
          waits=[("t", tt)], inc="v")
        return C("v"), C("s")

    xldh = load_xtile(16, 0)
    hv, hs = ln_quant1(16, 0, xldh, lambda: xqh_fm[:])
    xt_users[0] = hv

    w1ld_user = {}
    hevac_user = {}
    ysil_user = {}
    yevac_user = {}
    w2ld_user = {}
    out_dma = [0]

    for g in range(NGRP):
        s0, s1 = 2 * g, 2 * g + 1
        for sl, ss in enumerate((s0, s1)):
            b = ss % 2
            xld = load_xtile(ss, b)
            ln_quant1(ss, b, xld,
                      lambda sl=sl: xq_fm[:, :, sl * 128:(sl + 1) * 128])
            xt_users[b] = C("v")
        xq_v, xq_s = C("v"), C("s")
        if _DEBUG:
            for sl, ss in enumerate((s0, s1)):
                E("Y", lambda e, ss=ss: e.dma_start(out=dbg_xln[ss], in_=obuf[:, ss, :]),
                  waits=[("v", xq_v)], inc="dy", inc_n=16)
            if g == 0:
                E("Y", lambda e: e.dma_start(out=dbg_xln[16], in_=obuf[:, 16, :]),
                  inc="dy", inc_n=16)

        for hc in range(32):
            wb = hc % 2
            w = [("dy", wq_done)]
            if wb in w1ld_user:
                w.append(("t", w1ld_user[wb]))
            ldw1 = E("Y", lambda e, hc=hc, wb=wb:
                     e.dma_start(out=w1q_sb[:, wb, :, :],
                                 in_=w1q_r[:, :, hc * 128:(hc + 1) * 128]),
                     waits=w, inc="dy", inc_n=16)

            hb = hc % 2
            po = ps_b[:, hb * 512:hb * 512 + 256]
            ph = ps_b[:, hb * 512 + 256:hb * 512 + 384]
            for di in range(8):
                waits = []
                if di == 0:
                    waits = [("dy", ldw1), ("s", xq_s), ("v", xq_v),
                             ("v", hevac_user.get(hb, 0))]
                E("T", lambda e, di=di, wb=wb, po=po,
                  start=(di == 0), stop=(di == 7):
                  e.matmul(po, w1q_sb[:, wb, di, :], xq_fm[:, di, :],
                           start=start, stop=stop),
                  waits=waits)
            tw1 = None
            for di in range(8):
                waits = []
                if g == 0 and hc == 0 and di == 0:
                    waits = [("v", hv), ("s", hs)]
                tw1 = E("T", lambda e, di=di, wb=wb, ph=ph,
                        start=(di == 0), stop=(di == 7):
                        e.matmul(ph, w1q_sb[:, wb, di, :], xqh_fm[:, di, :],
                                 start=start, stop=stop),
                        waits=waits, inc="t" if di == 7 else None)
            w1ld_user[wb] = tw1

            E("V", lambda e, hb=hb, po=po:
              e.tensor_scalar(h_fm[:, hb, :, 1:129],
                              po.rearrange("p (g n) -> p g n", g=GRP),
                              swb[:, 1:2], None, ALU.mult),
              waits=[("t", tw1)])
            for si, (sl, ss) in enumerate(((0, s0), (1, s1))):
                E("V", lambda e, hb=hb, sl=sl, ss=ss, ph=ph:
                  e.tensor_scalar(h_fm[:, hb, sl, 0:1], ph[:, 2 * ss:2 * ss + 1],
                                  swb[:, 1:2], None, ALU.mult))
                hevac_user[hb] = E(
                    "V", lambda e, hb=hb, sl=sl, ss=ss, ph=ph:
                    e.tensor_scalar(h_fm[:, hb, sl, 129:130],
                                    ph[:, 2 * ss + 1:2 * ss + 2],
                                    swb[:, 1:2], None, ALU.mult),
                    inc="v" if si == 1 else None)

            cview = scr2[:, 0:GRP * 128].rearrange("p (g n) -> p g n", g=GRP)
            E("V", lambda e, hb=hb, hc=hc, cview=cview:
              e.tensor_scalar(cview, h_fm[:, hb, :, 0:128], cw_sb[:, hc:hc + 1],
                              None, ALU.mult))
            E("V", lambda e, hb=hb, hc=hc, cview=cview:
              e.scalar_tensor_tensor(cview, h_fm[:, hb, :, 1:129],
                                     cw_sb[:, 32 + hc:33 + hc], cview,
                                     ALU.mult, ALU.add))
            vc3 = E("V", lambda e, hb=hb, hc=hc, cview=cview:
                    e.scalar_tensor_tensor(cview, h_fm[:, hb, :, 2:130],
                                           cw_sb[:, 64 + hc:65 + hc], cview,
                                           ALU.mult, ALU.add), inc="v")
            w = [("v", vc3)]
            if hb in ysil_user:
                w.append(("t", ysil_user[hb]))
            ssil = E("S", lambda e, hb=hb, hc=hc, cview=cview:
                     e.activation(ysil[:, hb, :, :], cview, AF.Silu,
                                  bias=cw_sb[:, 96 + hc:97 + hc]),
                     waits=w, inc="s")
            tty = None
            for sl in range(GRP):
                w = []
                if sl == 0:
                    w = [("s", ssil)]
                    if (hc % 2) in yevac_user:
                        yv, ys = yevac_user[hc % 2]
                        w += [("v", yv), ("s", ys)]
                tty = E("T", lambda e, hb=hb, sl=sl:
                        e.transpose(ps_t2[:, sl, :], ysil[:, hb, sl, :], identf_sb[:]),
                        waits=w, inc="t" if sl == GRP - 1 else None)
            ysil_user[hb] = tty
            E("V", lambda e, hc=hc:
              e.tensor_copy(y_tm[:, :, hc * 128:(hc + 1) * 128], ps_t2[:, 0:GRP, :]),
              waits=[("t", tty)], inc="v")
            yevac_user[hc % 2] = (C("v"), C("s"))
        y_v, y_s = C("v"), C("s")

        for sl, yqf in ((0, yq_fm0), (1, yq_fm1)):
            E("V", lambda e, sl=sl:
              e.tensor_reduce(sml[:, A_AMAX:A_AMAX + 1], y_tm[:, sl, :], AX.X,
                              ALU.max, apply_absolute_value=True),
              waits=[("v", y_v), ("s", y_s)] if sl == 0 else [], drain=True)
            E("V", lambda e:
              e.tensor_scalar(sml[:, A_T15:A_T15 + 1], sml[:, A_AMAX:A_AMAX + 1],
                              1e-5, None, ALU.max), drain=True)
            E("V", lambda e:
              e.reciprocal(sml[:, A_AMAX:A_AMAX + 1], sml[:, A_T15:A_T15 + 1]), drain=True)
            E("V", lambda e:
              e.tensor_scalar(sml[:, A_SX:A_SX + 1], sml[:, A_AMAX:A_AMAX + 1], 127.0,
                              None, ALU.mult), drain=True)
            E("V", lambda e:
              e.reciprocal(sml[:, A_SXINV:A_SXINV + 1], sml[:, A_SX:A_SX + 1]), drain=True)
            vyq = None
            for hh in range(4):
                E("V", lambda e, sl=sl, hh=hh:
                  e.tensor_scalar(scr1[:], y_tm[:, sl, hh * 1024:(hh + 1) * 1024],
                                  sml[:, A_SX:A_SX + 1], MAGIC, ALU.mult, ALU.add))
                vyq = E("V", lambda e, hh=hh:
                        e.tensor_scalar(yq_tm[:, hh * 1024:(hh + 1) * 1024], scr1[:],
                                        -MAGIC, sml[:, A_SXINV:A_SXINV + 1],
                                        ALU.add, ALU.mult),
                        inc="v" if hh == 3 else None)
            for blk in range(4):
                ttq = None
                for c in range(8):
                    cc = blk * 8 + c
                    w = []
                    if c == 0:
                        w = [("v", C("v") if blk else vyq), ("s", C("s"))]
                    ttq = E("T", lambda e, cc=cc, c=c:
                            e.transpose(ps_pt[:, c, :],
                                        yq_tm[:, cc * 128:(cc + 1) * 128],
                                        identb_sb[:]),
                            waits=w, inc="t" if c == 7 else None)
                E("V", lambda e, blk=blk, yqf=yqf:
                  e.tensor_copy(yqf[:, blk * 8:(blk + 1) * 8, :], ps_pt[:]),
                  waits=[("t", ttq)], inc="v")
        yq_v, yq_s = C("v"), C("s")

        for hc in range(32):
            wb = hc % 2
            w = [("dy", wq_done)]
            if wb in w2ld_user:
                w.append(("t", w2ld_user[wb]))
            ldw2 = E("S", lambda e, hc=hc, wb=wb:
                     e.dma_start(out=w2q_sb[:, wb, :], in_=w2q_scr[hc]),
                     waits=w, inc="ds", inc_n=16)
            waits = [("ds", ldw2)]
            if hc == 0:
                waits += [("v", yq_v), ("s", yq_s)]
            lastmm = None
            for sl, yqf, psd in ((0, yq_fm0, ps_a), (1, yq_fm1, ps_b)):
                for h in range(2):
                    last = (sl == 1 and h == 1)
                    lastmm = E("T", lambda e, hc=hc, h=h, wb=wb, yqf=yqf, psd=psd,
                               start=(hc == 0), stop=(hc == 31):
                               e.matmul(psd[:, h * 512:(h + 1) * 512], yqf[:, hc, :],
                                        w2q_sb[:, wb, h * 512:(h + 1) * 512],
                                        start=start, stop=stop),
                               waits=waits, inc="t" if last else None)
                    waits = []
            w2ld_user[wb] = lastmm
        tw2 = C("t")

        for sl, ss, psd in ((0, s0, ps_a), (1, s1, ps_b)):
            for h in range(2):
                E("V", lambda e, ss=ss, psd=psd, h=h:
                  e.scalar_tensor_tensor(scr1[:, h * 512:(h + 1) * 512],
                                         psd[:, h * 512:(h + 1) * 512], swb[:, 3:4],
                                         obuf[:, ss, h * 512:(h + 1) * 512],
                                         ALU.mult, ALU.add),
                  waits=[("t", tw2)] if h == 0 else [])
            E("V", lambda e:
              e.tensor_reduce(sml[:, A_T15:A_T15 + 1], scr1[:], AX.X, ALU.add), drain=True)
            E("V", lambda e:
              e.tensor_scalar(sml[:, A_NEGMEAN:A_NEGMEAN + 1],
                              sml[:, A_T15:A_T15 + 1], -1.0 / D, None, ALU.mult), drain=True)
            E("V", lambda e:
              e.tensor_scalar(scr1[:], scr1[:], sml[:, A_NEGMEAN:A_NEGMEAN + 1],
                              None, ALU.add))
            E("V", lambda e:
              e.tensor_tensor(scr2[:], scr1[:], scr1[:], ALU.mult))
            E("V", lambda e:
              e.tensor_reduce(sml[:, A_SSQ:A_SSQ + 1], scr2[:], AX.X, ALU.add), drain=True)
            vv = E("V", lambda e:
                   e.tensor_scalar(sml[:, A_VAR:A_VAR + 1], sml[:, A_SSQ:A_SSQ + 1],
                                   1.0 / D, EPS, ALU.mult, ALU.add), inc="v", drain=True)
            sq = E("S", lambda e:
                   e.activation(sml[:, A_RSTD:A_RSTD + 1], sml[:, A_VAR:A_VAR + 1],
                                AF.Sqrt), waits=[("v", vv)], inc="s")
            E("V", lambda e:
              e.reciprocal(sml[:, A_T15:A_T15 + 1], sml[:, A_RSTD:A_RSTD + 1]),
              waits=[("s", sq)], drain=True)
            vout = E("V", lambda e:
                     e.tensor_scalar(outbuf[:], scr1[:], sml[:, A_T15:A_T15 + 1],
                                     None, ALU.mult),
                     waits=[("dy", out_dma[0])], inc="v")
            out_dma[0] = E("Y", lambda e, ss=ss:
                           e.dma_start(out=y_out[ss * 128:(ss + 1) * 128, :],
                                       in_=outbuf[:]),
                           waits=[("v", vout)], inc="dy", inc_n=16)

    p.wait("Y", [("dy", C("dy"))])
    p.barrier()
    return _finish(nc, p)


def _finish(nc, p):
    with nc.Block() as block:
        @block.tensor
        def _(t):
            for fn in p.ops["T"]:
                fn(t)

        @block.vector
        def _(v):
            for fn in p.ops["V"]:
                fn(v)

        @block.scalar
        def _(s):
            for fn in p.ops["S"]:
                fn(s)

        @block.gpsimd
        def _(g):
            for fn in p.ops["G"]:
                fn(g)

        @block.sync
        def _(y):
            for fn in p.ops["Y"]:
                fn(y)

    return nc


def _host_inputs(x, Qw, Kw, Vw, w1, conv_w, conv_b, w2):
    x2 = np.ascontiguousarray(np.asarray(x, np.float32).reshape(S, D))
    xT = np.ascontiguousarray(x2.T)
    w1t = np.ascontiguousarray(np.asarray(w1, np.float32).T)
    w2t = np.ascontiguousarray(np.asarray(w2, np.float32).T)
    cwf = np.asarray(conv_w, np.float32).reshape(HID, 3)
    cbf = np.asarray(conv_b, np.float32)
    cw = np.zeros((128, 128), np.float32)
    cw[:, 0:32] = cwf[:, 0].reshape(32, 128).T
    cw[:, 32:64] = cwf[:, 1].reshape(32, 128).T
    cw[:, 64:96] = cwf[:, 2].reshape(32, 128).T
    cw[:, 96:128] = cbf.reshape(32, 128).T
    identb = np.eye(128, dtype=ml_dtypes.bfloat16)
    identf = np.eye(128, dtype=np.float32)
    ones_col = np.ones((128, 1), np.float32)
    ones_row = np.ones((1, 128), np.float32)

    in_maps = []
    for j in range(NCORES):
        own_cols = (1024 * np.arange(NSUB)[:, None] + 128 * j
                    + np.arange(128)[None, :]).reshape(-1)
        halo_pos = np.zeros(32, np.int64)
        halo_valid = np.zeros((128, 1), np.float32)
        for i in range(NSUB):
            lp = 1024 * i + 128 * j - 1
            rp = 1024 * i + 128 * j + 128
            halo_pos[2 * i] = lp if lp >= 0 else 0
            halo_pos[2 * i + 1] = rp if rp < S else 0
            halo_valid[2 * i, 0] = 1.0 if lp >= 0 else 0.0
            halo_valid[2 * i + 1, 0] = 1.0 if rp < S else 0.0
        all_cols = np.concatenate([own_cols, halo_pos, np.zeros(96, np.int64)])
        xT_own = np.ascontiguousarray(xT[:, all_cols])
        x_tm = np.ascontiguousarray(x2[all_cols, :])
        for h in range(32):
            if halo_valid[h, 0] == 0.0:
                xT_own[:, 2048 + h] = 0.0
                x_tm[2048 + h, :] = 0.0
        xT_own[:, 2048 + 32:] = 0.0
        x_tm[2048 + 32:, :] = 0.0

        r = np.arange(128)[:, None]
        c = np.arange(1024)[None, :]
        dmask = np.where(c <= 128 * j + r, 0.0, NEG).astype(np.float32)

        hmask = np.full((KT, 128, 1024), NEG, np.float32)
        for h in range(32):
            if halo_valid[h, 0] == 1.0:
                pos = int(halo_pos[h])
                for kt in range(pos // 1024 + 1):
                    n = min(1024, pos - 1024 * kt + 1)
                    hmask[kt, h, 0:n] = 0.0
            else:
                hmask[0, h, 0] = 0.0
        hmask[0, 32:, 0] = 0.0

        in_maps.append({
            "xT_own": xT_own, "x_tm": x_tm,
            "qw": np.asarray(Qw, np.float32), "kw": np.asarray(Kw, np.float32),
            "vw": np.asarray(Vw, np.float32),
            "w1t": w1t, "w2t": w2t,
            "cw": cw, "dmask": dmask, "hmask": hmask, "hvalid": halo_valid,
            "identb": identb, "identf": identf,
            "ones_col": ones_col, "ones_row": ones_row,
        })
    return in_maps


_NC_CACHE = [None]


def kernel(L0_attn_x, L0_attn_Qw, L0_attn_Kw, L0_attn_Vw,
           L0_ffn_w1, L0_ffn_conv_w, L0_ffn_conv_b, L0_ffn_w2):
    in_maps = _host_inputs(L0_attn_x, L0_attn_Qw, L0_attn_Kw, L0_attn_Vw,
                           L0_ffn_w1, L0_ffn_conv_w, L0_ffn_conv_b, L0_ffn_w2)
    if _NC_CACHE[0] is None:
        _NC_CACHE[0] = build_program()
    nc = _NC_CACHE[0]
    res = run_bass_kernel_spmd(nc, in_maps, core_ids=list(range(NCORES)))
    out = np.empty((1, S, D), np.float32)
    ov = out.reshape(NSUB, NCORES, 128, D)
    for j in range(NCORES):
        ov[:, j] = np.asarray(res.results[j]["y"]).reshape(NSUB, 128, D)
    return out


# version 11
# speedup vs baseline: 1.2948x; 1.1334x over previous
import os
import numpy as np
import ml_dtypes
import concourse.bass as bass
import concourse.mybir as mybir
from concourse.bass_utils import run_bass_kernel_spmd

F32 = mybir.dt.float32
BF16 = mybir.dt.bfloat16
AF = mybir.ActivationFunctionType
ALU = mybir.AluOpType
AX = mybir.AxisListType

NCORES = 8
S = 16384
D = 1024
HID = 4096
NSUB = 16
NQ = NSUB + 1
KT = 16
MAGIC = 12582912.0
NM_INIT = 100.0
NEG = -1e9
EPS = 1e-5
GRP = 2
NGRP = NSUB // GRP

ENGINES = ("T", "V", "S", "G", "Y")


class Plan:

    def __init__(self, nc):
        self.nc = nc
        self.ops = {e: [] for e in ENGINES}
        self.sems = {}
        self.cnt = {}
        self.bar_round = 0
        self.all_waits = []
        self.meta = {e: [] for e in ENGINES}

    def sem(self, name):
        if name not in self.sems:
            self.sems[name] = self.nc.alloc_semaphore(name)
            self.cnt[name] = 0
        return self.sems[name]

    def emit(self, e, fn, waits=(), inc=None, inc_n=1, drain=False):
        sems = self.sems
        for s_, v_ in waits:
            if v_ > 0:
                self.all_waits.append((e, s_, v_, len(self.ops[e])))
        self.meta[e].append((tuple((s_, v_) for (s_, v_) in waits if v_ > 0),
                             inc, inc_n))
        wl = tuple((sems[s], v) for (s, v) in waits if v > 0)

        def run(eng, fn=fn, wl=wl, inc=inc, inc_n=inc_n, drain=drain):
            for s, v in wl:
                eng.wait_ge(s, v)
            r = fn(eng)
            if drain:
                eng.drain()
            if inc is not None:
                r.then_inc(sems[inc], inc_n)

        self.ops[e].append(run)
        if inc is not None:
            self.cnt[inc] += inc_n
            return self.cnt[inc]
        return None

    def wait(self, e, waits):
        sems = self.sems
        for s_, v_ in waits:
            if v_ > 0:
                self.all_waits.append((e, s_, v_, len(self.ops[e])))
        self.meta[e].append((tuple((s_, v_) for (s_, v_) in waits if v_ > 0),
                             None, 0))
        wl = tuple((sems[s], v) for (s, v) in waits if v > 0)

        def run(eng, wl=wl):
            for s, v in wl:
                eng.wait_ge(s, v)

        self.ops[e].append(run)

    def c(self, name):
        return self.cnt[name]

    def barrier(self):
        bar = self.sem("bar")
        own_dma = {"Y": "dy", "G": "dg", "S": "ds"}
        self.bar_round += 1
        target = len(ENGINES) * self.bar_round
        for e in ENGINES:
            waits = []
            if e in own_dma:
                waits.append((own_dma[e], self.cnt[own_dma[e]]))
            def run_b(eng, bar=self.sems["bar"], waits=tuple(
                    (self.sems[s], v) for (s, v) in waits if v > 0)):
                for s_, v_ in waits:
                    eng.wait_ge(s_, v_)
                eng.drain()
                eng.sem_inc(bar, 1)
            self.ops[e].append(run_b)
            self.meta[e].append((tuple((s_, v_) for (s_, v_) in waits if v_ > 0),
                                 "bar", 1))
            for s_, v_ in waits:
                if v_ > 0:
                    self.all_waits.append((e, s_, v_, len(self.ops[e]) - 1))
            self.cnt["bar"] += 1
        for e in ENGINES:
            self.wait(e, [("bar", target)])


class Arena:
    def __init__(self, nc, base, limit):
        self.nc = nc
        self.base = base
        self.limit = limit
        self.cur = base

    def alloc(self, name, shape, dtype, align=64):
        per_part = int(np.prod(shape[1:])) * mybir.dt.size(dtype)
        off = (self.cur + align - 1) // align * align
        assert off + per_part <= self.limit, (
            f"arena overflow: {name} at {off}+{per_part} > {self.limit}")
        self.cur = off + per_part
        return self.nc.alloc_sbuf_tensor_at(name, shape, dtype, offset=off)

    def mark(self):
        return self.cur

    def reset(self, to):
        self.cur = to



def _copy(e, dst, src):
    if hasattr(e, "tensor_copy"):
        e_fn = e.tensor_copy
    else:
        e_fn = e.copy
    return e_fn(dst, src)

_PHASES = int(os.environ.get("MK_PHASES", "4"))
_SKIP_AG = os.environ.get("MK_SKIP_AG", "0") == "1"
_P1B = int(os.environ.get("MK_P1B", "5"))
_KT = int(os.environ.get("MK_KT", "16"))
_NPAIR = int(os.environ.get("MK_NPAIR", "9999"))
_PSTEP = int(os.environ.get("MK_PSTEP", "6"))
_NGRP_LIM = int(os.environ.get("MK_NGRP", "99"))
_P4STOP = int(os.environ.get("MK_P4STOP", "9"))
_SEV = os.environ.get("MK_SEV", "1") == "1"
_DEBUG = os.environ.get("MK_DEBUG", "0") == "1"


def build_program():
    nc = bass.Bass("TRN2", target_bir_lowering=False)

    d = {}
    d["xT_own"] = nc.dram_tensor("xT_own", [D, NQ * 128], F32, kind="ExternalInput").ap()
    d["x_tm"] = nc.dram_tensor("x_tm", [NQ * 128, D], F32, kind="ExternalInput").ap()
    d["qw"] = nc.dram_tensor("qw", [D, D], F32, kind="ExternalInput").ap()
    d["kw"] = nc.dram_tensor("kw", [D, D], F32, kind="ExternalInput").ap()
    d["vw"] = nc.dram_tensor("vw", [D, D], F32, kind="ExternalInput").ap()
    d["w1t"] = nc.dram_tensor("w1t", [D, HID], F32, kind="ExternalInput").ap()
    d["w2t"] = nc.dram_tensor("w2t", [HID, D], F32, kind="ExternalInput").ap()
    d["cw"] = nc.dram_tensor("cw", [128, 128], F32, kind="ExternalInput").ap()
    d["dmask"] = nc.dram_tensor("dmask", [128, 1024], F32, kind="ExternalInput").ap()
    d["hmask"] = nc.dram_tensor("hmask", [KT, 128, 1024], F32, kind="ExternalInput").ap()
    d["hvalid"] = nc.dram_tensor("hvalid", [128, 1], F32, kind="ExternalInput").ap()
    d["identb"] = nc.dram_tensor("identb", [128, 128], BF16, kind="ExternalInput").ap()
    d["identf"] = nc.dram_tensor("identf", [128, 128], F32, kind="ExternalInput").ap()
    d["ones_col"] = nc.dram_tensor("ones_col", [128, 1], F32, kind="ExternalInput").ap()
    d["ones_row"] = nc.dram_tensor("ones_row", [1, 128], F32, kind="ExternalInput").ap()
    y_out = nc.dram_tensor("y", [NSUB * 128, D], F32, kind="ExternalOutput").ap()
    if _DEBUG:
        dbg_o = nc.dram_tensor("dbg_o", [NQ, 128, D], F32, kind="ExternalOutput").ap()
        dbg_l = nc.dram_tensor("dbg_l", [128, NQ], F32, kind="ExternalOutput").ap()
        dbg_nm = nc.dram_tensor("dbg_nm", [128, NQ], F32, kind="ExternalOutput").ap()
        dbg_q = nc.dram_tensor("dbg_q", [128, 8, NQ * 128], BF16, kind="ExternalOutput").ap()
        dbg_sw = nc.dram_tensor("dbg_sw", [128, 4], F32, kind="ExternalOutput").ap()
        dbg_xln = nc.dram_tensor("dbg_xln", [NQ, 128, D], F32, kind="ExternalOutput").ap()
        dbg_kt = nc.dram_tensor("dbg_kt", [128, 8, 1024], BF16, kind="ExternalOutput").ap()
        dbg_vt = nc.dram_tensor("dbg_vt", [128, 8, 1024], BF16, kind="ExternalOutput").ap()
        dbg_p = nc.dram_tensor("dbg_p", [128, 1024], BF16, kind="ExternalOutput").ap()
        dbg_ytm = nc.dram_tensor("dbg_ytm", [128, GRP, HID], F32, kind="ExternalOutput").ap()
        dbg_yq = nc.dram_tensor("dbg_yq", [128, 2, HID], BF16, kind="ExternalOutput").ap()
        dbg_s = nc.dram_tensor("dbg_s", [128, 1024], F32, kind="ExternalOutput").ap()
        dbg_sml = nc.dram_tensor("dbg_sml", [128, 16], F32, kind="ExternalOutput").ap()

    k_own = nc.dram_tensor("k_own", [D, 2048], BF16, kind="Internal").ap()
    v_own = nc.dram_tensor("v_own", [2048, D], BF16, kind="Internal").ap()
    k_all = nc.dram_tensor("k_all", [NCORES * D, 2048], BF16, kind="Internal").ap()
    v_all = nc.dram_tensor("v_all", [NCORES * 2048, D], BF16, kind="Internal").ap()
    w1q_scr = nc.dram_tensor("w1q_scr", [8, 128, HID], BF16, kind="Internal").ap()
    w2q_scr = nc.dram_tensor("w2q_scr", [32, 128, D], BF16, kind="Internal").ap()

    q_fm = nc.alloc_sbuf_tensor("q_fm", [128, 8, NQ * 128], BF16)
    dmask_sb = nc.alloc_sbuf_tensor("dmask_sb", [128, 1024], F32)
    cw_sb = nc.alloc_sbuf_tensor("cw_sb", [128, 128], F32)
    identb_sb = nc.alloc_sbuf_tensor("identb_sb", [128, 128], BF16)
    identf_sb = nc.alloc_sbuf_tensor("identf_sb", [128, 128], F32)
    hvalid_sb = nc.alloc_sbuf_tensor("hvalid_sb", [128, 1], F32)
    ones_col_sb = nc.alloc_sbuf_tensor("ones_col_sb", [128, 1], F32)
    ones_row_sb = nc.alloc_sbuf_tensor("ones_row_sb", [1, 128], F32)
    swb = nc.alloc_sbuf_tensor("swb", [128, 4], F32)
    nm_sb = nc.alloc_sbuf_tensor("nm_sb", [128, NQ], F32)
    l_sb = nc.alloc_sbuf_tensor("l_sb", [128, NQ], F32)
    sml = nc.alloc_sbuf_tensor("sml", [128, 20], F32)
    (A_ALPHA0, A_ALPHA1, A_LP0, A_LP1, A_NMRAW, A_NMNEW, A_MDIFF,
     A_LINV, A_NEGMEAN, A_SSQ, A_VAR, A_RSTD, A_AMAX, A_SX, A_SXINV, A_T15) = range(16)
    w4 = nc.alloc_sbuf_tensor("w4", [1, 8], F32)
    w4b = nc.alloc_sbuf_tensor("w4b", [1, 4], BF16)
    ones_col_bf = nc.alloc_sbuf_tensor("ones_col_bf", [128, 1], BF16)
    ones_row_bf = nc.alloc_sbuf_tensor("ones_row_bf", [1, 128], BF16)

    base = nc.lookup_mloc(w4).addr + 64
    base = (base + 63) // 64 * 64
    ar = Arena(nc, base, 229376)

    obuf = ar.alloc("obuf", [128, NQ, 1024], F32)
    p2base = ar.mark()

    ps_a = nc.alloc_psum_tensor("ps_a", [128, 1024], F32)
    ps_b = nc.alloc_psum_tensor("ps_b", [128, 1024], F32)
    ps_c = nc.alloc_psum_tensor("ps_c", [128, 1024], F32)
    ps_pt = nc.alloc_psum_tensor("ps_pt", [128, 8, 128], BF16)
    ps_d = nc.alloc_psum_tensor("ps_d", [128, 512], F32)
    ps_t2 = ps_d.ap().rearrange("p (a n) -> p a n", n=128)
    ps_sm = ps_d

    p = Plan(nc)
    for s in ("dy", "dg", "ds", "t", "v", "s", "bar", "g"):
        p.sem(s)
    E = p.emit
    C = p.c

    for nm_, dst in (("dmask", dmask_sb), ("cw", cw_sb), ("identb", identb_sb),
                     ("identf", identf_sb), ("hvalid", hvalid_sb),
                     ("ones_col", ones_col_sb), ("ones_row", ones_row_sb)):
        E("S", lambda e, nm_=nm_, dst=dst: e.dma_start(out=dst[:], in_=d[nm_]),
          inc="ds", inc_n=16)
    smalls_done = C("ds")
    E("G", lambda e: e.dma_start(out=ones_col_bf[:], in_=d["ones_col"]),
      inc="dg", inc_n=16)
    E("G", lambda e: e.dma_start(out=ones_row_bf[:], in_=d["ones_row"]),
      inc="dg", inc_n=16)
    p.wait("T", [("dg", C("dg"))])

    ar.reset(p2base)
    wtile = ar.alloc("wtile", [128, 2, 4096], F32)
    wqtile = ar.alloc("wqtile", [128, 2, 4096], BF16)
    wacc = ar.alloc("wacc", [128, 2], F32)
    wpart = ar.alloc("wpart", [128, 40], F32)
    whl = ar.alloc("whl", [128, 4], BF16)
    whl32 = ar.alloc("whl32", [128, 2], F32)
    whl32b = ar.alloc("whl32b", [128, 2], F32)

    wt_r1 = d["w1t"].rearrange("(c p) h -> c p h", p=128)
    wt_r2 = d["w2t"].rearrange("(c p) h -> c p h", p=128)

    wtile_user = {}
    wq_store = {}

    def w_pass(which, quantize):
        ntile, src, cols = ((8, wt_r1, 4096) if which == 0 else (32, wt_r2, 1024))
        scr = w1q_scr if which == 0 else w2q_scr
        swcol = 0 if which == 0 else 2
        for i in range(ntile):
            b = i % 2
            w = []
            if b in wtile_user:
                w.append(("v", wtile_user[b]))
            ld = E("Y", lambda e, i=i, b=b, src=src, cols=cols:
                   e.dma_start(out=wtile[:, b, 0:cols], in_=src[i]),
                   waits=w, inc="dy", inc_n=16)
            if not quantize:
                col = i if which == 0 else 8 + i
                E("V", lambda e, b=b, cols=cols:
                  e.tensor_reduce(sml[:, 8:8 + cols // 1024],
                                  wtile[:, b, 0:cols].rearrange("p (a q) -> p a q", q=1024),
                                  AX.X, ALU.add, apply_absolute_value=True),
                  waits=[("dy", ld)], drain=True)
                wtile_user[b] = E("V", lambda e, cols=cols, col=col:
                                  e.tensor_reduce(wpart[:, col:col + 1],
                                                  sml[:, 8:8 + cols // 1024],
                                                  AX.X, ALU.add),
                                  inc="v", drain=True)
            else:
                w2_ = [("dy", ld)]
                if b in wq_store:
                    w2_.append(("dy", wq_store[b]))
                E("V", lambda e, b=b, cols=cols, swcol=swcol:
                  e.tensor_scalar(wtile[:, b, 0:cols], wtile[:, b, 0:cols],
                                  swb[:, swcol:swcol + 1], MAGIC, ALU.mult, ALU.add),
                  waits=w2_)
                E("V", lambda e, b=b, cols=cols:
                  e.tensor_scalar(wtile[:, b, 0:cols], wtile[:, b, 0:cols],
                                  -MAGIC, 1.0, ALU.add, ALU.min))
                vq = E("V", lambda e, b=b, cols=cols:
                       e.tensor_scalar(wqtile[:, b, 0:cols], wtile[:, b, 0:cols],
                                       -1.0, None, ALU.max), inc="v")
                wtile_user[b] = vq
                wq_store[b] = E("Y", lambda e, i=i, b=b, cols=cols, scr=scr:
                                e.dma_start(out=scr[i], in_=wqtile[:, b, 0:cols]),
                                waits=[("v", vq)], inc="dy", inc_n=16)

    w_pass(0, False)
    w_pass(1, False)

    E("V", lambda e: e.tensor_reduce(wacc[:, 0:1], wpart[:, 0:8], AX.X, ALU.add), drain=True)
    E("V", lambda e: e.tensor_reduce(wacc[:, 1:2], wpart[:, 8:40], AX.X, ALU.add), drain=True)
    E("V", lambda e: e.tensor_copy(whl[:, 0:2], wacc[:]), drain=True)
    E("V", lambda e: e.tensor_copy(whl32[:], whl[:, 0:2]), drain=True)
    E("V", lambda e: e.tensor_sub(whl32b[:], wacc[:], whl32[:]), drain=True)
    vhl = E("V", lambda e: e.tensor_copy(whl[:, 2:4], whl32b[:]), inc="v", drain=True)
    E("T", lambda e: e.matmul(ps_sm.ap()[0:1, 384:386], ones_col_bf[:], whl[:, 0:2],
                              start=True, stop=False),
      waits=[("v", vhl), ("ds", smalls_done)])
    E("T", lambda e: e.matmul(ps_sm.ap()[0:1, 384:386], ones_col_bf[:], whl[:, 2:4],
                              start=False, stop=True), inc="t")
    vt2 = E("V", lambda e: e.tensor_copy(w4[0:1, 0:2], ps_sm.ap()[0:1, 384:386]),
            waits=[("t", C("t"))], inc="v", drain=True)
    E("V", lambda e: e.tensor_copy(w4b[0:1, 0:2], w4[0:1, 0:2]), drain=True)
    E("V", lambda e: e.tensor_copy(w4[0:1, 4:6], w4b[0:1, 0:2]), drain=True)
    E("V", lambda e: e.tensor_sub(w4[0:1, 6:8], w4[0:1, 0:2], w4[0:1, 4:6]), drain=True)
    vt2 = E("V", lambda e: e.tensor_copy(w4b[0:1, 2:4], w4[0:1, 6:8]), inc="v", drain=True)
    E("T", lambda e: e.matmul(ps_sm.ap()[:, 386:388], ones_row_bf[:], w4b[0:1, 0:2],
                              start=True, stop=False), waits=[("v", vt2)])
    E("T", lambda e: e.matmul(ps_sm.ap()[:, 386:388], ones_row_bf[:], w4b[0:1, 2:4],
                              start=False, stop=True), inc="t")
    E("V", lambda e: e.tensor_scalar(swb[:, 1:2], ps_sm.ap()[:, 386:387], 1.0 / (HID * D),
                                     None, ALU.mult), waits=[("t", C("t"))], drain=True)
    E("V", lambda e: e.tensor_scalar(swb[:, 3:4], ps_sm.ap()[:, 387:388], 1.0 / (HID * D),
                                     None, ALU.mult), drain=True)
    E("V", lambda e: e.reciprocal(swb[:, 0:1], swb[:, 1:2]), drain=True)
    E("V", lambda e: e.reciprocal(swb[:, 2:3], swb[:, 3:4]), inc="v", drain=True)

    w_pass(0, True)
    w_pass(1, True)
    wq_done = C("dy")

    p.barrier()
    if _PHASES < 1:
        p.wait("Y", [("dy", C("dy"))])
        p.barrier()
        return _finish(nc, p)

    ar.reset(p2base)
    xfm = ar.alloc("xfm", [128, 2, 8, 1024], BF16)
    wq_sb = ar.alloc("wq_sb", [128, 8, 1024], BF16)
    wk_sb = ar.alloc("wk_sb", [128, 8, 1024], BF16)
    wv_sb = ar.alloc("wv_sb", [128, 8, 1024], BF16)
    kvstage = ar.alloc("kvstage", [128, 4, 1024], BF16)

    for nm_, dst in (("qw", wq_sb), ("kw", wk_sb), ("vw", wv_sb)):
        E("G", lambda e, nm_=nm_, dst=dst:
          e.dma_start(out=dst[:], in_=d[nm_].rearrange("(c p) n -> p c n", p=128)),
          inc="dg", inc_n=16)
    w_qkv_loaded = C("dg")

    xT_r = d["xT_own"].rearrange("(c p) n -> p c n", p=128)
    k_own_r = k_own.rearrange("(c p) n -> c p n", p=128)
    v_own_r = v_own.rearrange("(t p) n -> t p n", p=128)

    stage_rot = [0]
    stage_user = {}

    def evac_store(psum, qcols, waits_t, store_fn):
        sl = stage_rot[0] % 4
        stage_rot[0] += 1
        w = list(waits_t)
        if sl in stage_user:
            w.append(("dy", stage_user[sl]))
        vc = E("V", lambda e, sl=sl, qcols=qcols, psum=psum:
               e.tensor_copy(kvstage[:, sl, 0:qcols], psum[:, 0:qcols]),
               waits=w, inc="v")
        dy = E("Y", lambda e, sl=sl, qcols=qcols, store_fn=store_fn:
               store_fn(e, kvstage[:, sl, 0:qcols]),
               waits=[("v", vc)], inc="dy", inc_n=16)
        stage_user[sl] = dy

    xfm_user = {}

    chunks = [(0, 1024, 0), (1024, 1024, 1), (2048, 128, 0)]
    if _P1B < 2:
        chunks = []
    for c0, cn, b in chunks:
        w = [("t", xfm_user[b])] if b in xfm_user else []
        ld = E("G", lambda e, b=b, c0=c0, cn=cn:
               e.dma_start(out=xfm[:, b, :, 0:cn], in_=xT_r[:, :, c0:c0 + cn]),
               waits=w, inc="dg", inc_n=16)
        for do in range(8):
            psum = [ps_a, ps_b][do % 2]
            hs = list(range(0, cn, 512))
            tC = None
            for h in hs:
                hn = min(512, cn - h)
                for di in range(8):
                    waits = []
                    if di == 0 and h == 0:
                        waits = [("dg", max(ld, w_qkv_loaded)), ("v", C("v"))]
                    last = (h == hs[-1] and di == 7)
                    tC = E("T", lambda e, psum=psum, h=h, hn=hn, di=di, do=do, b=b,
                           start=(di == 0), stop=(di == 7):
                           e.matmul(psum[:, h:h + hn],
                                    wq_sb[:, di, do * 128:(do + 1) * 128],
                                    xfm[:, b, di, h:h + hn], start=start, stop=stop),
                           waits=waits, inc="t" if last else None)
            xfm_user[b] = tC
            E("V", lambda e, psum=psum, do=do, c0=c0, cn=cn:
              e.tensor_copy(q_fm[:, do, c0:c0 + cn], psum[:, 0:cn]),
              waits=[("t", tC)], inc="v")

    for ci in range(2 if _P1B >= 3 else 0):
        c0 = ci * 1024
        b = ci % 2
        ld = E("G", lambda e, b=b, c0=c0:
               e.dma_start(out=xfm[:, b, :, :], in_=xT_r[:, :, c0:c0 + 1024]),
               waits=[("t", xfm_user[b])], inc="dg", inc_n=16)
        for do in range(8):
            psum = [ps_a, ps_b][do % 2]
            tC = None
            for h in range(0, 1024, 512):
                for di in range(8):
                    waits = [("dg", ld), ("v", C("v"))] if (di == 0 and h == 0) else []
                    last = (h == 512 and di == 7)
                    tC = E("T", lambda e, psum=psum, h=h, di=di, do=do, b=b,
                           start=(di == 0), stop=(di == 7):
                           e.matmul(psum[:, h:h + 512],
                                    wk_sb[:, di, do * 128:(do + 1) * 128],
                                    xfm[:, b, di, h:h + 512], start=start, stop=stop),
                           waits=waits, inc="t" if last else None)
            evac_store(psum, 1024, [("t", tC)],
                       lambda e, src, do=do, c0=c0:
                       e.dma_start(out=k_own_r[do][:, c0:c0 + 1024], in_=src))
        for tc in range(8 if _P1B >= 4 else 0):
            psum = [ps_a, ps_b][tc % 2]
            tC = None
            for h in range(0, 1024, 512):
                for di in range(8):
                    waits = [("v", C("v"))] if (di == 0 and h == 0) else []
                    last = (h == 512 and di == 7)
                    tC = E("T", lambda e, psum=psum, h=h, di=di, tc=tc, b=b,
                           start=(di == 0), stop=(di == 7):
                           e.matmul(psum[:, h:h + 512],
                                    xfm[:, b, di, tc * 128:(tc + 1) * 128],
                                    wv_sb[:, di, h:h + 512], start=start, stop=stop),
                           waits=waits, inc="t" if last else None)
            xfm_user[b] = tC
            evac_store(psum, 1024, [("t", tC)],
                       lambda e, src, ci=ci, tc=tc:
                       e.dma_start(out=v_own_r[ci * 8 + tc], in_=src))

    if _P1B < 5:
        ag_done = 0
    elif _SKIP_AG:
        E("G", lambda e: e.dma_start(out=k_all[0:D, :], in_=k_own),
          waits=[("dy", C("dy"))], inc="g", inc_n=16)
        E("G", lambda e: e.dma_start(out=v_all[0:2048, :], in_=v_own),
          inc="g", inc_n=16)
        ag_done = C("g")
    else:
        E("G", lambda e: e.collective_compute(
            "AllGather", ALU.bypass, replica_groups=[list(range(NCORES))],
            ins=[k_own], outs=[k_all]),
          waits=[("dy", C("dy"))], inc="g")
        E("G", lambda e: e.collective_compute(
            "AllGather", ALU.bypass, replica_groups=[list(range(NCORES))],
            ins=[v_own], outs=[v_all]), inc="g")
        ag_done = C("g")

    p.barrier()
    if _PHASES < 2:
        p.wait("Y", [("dy", C("dy"))])
        p.barrier()
        return _finish(nc, p)

    ar.reset(p2base)
    kbuf = ar.alloc("kbuf", [128, 2, 8, 1024], BF16)
    vbuf = ar.alloc("vbuf", [128, 2, 8, 1024], BF16)
    p_sb = ar.alloc("p_sb", [128, 2, 1024], BF16)
    pt_sb = ar.alloc("pt_sb", [128, 8, 128], BF16)
    hmbuf = ar.alloc("hmbuf", [128, 2, 1024], F32)
    if _DEBUG:
        dbgbuf = ar.alloc("dbgbuf", [128, 1024], F32)

    E("V", lambda e: e.memset(obuf[:], 0.0), inc="v")
    E("V", lambda e: e.memset(l_sb[:], 0.0), inc="v")
    E("V", lambda e: e.memset(nm_sb[:], NM_INIT), inc="v")

    k_all_r = k_all.rearrange("(j c p) n -> p c j n", p=128, c=8)
    v_all_r = v_all.rearrange("(j t p) n -> p j t n", p=128, t=16)

    kt_load = {}
    kt_users = {}
    hm_load = {}
    hm_users = {}

    def load_kt(kt):
        b = kt % 2
        w = [("g", ag_done)]
        if b in kt_users:
            w.append(("t", kt_users[b]))
        for dc in range(8):
            E("Y", lambda e, kt=kt, b=b, dc=dc:
              e.dma_start(out=kbuf[:, b, dc, :].rearrange("p (j n) -> p j n", n=128),
                          in_=k_all_r[:, dc, :, kt * 128:(kt + 1) * 128]),
              waits=(w if dc == 0 else []), inc="dy", inc_n=16)
        kt_load[kt] = E("Y", lambda e, kt=kt, b=b:
                        e.dma_start(out=vbuf[:, b, :, :], in_=v_all_r[:, :, kt, :]),
                        inc="dy", inc_n=16)
        w = [("v", hm_users[b])] if b in hm_users else []
        hm_load[kt] = E("G", lambda e, kt=kt, b=b:
                        e.dma_start(out=hmbuf[:, b, :], in_=d["hmask"][kt]),
                        waits=w, inc="dg", inc_n=16)

    pair_idx = [0]
    exp_of = {}
    nm_of = {}
    prev_evac = [(0, 0)]
    prev_oupd = [0]
    pend = []

    def flash_head(kt, ss):
        n = pair_idx[0]
        pair_idx[0] += 1
        par = n % 2
        b = kt % 2
        S = [ps_a, ps_c][par]
        tsc = None
        for di in range(8):
            if _PSTEP < 1:
                break
            waits = []
            if di == 0:
                waits = [("dy", kt_load[kt]),
                         ("s", exp_of.get(n - 2, 0)), ("v", nm_of.get(n - 2, 0))]
                if ss == NQ - 1:
                    waits.append(("dg", hm_load[kt]))
            for h in range(2):
                last = (di == 7 and h == 1)
                tsc = E("T", lambda e, di=di, h=h, ss=ss, b=b, S=S,
                        start=(di == 0), stop=(di == 7):
                        e.matmul(S[:, h * 512:(h + 1) * 512],
                                 q_fm[:, di, ss * 128:(ss + 1) * 128],
                                 kbuf[:, b, di, h * 512:(h + 1) * 512],
                                 start=start, stop=stop),
                        waits=waits if h == 0 else [],
                        inc="t" if last else None)
                waits = []
        if _PSTEP < 2:
            return
        if ss == kt or ss == NQ - 1:
            mask_fn = ((lambda: dmask_sb[:]) if ss == kt
                       else (lambda b=b: hmbuf[:, b, :]))
            madd = E("V", lambda e, mask_fn=mask_fn, S=S:
                     e.tensor_tensor(S[:], S[:], mask_fn(), ALU.add),
                     waits=[("t", tsc)], inc="v")
            if ss == NQ - 1:
                hm_users[b] = madd
        rslot = A_NMRAW if par == 0 else 16
        rmx = E("V", lambda e, S=S, rslot=rslot:
                e.tensor_reduce(sml[:, rslot:rslot + 1], S[:], AX.X, ALU.max,
                                negate=True), waits=[("t", tsc)], inc="v")
        nm_of[n] = rmx
        if _PSTEP < 3:
            return
        aslot = A_ALPHA0 + par
        lslot = A_LP0 + par
        E("S", lambda e, ss=ss, rslot=rslot:
          e.activation(sml[:, A_T15:A_T15 + 1], sml[:, rslot:rslot + 1],
                       AF.Relu, bias=nm_sb[:, ss:ss + 1], scale=-1.0 / 32.0),
          waits=[("v", rmx)])
        E("S", lambda e, aslot=aslot:
          e.activation(sml[:, aslot:aslot + 1], sml[:, A_T15:A_T15 + 1], AF.Exp,
                       scale=-1.0))
        E("S", lambda e, ss=ss:
          e.activation(sml[:, A_NMNEW:A_NMNEW + 1], sml[:, A_T15:A_T15 + 1],
                       AF.Identity, bias=nm_sb[:, ss:ss + 1], scale=-1.0))
        E("S", lambda e, ss=ss:
          e.copy(nm_sb[:, ss:ss + 1], sml[:, A_NMNEW:A_NMNEW + 1]))
        sexp = E("S", lambda e, lslot=lslot, par=par, S=S:
                 e.activation(p_sb[:, par, :], S[:], AF.Exp,
                              bias=sml[:, A_NMNEW:A_NMNEW + 1],
                              scale=1.0 / 32.0, accum_out=sml[:, lslot:lslot + 1]),
                 inc="s")
        E("S", lambda e, ss=ss, aslot=aslot:
          e.activation(sml[:, A_MDIFF:A_MDIFF + 1], l_sb[:, ss:ss + 1],
                       AF.Identity, scale=sml[:, aslot:aslot + 1]))
        sexp = E("S", lambda e, ss=ss, lslot=lslot:
                 e.activation(l_sb[:, ss:ss + 1], sml[:, A_MDIFF:A_MDIFF + 1],
                              AF.Identity, bias=sml[:, lslot:lslot + 1]),
                 inc="s")
        exp_of[n] = sexp
        if _DEBUG and n == 0:
            dmp = E("Y", lambda e: e.dma_start(out=dbg_sml, in_=dbgbuf[:, 0:16]),
                    waits=[("v", rmx), ("s", sexp)], inc="dy", inc_n=16)
            dmp = E("Y", lambda e: e.dma_start(out=dbg_p, in_=p_sb[:, 0, :]),
                    inc="dy", inc_n=16)
            vdbg = E("V", lambda e: e.tensor_copy(dbgbuf[:], ps_a[:]),
                     waits=[("dy", dmp)], inc="v")
            dmp = E("Y", lambda e: e.dma_start(out=dbg_s, in_=dbgbuf[:]),
                    waits=[("v", vdbg)], inc="dy", inc_n=16)
            p.wait("S", [("dy", dmp)])
            p.wait("V", [("dy", dmp)])

        pend.append((n, kt, ss, par, b))

    def flash_tail():
        if not pend:
            return
        n, kt, ss, par, b = pend.pop(0)
        if _PSTEP < 4:
            return
        aslot = A_ALPHA0 + par
        ev, es = prev_evac[0]
        ttr = None
        for c in range(8):
            w = []
            if c == 0:
                w = [("s", exp_of[n]), ("v", ev), ("s", es)]
            ttr = E("T", lambda e, c=c, par=par:
                    e.transpose(ps_pt[:, c, :], p_sb[:, par, c * 128:(c + 1) * 128],
                                identb_sb[:]),
                    waits=w, inc="t" if c == 7 else None)
        E("V", lambda e: e.tensor_copy(pt_sb[:], ps_pt[:]),
          waits=[("t", ttr)], inc="v")
        prev_evac[0] = (C("v"), 0)
        evac_v, evac_s = C("v"), 0
        if _PSTEP < 5:
            return
        tpv = None
        for kc in range(8):
            waits = []
            if kc == 0:
                waits = [("v", evac_v), ("s", evac_s), ("v", prev_oupd[0])]
            for h in range(2):
                last = (kc == 7 and h == 1)
                tpv = E("T", lambda e, kc=kc, h=h, b=b, start=(kc == 0), stop=(kc == 7):
                        e.matmul(ps_b[:, h * 512:(h + 1) * 512], pt_sb[:, kc, :],
                                 vbuf[:, b, kc, h * 512:(h + 1) * 512],
                                 start=start, stop=stop),
                        waits=waits if h == 0 else [],
                        inc="t" if last else None)
                waits = []
        kt_users[b] = tpv
        if _PSTEP < 6:
            return
        for h in range(2):
            E("V", lambda e, ss=ss, h=h, aslot=aslot:
              e.scalar_tensor_tensor(obuf[:, ss, h * 512:(h + 1) * 512],
                                     obuf[:, ss, h * 512:(h + 1) * 512],
                                     sml[:, aslot:aslot + 1],
                                     ps_b[:, h * 512:(h + 1) * 512],
                                     ALU.mult, ALU.add),
              waits=[("t", tpv)] if h == 0 else [], inc="v")
        prev_oupd[0] = C("v")

    load_kt(0)
    if _DEBUG:
        E("Y", lambda e: e.dma_start(out=dbg_kt, in_=kbuf[:, 0, :, :]),
          waits=[("dy", kt_load[0])], inc="dy", inc_n=16)
        E("Y", lambda e: e.dma_start(out=dbg_vt, in_=vbuf[:, 0, :, :]),
          inc="dy", inc_n=16)
    for kt in range(min(KT, _KT)):
        if kt + 1 < min(KT, _KT):
            while pend:
                flash_tail()
            load_kt(kt + 1)
        for ss in list(range(kt, NSUB)) + [NQ - 1]:
            if pair_idx[0] < _NPAIR:
                flash_head(kt, ss)
                if len(pend) > 1:
                    flash_tail()
    while pend:
        flash_tail()

    p.barrier()
    if _DEBUG:
        for ss in range(NQ):
            E("Y", lambda e, ss=ss: e.dma_start(out=dbg_o[ss], in_=obuf[:, ss, :]),
              inc="dy", inc_n=16)
        E("Y", lambda e: e.dma_start(out=dbg_l, in_=l_sb[:]), inc="dy", inc_n=16)
        E("Y", lambda e: e.dma_start(out=dbg_nm, in_=nm_sb[:]), inc="dy", inc_n=16)
        E("Y", lambda e: e.dma_start(out=dbg_q, in_=q_fm[:]), inc="dy", inc_n=16)
        E("Y", lambda e: e.dma_start(out=dbg_sw, in_=swb[:]), inc="dy", inc_n=16)
        p.wait("Y", [("dy", C("dy"))])
        p.barrier()
    if _PHASES < 3:
        p.wait("Y", [("dy", C("dy"))])
        p.barrier()
        return _finish(nc, p)

    ar.reset(p2base)
    y_tm = ar.alloc("y_tm", [128, GRP, HID], F32)
    xq_fm = ar.alloc("xq_fm", [128, 8, GRP * 128], BF16)
    xqh_fm = ar.alloc("xqh_fm", [128, 8, 128], BF16)
    h_fm = ar.alloc("h_fm", [128, 2, GRP, 130], F32)
    ysil = ar.alloc("ysil", [128, 2, GRP, 128], F32)
    w1q_sb = ar.alloc("w1q_sb", [128, 2, 8, 128], BF16)
    w2q_sb = ar.alloc("w2q_sb", [128, 2, 1024], BF16)
    xtile = ar.alloc("xtile", [128, 1, 1024], F32)
    scr1 = ar.alloc("scr1", [128, 1024], F32)
    scr2 = ar.alloc("scr2", [128, 1024], F32)
    xq_tm = ar.alloc("xq_tm", [128, 1024], BF16)
    convbuf = ar.alloc("convbuf", [128, 512], F32)
    yq_tm2 = ar.alloc("yq_tm2", [128, 2, HID], BF16)
    yq_fm0 = ar.alloc("yq_fm0", [128, 32, 128], BF16)
    yq_fm1 = ar.alloc("yq_fm1", [128, 32, 128], BF16)
    outbuf = y_tm.ap()[:, 0, 0:1024]

    w1q_r = w1q_scr.rearrange("c p h -> p c h")

    xt_users = {}

    def load_xtile(ss, b):
        w = [("v", xt_users[b])] if b in xt_users else []
        return E("G", lambda e, ss=ss, b=b:
                 e.dma_start(out=xtile[:, b, :],
                             in_=d["x_tm"][ss * 128:(ss + 1) * 128, :]),
                 waits=w, inc="dg", inc_n=16)

    def ln_quant1(ss, b, xld, dest_fn):
        E("V", lambda e, ss=ss:
          e.reciprocal(sml[:, A_LINV:A_LINV + 1], l_sb[:, ss:ss + 1]),
          waits=[("dg", xld)], drain=True)
        E("V", lambda e, ss=ss, b=b:
          e.scalar_tensor_tensor(obuf[:, ss, :], obuf[:, ss, :],
                                 sml[:, A_LINV:A_LINV + 1], xtile[:, b, :],
                                 ALU.mult, ALU.add))
        E("V", lambda e, ss=ss:
          e.tensor_reduce(sml[:, A_T15:A_T15 + 1], obuf[:, ss, :], AX.X, ALU.add), drain=True)
        E("V", lambda e:
          e.tensor_scalar(sml[:, A_NEGMEAN:A_NEGMEAN + 1],
                          sml[:, A_T15:A_T15 + 1], -1.0 / D, None, ALU.mult), drain=True)
        E("V", lambda e, ss=ss:
          e.tensor_scalar(obuf[:, ss, :], obuf[:, ss, :],
                          sml[:, A_NEGMEAN:A_NEGMEAN + 1], None, ALU.add))
        E("V", lambda e, ss=ss:
          e.tensor_tensor(scr2[:], obuf[:, ss, :], obuf[:, ss, :], ALU.mult))
        E("V", lambda e:
          e.tensor_reduce(sml[:, A_SSQ:A_SSQ + 1], scr2[:], AX.X, ALU.add), drain=True)
        vv = E("V", lambda e:
               e.tensor_scalar(sml[:, A_VAR:A_VAR + 1], sml[:, A_SSQ:A_SSQ + 1],
                               1.0 / D, EPS, ALU.mult, ALU.add), inc="v", drain=True)
        sq = E("S", lambda e:
               e.activation(sml[:, A_RSTD:A_RSTD + 1], sml[:, A_VAR:A_VAR + 1],
                            AF.Sqrt), waits=[("v", vv)], inc="s")
        E("V", lambda e:
          e.reciprocal(sml[:, A_T15:A_T15 + 1], sml[:, A_RSTD:A_RSTD + 1]),
          waits=[("s", sq)], drain=True)
        E("V", lambda e, ss=ss:
          e.tensor_scalar(obuf[:, ss, :], obuf[:, ss, :],
                          sml[:, A_T15:A_T15 + 1], None, ALU.mult))
        E("V", lambda e, ss=ss:
          e.tensor_reduce(sml[:, A_AMAX:A_AMAX + 1], obuf[:, ss, :], AX.X, ALU.max,
                          apply_absolute_value=True), drain=True)
        E("V", lambda e:
          e.tensor_scalar(sml[:, A_T15:A_T15 + 1], sml[:, A_AMAX:A_AMAX + 1],
                          1e-5, None, ALU.max), drain=True)
        E("V", lambda e: e.reciprocal(sml[:, A_AMAX:A_AMAX + 1], sml[:, A_T15:A_T15 + 1]), drain=True)
        E("V", lambda e:
          e.tensor_scalar(sml[:, A_SX:A_SX + 1], sml[:, A_AMAX:A_AMAX + 1], 127.0,
                          None, ALU.mult), drain=True)
        E("V", lambda e: e.reciprocal(sml[:, A_SXINV:A_SXINV + 1], sml[:, A_SX:A_SX + 1]), drain=True)
        sxinv_slot = A_SXINV
        if ss == NQ - 1:
            E("V", lambda e:
              e.tensor_scalar(sml[:, A_MDIFF:A_MDIFF + 1], sml[:, A_SXINV:A_SXINV + 1],
                              hvalid_sb[:], None, ALU.mult), drain=True)
            sxinv_slot = A_MDIFF
        E("V", lambda e, ss=ss:
          e.tensor_scalar(scr1[:], obuf[:, ss, :], sml[:, A_SX:A_SX + 1], MAGIC,
                          ALU.mult, ALU.add))
        vq = E("V", lambda e, sxinv_slot=sxinv_slot:
               e.tensor_scalar(xq_tm[:], scr1[:], -MAGIC,
                               sml[:, sxinv_slot:sxinv_slot + 1],
                               ALU.add, ALU.mult), inc="v")
        tt = None
        for c in range(8):
            w = [("v", vq), ("s", C("s"))] if c == 0 else []
            tt = E("T", lambda e, c=c:
                   e.transpose(ps_pt[:, c, :], xq_tm[:, c * 128:(c + 1) * 128],
                               identb_sb[:]),
                   waits=w, inc="t" if c == 7 else None)
        E("V", lambda e, dest_fn=dest_fn: e.tensor_copy(dest_fn(), ps_pt[:]),
          waits=[("t", tt)], inc="v")
        return C("v"), C("s")

    xldh = load_xtile(16, 0)
    hv, hs = ln_quant1(16, 0, xldh, lambda: xqh_fm[:])
    xt_users[0] = hv

    w1ld_user = {}
    out_dma = [0]
    hevac_user = {}
    silu_of = {}
    convv_of = {}
    tw1_of = {}
    tty_of = {}
    yevac_of = {}
    w2ld_user = {}

    for g in range(min(NGRP, _NGRP_LIM)):
        s0, s1 = 2 * g, 2 * g + 1
        for sl, ss in enumerate((s0, s1)):
            xld = load_xtile(ss, 0)
            ln_quant1(ss, 0, xld,
                      lambda sl=sl: xq_fm[:, :, sl * 128:(sl + 1) * 128])
            xt_users[0] = C("v")
        xq_v, xq_s = C("v"), C("s")
        if _DEBUG:
            for sl, ss in enumerate((s0, s1)):
                E("Y", lambda e, ss=ss: e.dma_start(out=dbg_xln[ss], in_=obuf[:, ss, :]),
                  waits=[("v", xq_v)], inc="dy", inc_n=16)
            if g == 0:
                E("Y", lambda e: e.dma_start(out=dbg_xln[16], in_=obuf[:, 16, :]),
                  inc="dy", inc_n=16)

        def w1_stage(hc, g=g, s0=s0, s1=s1, xq_v=xq_v, xq_s=xq_s):
            wb = hc % 2
            hb = hc % 2
            w = [("dy", wq_done)]
            if wb in w1ld_user:
                w.append(("t", w1ld_user[wb]))
            ldw1 = E("Y", lambda e, hc=hc, wb=wb:
                     e.dma_start(out=w1q_sb[:, wb, :, :],
                                 in_=w1q_r[:, :, hc * 128:(hc + 1) * 128]),
                     waits=w, inc="dy", inc_n=16)
            po = ps_b[:, hb * 512:hb * 512 + 256]
            ph = ps_b[:, hb * 512 + 256:hb * 512 + 384]
            tw1 = None
            for di in range(8):
                waits = []
                if di == 0:
                    waits = [("dy", ldw1), ("s", xq_s), ("v", xq_v),
                             ("v", hevac_user.get(hb, 0))]
                E("T", lambda e, di=di, wb=wb, po=po,
                  start=(di == 0), stop=(di == 7):
                  e.matmul(po, w1q_sb[:, wb, di, :], xq_fm[:, di, :],
                           start=start, stop=stop),
                  waits=waits)
            for di in range(8):
                waits = []
                if g == 0 and hc == 0 and di == 0:
                    waits = [("v", hv), ("s", hs)]
                tw1 = E("T", lambda e, di=di, wb=wb, ph=ph,
                        start=(di == 0), stop=(di == 7):
                        e.matmul(ph, w1q_sb[:, wb, di, :], xqh_fm[:, di, :],
                                 start=start, stop=stop),
                        waits=waits, inc="t" if di == 7 else None)
            w1ld_user[wb] = tw1
            tw1_of[hc] = tw1
            E("V", lambda e, hb=hb, po=po:
              e.tensor_scalar(h_fm[:, hb, :, 1:129],
                              po.rearrange("p (g n) -> p g n", g=GRP),
                              swb[:, 1:2], None, ALU.mult),
              waits=[("t", tw1)])
            for si, (sl, ss) in enumerate(((0, s0), (1, s1))):
                E("V", lambda e, hb=hb, sl=sl, ss=ss, ph=ph:
                  e.tensor_scalar(h_fm[:, hb, sl, 0:1], ph[:, 2 * ss:2 * ss + 1],
                                  swb[:, 1:2], None, ALU.mult))
                E("V", lambda e, hb=hb, sl=sl, ss=ss, ph=ph:
                  e.tensor_scalar(h_fm[:, hb, sl, 129:130],
                                  ph[:, 2 * ss + 1:2 * ss + 2],
                                  swb[:, 1:2], None, ALU.mult))
            cview = convbuf[:, hb * 256:hb * 256 + GRP * 128].rearrange(
                "p (g n) -> p g n", g=GRP)
            E("V", lambda e, hb=hb, hc=hc, cview=cview:
              e.tensor_scalar(cview, h_fm[:, hb, :, 0:128], cw_sb[:, hc:hc + 1],
                              None, ALU.mult),
              waits=[("s", silu_of.get(hc - 2, 0))])
            E("V", lambda e, hb=hb, hc=hc, cview=cview:
              e.scalar_tensor_tensor(cview, h_fm[:, hb, :, 1:129],
                                     cw_sb[:, 32 + hc:33 + hc], cview,
                                     ALU.mult, ALU.add))
            vc3 = E("V", lambda e, hb=hb, hc=hc, cview=cview:
                    e.scalar_tensor_tensor(cview, h_fm[:, hb, :, 2:130],
                                           cw_sb[:, 64 + hc:65 + hc], cview,
                                           ALU.mult, ALU.add), inc="v")
            convv_of[hc] = vc3
            hevac_user[hb] = vc3
            w = [("v", vc3)]
            if hc - 2 in tty_of:
                w.append(("t", tty_of[hc - 2]))
            silu_of[hc] = E("S", lambda e, hb=hb, hc=hc, cview=cview:
                            e.activation(ysil[:, hb, :, :], cview, AF.Silu,
                                         bias=cw_sb[:, 96 + hc:97 + hc]),
                            waits=w, inc="s")

        def t2_stage(hc):
            hb = hc % 2
            tty = None
            for sl in range(GRP):
                w = []
                if sl == 0:
                    w = [("s", silu_of[hc]), ("v", yevac_of.get(hc - 2, 0))]
                tty = E("T", lambda e, hb=hb, sl=sl:
                        e.transpose(ps_a[:, hb * 512 + sl * 128:hb * 512 + (sl + 1) * 128],
                                    ysil[:, hb, sl, :], identf_sb[:]),
                        waits=w, inc="t" if sl == GRP - 1 else None)
            tty_of[hc] = tty
            w2_ = [("t", tty)]
            if hc < 8:
                w2_.append(("dy", out_dma[0]))
            yevac_of[hc] = E("V", lambda e, hc=hc, hb=hb:
                             e.tensor_copy(y_tm[:, :, hc * 128:(hc + 1) * 128],
                                           ps_a[:, hb * 512:hb * 512 + GRP * 128]
                                           .rearrange("p (g n) -> p g n", g=GRP)),
                             waits=w2_, inc="v")

        if _P4STOP < 2:
            continue
        for hc in range(32):
            w1_stage(hc)
            if hc >= 1:
                t2_stage(hc - 1)
        t2_stage(31)
        y_v, y_s = C("v"), C("s")
        if _DEBUG and g == 0:
            dmp = E("Y", lambda e: e.dma_start(out=dbg_ytm, in_=y_tm[:]),
                    waits=[("v", y_v), ("s", y_s)], inc="dy", inc_n=16)
            p.wait("V", [("dy", dmp)])
            p.wait("T", [("dy", dmp)])
        if _P4STOP < 3:
            continue

        vyq_of = {}
        for sl, yqf in ((0, yq_fm0), (1, yq_fm1)):
            E("V", lambda e, sl=sl:
              e.tensor_reduce(sml[:, A_AMAX:A_AMAX + 1], y_tm[:, sl, :], AX.X,
                              ALU.max, apply_absolute_value=True),
              waits=[("v", y_v), ("s", y_s)] if sl == 0 else [], drain=True)
            E("V", lambda e:
              e.tensor_scalar(sml[:, A_T15:A_T15 + 1], sml[:, A_AMAX:A_AMAX + 1],
                              1e-5, None, ALU.max), drain=True)
            E("V", lambda e:
              e.reciprocal(sml[:, A_AMAX:A_AMAX + 1], sml[:, A_T15:A_T15 + 1]), drain=True)
            E("V", lambda e:
              e.tensor_scalar(sml[:, A_SX:A_SX + 1], sml[:, A_AMAX:A_AMAX + 1], 127.0,
                              None, ALU.mult), drain=True)
            E("V", lambda e:
              e.reciprocal(sml[:, A_SXINV:A_SXINV + 1], sml[:, A_SX:A_SX + 1]), drain=True)
            vyq = None
            for hh in range(4):
                E("V", lambda e, sl=sl, hh=hh:
                  e.tensor_scalar(scr1[:], y_tm[:, sl, hh * 1024:(hh + 1) * 1024],
                                  sml[:, A_SX:A_SX + 1], MAGIC, ALU.mult, ALU.add))
                vyq = E("V", lambda e, sl=sl, hh=hh:
                        e.tensor_scalar(yq_tm2[:, sl, hh * 1024:(hh + 1) * 1024],
                                        scr1[:], -MAGIC, sml[:, A_SXINV:A_SXINV + 1],
                                        ALU.add, ALU.mult),
                        inc="v" if hh == 3 else None)
            vyq_of[sl] = vyq
        _yqt = int(os.environ.get("MK_YQT", "99"))
        for sl, yqf in ((0, yq_fm0), (1, yq_fm1))[:max(_yqt // 4, 0) + (1 if _yqt % 4 else 0)] if _yqt < 8 else ((0, yq_fm0), (1, yq_fm1)):
            for blk in range(min(4, _yqt) if _yqt < 8 else 4):
                ttq = None
                for c in range(8):
                    cc = blk * 8 + c
                    w = []
                    if c == 0:
                        w = [("v", vyq_of[sl] if (blk == 0 and sl == 0)
                              else max(C("v"), vyq_of[sl]))]
                    ttq = E("T", lambda e, cc=cc, c=c, sl=sl:
                            e.transpose(ps_pt[:, c, :],
                                        yq_tm2[:, sl, cc * 128:(cc + 1) * 128],
                                        identb_sb[:]),
                            waits=w, inc="t" if c == 7 else None)
                E("V", lambda e, blk=blk, yqf=yqf:
                  e.tensor_copy(yqf[:, blk * 8:(blk + 1) * 8, :], ps_pt[:]),
                  waits=[("t", ttq)], inc="v")
        yq_v, yq_s = C("v"), C("s")
        if _DEBUG and g == 0:
            dmp = E("Y", lambda e: e.dma_start(out=dbg_yq, in_=yq_tm2[:]),
                    waits=[("v", yq_v)], inc="dy", inc_n=16)
            p.wait("T", [("dy", dmp)])
        if _P4STOP < 4:
            continue

        for hc in range(32):
            wb = hc % 2
            w = [("dy", wq_done)]
            if wb in w2ld_user:
                w.append(("t", w2ld_user[wb]))
            ldw2 = E("S", lambda e, hc=hc, wb=wb:
                     e.dma_start(out=w2q_sb[:, wb, :], in_=w2q_scr[hc]),
                     waits=w, inc="ds", inc_n=16)
            waits = [("ds", ldw2)]
            if hc == 0:
                waits += [("v", yq_v), ("s", yq_s)]
            lastmm = None
            for sl, yqf, psd in ((0, yq_fm0, ps_a), (1, yq_fm1, ps_c)):
                for h in range(2):
                    last = (sl == 1 and h == 1)
                    lastmm = E("T", lambda e, hc=hc, h=h, wb=wb, yqf=yqf, psd=psd,
                               start=(hc == 0), stop=(hc == 31):
                               e.matmul(psd[:, h * 512:(h + 1) * 512], yqf[:, hc, :],
                                        w2q_sb[:, wb, h * 512:(h + 1) * 512],
                                        start=start, stop=stop),
                               waits=waits, inc="t" if last else None)
                    waits = []
            w2ld_user[wb] = lastmm
        tw2 = C("t")
        if _P4STOP < 5:
            continue

        for sl, ss, psd in ((0, s0, ps_a), (1, s1, ps_c)):
            for h in range(2):
                E("V", lambda e, ss=ss, psd=psd, h=h:
                  e.scalar_tensor_tensor(scr1[:, h * 512:(h + 1) * 512],
                                         psd[:, h * 512:(h + 1) * 512], swb[:, 3:4],
                                         obuf[:, ss, h * 512:(h + 1) * 512],
                                         ALU.mult, ALU.add),
                  waits=[("t", tw2)] if h == 0 else [])
            E("V", lambda e:
              e.tensor_reduce(sml[:, A_T15:A_T15 + 1], scr1[:], AX.X, ALU.add), drain=True)
            E("V", lambda e:
              e.tensor_scalar(sml[:, A_NEGMEAN:A_NEGMEAN + 1],
                              sml[:, A_T15:A_T15 + 1], -1.0 / D, None, ALU.mult), drain=True)
            E("V", lambda e:
              e.tensor_scalar(scr1[:], scr1[:], sml[:, A_NEGMEAN:A_NEGMEAN + 1],
                              None, ALU.add))
            E("V", lambda e:
              e.tensor_tensor(scr2[:], scr1[:], scr1[:], ALU.mult))
            E("V", lambda e:
              e.tensor_reduce(sml[:, A_SSQ:A_SSQ + 1], scr2[:], AX.X, ALU.add), drain=True)
            vv = E("V", lambda e:
                   e.tensor_scalar(sml[:, A_VAR:A_VAR + 1], sml[:, A_SSQ:A_SSQ + 1],
                                   1.0 / D, EPS, ALU.mult, ALU.add), inc="v", drain=True)
            sq = E("S", lambda e:
                   e.activation(sml[:, A_RSTD:A_RSTD + 1], sml[:, A_VAR:A_VAR + 1],
                                AF.Sqrt), waits=[("v", vv)], inc="s")
            E("V", lambda e:
              e.reciprocal(sml[:, A_T15:A_T15 + 1], sml[:, A_RSTD:A_RSTD + 1]),
              waits=[("s", sq)], drain=True)
            vout = E("V", lambda e:
                     e.tensor_scalar(outbuf, scr1[:], sml[:, A_T15:A_T15 + 1],
                                     None, ALU.mult),
                     waits=[("dy", out_dma[0])], inc="v")
            out_dma[0] = E("Y", lambda e, ss=ss:
                           e.dma_start(out=y_out[ss * 128:(ss + 1) * 128, :],
                                       in_=outbuf),
                           waits=[("v", vout)], inc="dy", inc_n=16)

    p.wait("Y", [("dy", C("dy"))])
    p.barrier()
    return _finish(nc, p)


def _check_waits(p):
    bad = [(e, s, v, i) for (e, s, v, i) in p.all_waits if v > p.cnt[s]]
    if bad:
        for b in bad[:10]:
            print("IMPOSSIBLE WAIT:", b, "final:", p.cnt[b[1]])
        raise RuntimeError(f"{len(bad)} impossible waits")
    ptr = {e: 0 for e in ENGINES}
    cnt = {s: 0 for s in p.cnt}
    progress = True
    while progress:
        progress = False
        for e in ENGINES:
            while ptr[e] < len(p.meta[e]):
                waits, inc, inc_n = p.meta[e][ptr[e]]
                if any(cnt[s] < v for (s, v) in waits):
                    break
                if inc is not None:
                    cnt[inc] += inc_n
                ptr[e] += 1
                progress = True
    stuck = {e: ptr[e] for e in ENGINES if ptr[e] < len(p.meta[e])}
    if stuck:
        for e, i in stuck.items():
            waits = p.meta[e][i][0]
            blocked = [(s, v, cnt[s]) for (s, v) in waits if cnt[s] < v]
            print(f"DEADLOCK {e} op#{i}/{len(p.meta[e])} blocked on {blocked}")
        raise RuntimeError("planned deadlock")


def _finish(nc, p):
    _check_waits(p)
    with nc.Block() as block:
        @block.tensor
        def _(t):
            for fn in p.ops["T"]:
                fn(t)

        @block.vector
        def _(v):
            for fn in p.ops["V"]:
                fn(v)

        @block.scalar
        def _(s):
            for fn in p.ops["S"]:
                fn(s)

        @block.gpsimd
        def _(g):
            for fn in p.ops["G"]:
                fn(g)

        @block.sync
        def _(y):
            for fn in p.ops["Y"]:
                fn(y)

    return nc


def _host_inputs(x, Qw, Kw, Vw, w1, conv_w, conv_b, w2):
    x2 = np.ascontiguousarray(np.asarray(x, np.float32).reshape(S, D))
    xT = np.ascontiguousarray(x2.T)
    w1t = np.ascontiguousarray(np.asarray(w1, np.float32).T)
    w2t = np.ascontiguousarray(np.asarray(w2, np.float32).T)
    cwf = np.asarray(conv_w, np.float32).reshape(HID, 3)
    cbf = np.asarray(conv_b, np.float32)
    cw = np.zeros((128, 128), np.float32)
    cw[:, 0:32] = cwf[:, 0].reshape(32, 128).T
    cw[:, 32:64] = cwf[:, 1].reshape(32, 128).T
    cw[:, 64:96] = cwf[:, 2].reshape(32, 128).T
    cw[:, 96:128] = cbf.reshape(32, 128).T
    identb = np.eye(128, dtype=ml_dtypes.bfloat16)
    identf = np.eye(128, dtype=np.float32)
    ones_col = np.ones((128, 1), np.float32)
    ones_row = np.ones((1, 128), np.float32)

    in_maps = []
    for j in range(NCORES):
        own_cols = (1024 * np.arange(NSUB)[:, None] + 128 * j
                    + np.arange(128)[None, :]).reshape(-1)
        halo_pos = np.zeros(32, np.int64)
        halo_valid = np.zeros((128, 1), np.float32)
        for i in range(NSUB):
            lp = 1024 * i + 128 * j - 1
            rp = 1024 * i + 128 * j + 128
            halo_pos[2 * i] = lp if lp >= 0 else 0
            halo_pos[2 * i + 1] = rp if rp < S else 0
            halo_valid[2 * i, 0] = 1.0 if lp >= 0 else 0.0
            halo_valid[2 * i + 1, 0] = 1.0 if rp < S else 0.0
        all_cols = np.concatenate([own_cols, halo_pos, np.zeros(96, np.int64)])
        xT_own = np.ascontiguousarray(xT[:, all_cols])
        x_tm = np.ascontiguousarray(x2[all_cols, :])
        for h in range(32):
            if halo_valid[h, 0] == 0.0:
                xT_own[:, 2048 + h] = 0.0
                x_tm[2048 + h, :] = 0.0
        xT_own[:, 2048 + 32:] = 0.0
        x_tm[2048 + 32:, :] = 0.0

        r = np.arange(128)[:, None]
        c = np.arange(1024)[None, :]
        dmask = np.where(c <= 128 * j + r, 0.0, NEG).astype(np.float32)

        hmask = np.full((KT, 128, 1024), NEG, np.float32)
        for h in range(32):
            if halo_valid[h, 0] == 1.0:
                pos = int(halo_pos[h])
                for kt in range(pos // 1024 + 1):
                    n = min(1024, pos - 1024 * kt + 1)
                    hmask[kt, h, 0:n] = 0.0
            else:
                hmask[0, h, 0] = 0.0
        hmask[0, 32:, 0] = 0.0

        in_maps.append({
            "xT_own": xT_own, "x_tm": x_tm,
            "qw": np.asarray(Qw, np.float32), "kw": np.asarray(Kw, np.float32),
            "vw": np.asarray(Vw, np.float32),
            "w1t": w1t, "w2t": w2t,
            "cw": cw, "dmask": dmask, "hmask": hmask, "hvalid": halo_valid,
            "identb": identb, "identf": identf,
            "ones_col": ones_col, "ones_row": ones_row,
        })
    return in_maps


_NC_CACHE = [None]


def kernel(L0_attn_x, L0_attn_Qw, L0_attn_Kw, L0_attn_Vw,
           L0_ffn_w1, L0_ffn_conv_w, L0_ffn_conv_b, L0_ffn_w2):
    in_maps = _host_inputs(L0_attn_x, L0_attn_Qw, L0_attn_Kw, L0_attn_Vw,
                           L0_ffn_w1, L0_ffn_conv_w, L0_ffn_conv_b, L0_ffn_w2)
    if _NC_CACHE[0] is None:
        _NC_CACHE[0] = build_program()
    nc = _NC_CACHE[0]
    res = run_bass_kernel_spmd(nc, in_maps, core_ids=list(range(NCORES)))
    out = np.empty((1, S, D), np.float32)
    ov = out.reshape(NSUB, NCORES, 128, D)
    for j in range(NCORES):
        ov[:, j] = np.asarray(res.results[j]["y"]).reshape(NSUB, 128, D)
    return out
